# version 14
# baseline (speedup 1.0000x reference)
import contextlib
import os
import numpy as np
import ml_dtypes
import concourse.bass as bass
import concourse.mybir as mybir
from concourse.bass_utils import run_bass_kernel_spmd

F32 = mybir.dt.float32
BF16 = mybir.dt.bfloat16
I32 = mybir.dt.int32
AF = mybir.ActivationFunctionType
ALU = mybir.AluOpType
AX = mybir.AxisListType

D = 2048
TLOC = 2048
NCHUNK = 8
CH = 256
NH = 16
HD = 128
A_CHUNKS = [0, 3, 4, 7, 8, 11, 12, 15]
B_CHUNKS = [1, 2, 5, 6, 9, 10, 13, 14]
EXT = [4, 8, 12, 16, 20, 24, 28, 32]
NEG = -1e30

ABS2LOC = {}
for i, c in enumerate(A_CHUNKS):
    ABS2LOC[c] = (0, i)
for i, c in enumerate(B_CHUNKS):
    ABS2LOC[c] = (1, i)


def shard_x(x):
    shards = []
    for c in range(8):
        b = c // 2
        chunks = A_CHUNKS if c % 2 == 0 else B_CHUNKS
        xb = np.asarray(x[b]).reshape(16, CH, D)
        shards.append(np.ascontiguousarray(xb[chunks].reshape(TLOC, D)))
    return shards


def unshard_y(ys):
    out = np.empty((4, 4096, D), dtype=ys[0].dtype)
    for c in range(8):
        b = c // 2
        chunks = A_CHUNKS if c % 2 == 0 else B_CHUNKS
        out[b].reshape(16, CH, D)[chunks] = ys[c].reshape(NCHUNK, CH, D)
    return out


def maskr_host():
    out = []
    for c in range(8):
        chunks = A_CHUNKS if c % 2 == 0 else B_CHUNKS
        r = np.zeros((128, 32), dtype=np.float32)
        for s in range(NCHUNK):
            q0 = chunks[s] * CH
            for j in range(4):
                kb = EXT[s] - 4 + j
                r[:, s * 4 + j] = kb * 128 - q0
        out.append(r)
    return out


class Arena:
    BASE = 20 * 1024
    END = 212800

    def __init__(self, nc):
        self.nc = nc
        self.free_list = [(self.BASE, self.END)]
        self.live = {}
        self.n = 0

    def alloc(self, name, shape, dtype):
        per = int(np.prod(shape[1:])) * mybir.dt.size(dtype)
        per = (per + 63) & ~63
        for i, (a, b) in enumerate(self.free_list):
            if b - a >= per:
                self.free_list[i] = (a + per, b)
                if self.free_list[i][0] == self.free_list[i][1]:
                    self.free_list.pop(i)
                self.n += 1
                h = self.nc.alloc_sbuf_tensor_at(f"{name}_{self.n}", shape, dtype, offset=a)
                self.live[id(h)] = (a, per)
                return h
        raise MemoryError(f"SBUF arena full for {name} ({per}B): {self.free_list}")

    def free(self, *handles):
        for h in handles:
            a, per = self.live.pop(id(h))
            self.free_list.append((a, a + per))
        self.free_list.sort()
        out = []
        for a, b in self.free_list:
            if out and out[-1][1] == a:
                out[-1] = (out[-1][0], b)
            else:
                out.append((a, b))
        self.free_list = [tuple(x) for x in out]


class Prog:

    def __init__(self, nc):
        self.nc = nc
        self.cnt = {}
        self.hdl = {}
        self.pool = {}
        self.nsem = 0
        self.ar = Arena(nc)
        self.blk_ctx = None
        self.blk = None

    DMA_G = {"amk", "bld", "cw", "dbs", "gd", "gcd", "kps", "pbl"}
    DMA_S = {"aol", "ggl", "gld", "kal", "wtl", "xlt", "xtl"}

    def kind_of(self, name):
        if name == "cc":
            return "cc"
        if name in self.DMA_G or name.startswith("Dg_"):
            return "dmag"
        if name in self.DMA_S or name.startswith("Ds_"):
            return "dmas"
        return "eng"

    def sem(self, name):
        if name not in self.hdl:
            kind = self.kind_of(name)
            pool = self.pool.setdefault(kind, [])
            if pool:
                h, c = pool.pop()
                self.hdl[name] = h
                self.cnt[name] = c
            else:
                self.hdl[name] = self.nc.alloc_semaphore(f"s{self.nsem}")
                self.nsem += 1
                self.cnt[name] = 0
        return self.hdl[name]

    def phase_boundary(self, keep=()):
        for name in list(self.hdl.keys()):
            if name in keep:
                continue
            self.pool.setdefault(self.kind_of(name), []).append(
                (self.hdl.pop(name), self.cnt.pop(name)))

    def inc(self, instr, name, n=1):
        instr.then_inc(self.sem(name), n)
        self.cnt[name] += n
        return self.cnt[name]

    def cnt_of(self, name):
        self.sem(name)
        return self.cnt[name]

    def base(self, *names):
        return [self.cnt_of(n) for n in names]

    def plan(self, stream, nslots, n, unit=16, q="g"):
        stream = f"D{q}_{stream}"
        for i in range(nslots):
            self.sem(f"{stream}{i}")
        cnt = {i: self.cnt[f"{stream}{i}"] for i in range(nslots)}
        out = []
        for i in range(n):
            sl = i % nslots
            cnt[sl] += unit
            out.append((f"{stream}{sl}", cnt[sl]))
        return out

    def wait(self, eng, name, val=None):
        eng.wait_ge(self.sem(name), self.cnt[name] if val is None else val)

    def new_block(self):
        if self.blk_ctx is not None:
            self.blk_ctx.__exit__(None, None, None)
        self.blk_ctx = self.nc.Block()
        self.blk = self.blk_ctx.__enter__()

    def end(self):
        if self.blk_ctx is not None:
            self.blk_ctx.__exit__(None, None, None)
            self.blk_ctx = None


class VD:
    OPS = {"tensor_scalar", "tensor_tensor", "tensor_copy", "tensor_reduce",
           "bn_stats", "bn_aggr", "reciprocal", "scalar_tensor_tensor",
           "select", "memset", "activation", "copy"}
    ENABLED = bool(os.environ.get("BASS_SIM_FP64") or os.environ.get("KSIM_DRAINS"))

    def __init__(self, v):
        object.__setattr__(self, "_v", v)

    def __getattr__(self, n):
        f = getattr(self._v, n)
        if n in VD.OPS:
            def wrap(*a, **k):
                r = f(*a, **k)
                self._v.drain()
                return r
            return wrap
        return f


DRAM_REG = {
    "d_wkb": ([128, 8 * TLOC], BF16),
    "d_ctT": ([128, TLOC], F32),
    "d_wT": ([8, TLOC], F32),
    "d_hbf": ([128, 8 * TLOC], BF16),
    "d_hs": ([128, 8 * TLOC], BF16),
    "d_hs2": ([128, 8 * TLOC], BF16),
    "x_in": ([TLOC, D], F32),
    "y_out": ([TLOC, D], F32),
    "maskr": ([128, 32], F32),
    "qn_d": ([NH, 128, TLOC], BF16),
    "kn_d": ([NH * 128, TLOC], BF16),
    "v_d": ([NH * TLOC, 128], BF16),
    "kn_ag": ([2 * NH * 128, TLOC], BF16),
    "aout_d": ([NH * 128, TLOC], BF16),
    "v_ag": ([2 * NH * TLOC, 128], BF16),
    "kp_d": ([TLOC, D], BF16),
    "qp_d": ([NH, 128, TLOC], BF16),
    "kvw_d": ([TLOC, 2 * D], BF16),
    "kvp_d": ([D, 132], F32),
    "kvp_ag": ([2, D, 132], F32),
    "gwT_d": ([D, 5504], BF16),
    "uwT_d": ([D, 5504], BF16),
    "dwT_d": ([5504, D], BF16),
    "gwT_sl": ([256, 5504], BF16),
    "uwT_sl": ([256, 5504], BF16),
    "dwT_sl": ([688, D], BF16),
    "gidxb_d": ([5504, 256], BF16),
    "uidxb_d": ([5504, 256], BF16),
    "didxb_d": ([2048, 768], BF16),
    "wqt_bf": ([D, D], BF16), "wkt_bf": ([D, D], BF16),
    "wvt_bf": ([D, D], BF16), "wot_bf": ([D, D], BF16),
    "qw1_bf": ([D, D], BF16), "kw1_bf": ([D, D], BF16),
    "vw1_bf": ([D, D], BF16), "ow1_bf": ([D, D], BF16),
}
for i in range(0, 7):
    DRAM_REG[f"x{i}"] = ([TLOC, D], F32)
    DRAM_REG[f"xln{i}"] = ([TLOC, D], BF16)

WEIGHT_REG = {
    "L0_attn_Wq": ([D, D], F32), "L0_attn_Wk": ([D, D], F32),
    "L0_attn_Wv": ([D, D], F32), "L0_attn_Wo": ([D, D], F32),
    "L0_attn_gq": ([HD], F32), "L0_attn_gk": ([HD], F32),
    "L0_ffn_w_router": ([D, 128], F32),
    "L0_ffn_w_gate": ([D, 8192], F32), "L0_ffn_w_up": ([D, 8192], F32),
    "L0_ffn_w_down": ([1024, D], F32),
    "L0_ffn_amplitudes": ([128, 8, 2], F32), "L0_ffn_expert_scale": ([128], F32),
    "L0_ffn_shared_w_gate": ([D, D], F32), "L0_ffn_shared_w_up": ([D, D], F32),
    "L0_ffn_shared_w_down": ([D, D], F32),
    "L1_attn_q_w": ([D, D], F32), "L1_attn_q_b": ([D], F32),
    "L1_attn_k_w": ([D, D], F32), "L1_attn_k_b": ([D], F32),
    "L1_attn_v_w": ([D, D], F32), "L1_attn_v_b": ([D], F32),
    "L1_attn_o_w": ([D, D], F32), "L1_attn_o_b": ([D], F32),
    "L1_ffn_lut": ([16], F32),
    "gate_idx_sl": ([5504, 256], I32), "up_idx_sl": ([5504, 256], I32),
    "down_idx_sl": ([2048, 768], I32),
}


def declare_dram(nc, names, kind_map):
    aps = {}
    reg = dict(DRAM_REG)
    reg.update(WEIGHT_REG)
    for n in names:
        shape, dt = reg[n]
        aps[n] = nc.dram_tensor(n, shape, dt, kind=kind_map.get(n, "Internal")).ap()
    return aps


def emit_xlnT_load(p, eng, src_ap, xlt, sem, ncols=16, col0=0):
    R = src_ap.shape[0]
    for dc in range(ncols):
        c = col0 + dc
        p.inc(eng.dma_start(
            out=xlt[:, dc * R:(dc + 1) * R],
            in_=src_ap[:, c * 128:(c + 1) * 128],
            transpose=True), sem, 16)
    return p.cnt[sem]


def ph_init(p, g):
    nc = p.nc
    g["ident"] = p.ar.alloc("g_ident", [128, 128], F32)
    g["identb"] = p.ar.alloc("g_identb", [128, 128], BF16)
    g["ones_r"] = p.ar.alloc("g_ones", [1, 128], BF16)
    g["onesf_r"] = p.ar.alloc("g_onesf", [1, 128], F32)
    g["onesf_c"] = p.ar.alloc("g_onesfc", [128, 1], F32)
    g["onesb_512"] = p.ar.alloc("g_onesb512", [1, 512], BF16)
    g["onesb_c"] = p.ar.alloc("g_onesbc", [128, 1], BF16)
    g["sel"] = p.ar.alloc("g_sel", [8, 8 * 128], F32)
    g["eps"] = p.ar.alloc("g_eps", [128, 1], F32)
    g["Dt"] = p.ar.alloc("g_Dt", [128, 256], F32)
    g["ps"] = [nc.alloc_psum_tensor(f"g_ps{i}", [128, 512], F32) for i in range(8)]
    p.new_block()
    blk = p.blk

    @blk.gpsimd
    def _(g_):
        g_.memset(g["ident"][:], 0.0)
        g_.memset(g["identb"][:], 0.0)
        g_.memset(g["sel"][:], 0.0)
        g_.memset(g["ones_r"][:], 1.0)
        g_.memset(g["onesf_r"][:], 1.0)
        g_.memset(g["onesf_c"][:], 1.0)
        g_.memset(g["onesb_512"][:], 1.0)
        g_.memset(g["onesb_c"][:], 1.0)
        g_.memset(g["eps"][:], 1e-5)
        g_.drain()
        g_.affine_select(out=g["ident"][:], in_=g["ident"][:], compare_op=ALU.not_equal,
                         fill=1.0, base=0, pattern=[[-1, 128]], channel_multiplier=1)
        g_.affine_select(out=g["identb"][:], in_=g["identb"][:], compare_op=ALU.not_equal,
                         fill=1.0, base=0, pattern=[[-1, 128]], channel_multiplier=1)
        g_.iota(g["Dt"][:], pattern=[[-1, 256]], base=0, channel_multiplier=1,
                allow_small_or_imprecise_dtypes=True)
        for k in range(8):
            g_.affine_select(out=g["sel"][:, k * 128:(k + 1) * 128],
                             in_=g["sel"][:, k * 128:(k + 1) * 128],
                             compare_op=ALU.not_equal, fill=1.0, base=-k,
                             pattern=[[0, 128]], channel_multiplier=1)
        g_.drain()
        p.inc(g_.nop(), "init", 1)


def ph_ln0(p, io, g, xsrc="x0", xdst="xln0"):
    p.phase_boundary()
    nc = p.nc
    ar = p.ar
    x_ap, xln_ap = io[xsrc], io[xdst]
    xs = [ar.alloc(f"ln0_x{i}", [128, D], F32) for i in range(3)]
    stats = ar.alloc("ln0_stats", [128, 16 * 24], F32)
    mv = ar.alloc("ln0_mv", [128, 32], F32)
    rstd = ar.alloc("ln0_rstd", [128, 16], F32)
    rstd2 = ar.alloc("ln0_rstd2", [128, 16], F32)
    xlb = [ar.alloc(f"ln0_xlb{i}", [128, D], BF16) for i in range(2)]
    p.new_block()
    blk = p.blk
    b_vs, b_rs, b_nm = p.base("vs", "rs", "nm")
    LD = p.plan("Lld", 3, 32, q="s")
    ST = p.plan("Lst", 2, 16)

    @blk.sync
    def _(s):
        for t in range(16):
            if t >= 3:
                s.wait_ge(p.sem("vs"), b_vs + t - 2)
            tgt = p.inc(s.dma_start(out=xs[t % 3][:], in_=x_ap[t * 128:(t + 1) * 128, :]), LD[t][0], 16)
            assert tgt == LD[t][1]

    @blk.vector
    def _(v):
        v = VD(v)
        for t in range(16):
            v.wait_ge(p.sem(LD[t][0]), LD[t][1])
            for q in range(4):
                v.bn_stats(stats[:, t * 24 + q * 6:t * 24 + (q + 1) * 6],
                           xs[t % 3][:, q * 512:(q + 1) * 512])
            v.drain()
            p.inc(v.bn_aggr(mv[:, 2 * t:2 * t + 2], stats[:, t * 24:(t + 1) * 24]), "vs", 1)

    @blk.scalar
    def _(se):
        se = VD(se)
        p.wait(se, "vs")
        p.inc(se.activation(rstd[:], mv[:, 1:32:2], AF.Sqrt, bias=g["eps"][:, 0:1], scale=1.0), "rs", 1)

    @blk.vector
    def _(v):
        v = VD(v)
        p.wait(v, "rs")
        p.inc(v.reciprocal(rstd2[:], rstd[:]), "rs", 1)

    @blk.sync
    def _(s):
        s.wait_ge(p.sem("rs"), b_rs + 2)
        for t in range(16):
            if t >= 3:
                s.wait_ge(p.sem("nm"), b_nm + t - 2)
            p.inc(s.dma_start(out=xs[t % 3][:], in_=x_ap[t * 128:(t + 1) * 128, :]), LD[16 + t][0], 16)

    @blk.vector
    def _(v):
        v = VD(v)
        for t in range(16):
            v.wait_ge(p.sem(LD[16 + t][0]), LD[16 + t][1])
            if t >= 2:
                v.wait_ge(p.sem(ST[t - 2][0]), ST[t - 2][1])
            p.inc(v.tensor_scalar(xlb[t % 2][:], xs[t % 3][:], mv[:, 2 * t:2 * t + 1],
                                  rstd2[:, t:t + 1], ALU.subtract, ALU.mult), "nm", 1)

    @blk.gpsimd
    def _(g_):
        for t in range(16):
            g_.wait_ge(p.sem("nm"), b_nm + t + 1)
            p.inc(g_.dma_start(out=xln_ap[t * 128:(t + 1) * 128, :], in_=xlb[t % 2][:]), ST[t][0], 16)
        for i in range(2):
            g_.wait_ge(p.sem(ST[15 - i][0]), ST[15 - i][1])
    ar.free(*xs, stats, mv, rstd, rstd2, *xlb)


def ph_l0ffn(p, io, g, xln_src, x_src, x_dst, xln_dst, tag, dbg=None):
    p.phase_boundary()
    nc = p.nc
    ar = p.ar
    xln_ap, x_ap, xo_ap, xlo_ap = io[xln_src], io[x_src], io[x_dst], io[xln_dst]
    ps = g["ps"]

    xlt = ar.alloc(f"{tag}_xlt", [128, 16 * TLOC], BF16)
    hbf = ar.alloc(f"{tag}_hbf", [128, 8 * TLOC], BF16)
    wkb = ar.alloc(f"{tag}_wkb", [128, 8 * TLOC], BF16)
    statsE = ar.alloc(f"{tag}_stE", [128, 16 * 24], F32)
    wrb = ar.alloc(f"{tag}_wrb", [128, 16 * 128], BF16)
    Af = ar.alloc(f"{tag}_Af", [128, 8], F32)

    amp2 = ar.alloc(f"{tag}_amp", [128, 16], F32)
    escb = ar.alloc(f"{tag}_esc", [128, 1], F32)
    ppb = ar.alloc(f"{tag}_pp", [128, 11], F32)
    p.new_block()
    blk = p.blk

    @blk.scalar
    def _(se):
        se = VD(se)
        emit_xlnT_load(p, se, xln_ap, xlt, "xlt")

    @blk.gpsimd
    def _(g_):
        p.inc(g_.dma_start(out=wrb[:, :].rearrange("p (c f) -> p c f", c=16),
                           in_=io["L0_ffn_w_router"].rearrange("(c p) f -> p c f", p=128)), "gd", 16)
        p.inc(g_.dma_start(out=amp2[:, :], in_=io["L0_ffn_amplitudes"].rearrange("e k c -> e (k c)")), "gd", 16)
        p.inc(g_.dma_start(out=escb[:, :], in_=io["L0_ffn_expert_scale"][:, None]), "gd", 16)

    @blk.scalar
    def _(se):
        se = VD(se)
        p.wait(se, "gd")
        p.inc(se.activation(amp2[:], amp2[:], AF.Square), "ae", 1)

    @blk.vector
    def _(v):
        v = VD(v)
        p.wait(v, "ae")
        v.tensor_tensor(ppb[:, 0:8], amp2[:, 0:16:2], amp2[:, 1:16:2], ALU.add)
        v.tensor_reduce(ppb[:, 8:9], ppb[:, 0:8], axis=AX.X, op=ALU.add)
        v.tensor_scalar(ppb[:, 9:10], ppb[:, 8:9], 1e-8, None, ALU.add)
        v.reciprocal(ppb[:, 10:11], ppb[:, 9:10])
        p.inc(v.tensor_scalar(Af[:], ppb[:, 0:8], ppb[:, 10:11], escb[:, 0:1],
                              ALU.mult, ALU.mult), "ve", 1)

    lg = [ar.alloc(f"{tag}_lg{i}", [128, 128], F32) for i in range(2)]
    cur = ar.alloc(f"{tag}_cur", [128, 128], F32)
    eqn = ar.alloc(f"{tag}_eqn", [128, 128], F32)
    esb = [ar.alloc(f"{tag}_esb{i}", [128, 128], F32) for i in range(2)]
    sml = [ar.alloc(f"{tag}_sml{i}", [128, 12], F32) for i in range(2)]
    coeff = [ar.alloc(f"{tag}_cf{i}", [128, 128], F32) for i in range(2)]
    ctT = ar.alloc(f"{tag}_ctT", [128, TLOC], F32)
    wT = ar.alloc(f"{tag}_wT", [8, TLOC], F32)
    p.new_block()
    blk = p.blk
    b_rlg, b_rlx, b_rng, b_rex, b_rcf, b_rtp, b_rtx, b_rwkx = p.base(
        "rlg", "rlx", "rng", "rex", "rcf", "rtp", "rtx", "rwkx")

    @blk.tensor
    def _(t):
        p.wait(t, "xlt")
        p.wait(t, "gd")
        def emit_tr(s):
            t.wait_ge(p.sem("rcf"), b_rcf + s + 1)
            if s >= 2:
                t.wait_ge(p.sem("rtx"), b_rtx + s - 1)
            p.inc(t.transpose(ps[2 + s % 2][:, 0:128], coeff[s % 2][:], g["ident"][:]), "rtp", 1)
        for s in range(16):
            if s >= 2:
                t.wait_ge(p.sem("rlx"), b_rlx + s - 1)
            for dc in range(16):
                mm = t.matmul(ps[s % 2][:, 0:128],
                              xlt[:, dc * TLOC + s * 128: dc * TLOC + (s + 1) * 128],
                              wrb[:, dc * 128:(dc + 1) * 128],
                              start=(dc == 0), stop=(dc == 15))
            p.inc(mm, "rlg", 1)
            if s >= 2:
                emit_tr(s - 2)
        emit_tr(14)
        emit_tr(15)

    @blk.vector
    def _(v):
        v = VD(v)
        for s in range(16):
            m = sml[s % 2]
            v.wait_ge(p.sem("rlg"), b_rlg + s + 1)
            p.inc(v.tensor_scalar(lg[s % 2][:], ps[s % 2][:, 0:128], 1.0, NEG,
                                  ALU.mult, ALU.max, accum_out=m[:, 0:1]), "rlx", 1)
            p.inc(v.tensor_scalar(m[:, 1:2], m[:, 0:1], -1.0, None, ALU.mult), "rng", 1)
            v.tensor_copy(cur[:], lg[s % 2][:])
            mprev = m[:, 0:1]
            for it in range(3):
                v.tensor_scalar(eqn[:], cur[:], mprev, NEG, ALU.is_equal, ALU.mult)
                v.tensor_tensor(cur[:], cur[:], eqn[:], ALU.add)
                v.tensor_reduce(m[:, 2 + it:3 + it], cur[:], axis=AX.X, op=ALU.max)
                mprev = m[:, 2 + it:3 + it]
            v.wait_ge(p.sem("rex"), b_rex + s + 1)
            v.tensor_scalar(eqn[:], lg[s % 2][:], m[:, 4:5], None, ALU.is_ge)
            v.tensor_tensor(esb[s % 2][:], esb[s % 2][:], eqn[:], ALU.mult)
            v.tensor_reduce(m[:, 6:7], esb[s % 2][:], axis=AX.X, op=ALU.add)
            v.scalar_tensor_tensor(m[:, 7:8], m[:, 5:6], 1e-8, m[:, 6:7],
                                   op0=ALU.mult, op1=ALU.add)
            v.reciprocal(m[:, 8:9], m[:, 7:8])
            if s >= 2:
                v.wait_ge(p.sem("rtp"), b_rtp + s - 1)
            p.inc(v.tensor_scalar(coeff[s % 2][:], esb[s % 2][:], m[:, 8:9], None,
                                  ALU.mult), "rcf", 1)

    @blk.scalar
    def _(se):
        se = VD(se)
        def emit_cp(s):
            se.wait_ge(p.sem("rtp"), b_rtp + s + 1)
            p.inc(se.copy(ctT[:, s * 128:(s + 1) * 128], ps[2 + s % 2][:, 0:128]), "rtx", 1)
        for s in range(16):
            se.wait_ge(p.sem("rng"), b_rng + s + 1)
            if s >= 2:
                se.wait_ge(p.sem("rcf"), b_rcf + s - 1)
            p.inc(se.activation(esb[s % 2][:], lg[s % 2][:], AF.Exp, bias=sml[s % 2][:, 1:2],
                                scale=1.0, accum_out=sml[s % 2][:, 5:6]), "rex", 1)
            if s >= 2:
                emit_cp(s - 2)
        emit_cp(14)
        emit_cp(15)

    @blk.tensor
    def _(t):
        p.wait(t, "rtx")
        p.wait(t, "ve")
        for tb in range(4):
            mm = t.matmul(ps[4 + tb][0:8, 0:512], Af[:], ctT[:, tb * 512:(tb + 1) * 512],
                          start=True, stop=True)
        p.inc(mm, "rwt", 1)

    @blk.scalar
    def _(se):
        se = VD(se)
        p.wait(se, "rwt")
        for tb in range(4):
            cp = se.copy(wT[:, tb * 512:(tb + 1) * 512], ps[4 + tb][0:8, 0:512])
        p.inc(cp, "rwte", 1)

    @blk.tensor
    def _(t):
        p.wait(t, "rwte")
        for k in range(8):
            if k >= 1:
                t.wait_ge(p.sem("rwkx"), b_rwkx + k)
            for tb in range(4):
                mm = t.matmul(ps[4 + tb][:, 0:512], g["sel"][:, k * 128:(k + 1) * 128],
                              wT[0:8, tb * 512:(tb + 1) * 512], start=True, stop=True)
            p.inc(mm, "rwk", 1)

    @blk.vector
    def _(v):
        v = VD(v)
        b_rwk = p.cnt_of("rwk") - 8
        for k in range(8):
            v.wait_ge(p.sem("rwk"), b_rwk + k + 1)
            for tb in range(4):
                cp = v.tensor_copy(wkb[:, k * TLOC + tb * 512: k * TLOC + (tb + 1) * 512],
                                   ps[4 + tb][:, 0:512])
            p.inc(cp, "rwkx", 1)

    if dbg is not None and "d_wkb" in dbg:
        p.new_block()
        blk = p.blk

        @blk.gpsimd
        def _(g_):
            p.inc(g_.dma_start(out=dbg["d_wkb"][:, :], in_=wkb[:, :]), "dbs", 16)
            p.inc(g_.dma_start(out=dbg["d_ctT"][:, :], in_=ctT[:, :]), "dbs", 16)
            p.inc(g_.dma_start(out=dbg["d_wT"][:, :], in_=wT[:, :]), "dbs", 16)
            p.wait(g_, "dbs")
    ar.free(*lg, cur, eqn, *esb, *sml, *coeff, ctT, wT, amp2, escb, ppb, wrb, Af)

    wg_sb = [ar.alloc(f"{tag}_wg{i}", [128, 2048], BF16) for i in range(2)]
    wu_sb = [ar.alloc(f"{tag}_wu{i}", [128, 2048], BF16) for i in range(2)]
    silu_sb = [ar.alloc(f"{tag}_sl{i}", [128, 512], BF16) for i in range(2)]
    basis_sb = ar.alloc(f"{tag}_bs", [128, 512], BF16)
    bw_sb = ar.alloc(f"{tag}_bw", [128, 512], BF16)
    hA = ar.alloc(f"{tag}_hA", [128, 2048], F32)
    hB = ar.alloc(f"{tag}_hB", [128, 2048], F32)
    p.new_block()
    blk = p.blk
    b_gmm, b_umm, b_sl, b_bu = p.base("gmm", "umm", "sl", "bu")
    WL = p.plan("wl", 2, 64, unit=32)
    wg_ap = io["L0_ffn_w_gate"]
    wu_ap = io["L0_ffn_w_up"]

    @blk.gpsimd
    def _(g_):
        for j in range(64):
            ic, k = j // 8, j % 8
            c0 = k * 1024 + ic * 128
            if j >= 2:
                g_.wait_ge(p.sem("bu"), b_bu + 4 * (j - 1))
            p.inc(g_.dma_start(out=wg_sb[j % 2][:, :].rearrange("p (c f) -> p c f", c=16),
                               in_=wg_ap[:, c0:c0 + 128].rearrange("(c p) f -> p c f", p=128)), WL[j][0], 16)
            tgt = p.inc(g_.dma_start(out=wu_sb[j % 2][:, :].rearrange("p (c f) -> p c f", c=16),
                               in_=wu_ap[:, c0:c0 + 128].rearrange("(c p) f -> p c f", p=128)), WL[j][0], 16)
            assert tgt == WL[j][1]

    @blk.tensor
    def _(t):
        for j in range(64):
            ic, k = j // 8, j % 8
            t.wait_ge(p.sem(WL[j][0]), WL[j][1])
            for tb in range(4):
                jj = j * 4 + tb
                if jj >= 2:
                    t.wait_ge(p.sem("sl"), b_sl + jj - 1)
                    t.wait_ge(p.sem("bu"), b_bu + jj - 1)
                for dc in range(16):
                    mm = t.matmul(ps[tb % 2][:, 0:512], wg_sb[j % 2][:, dc * 128:(dc + 1) * 128],
                                  xlt[:, dc * TLOC + tb * 512: dc * TLOC + (tb + 1) * 512],
                                  start=(dc == 0), stop=(dc == 15))
                p.inc(mm, "gmm", 1)
                for dc in range(16):
                    mm = t.matmul(ps[2 + tb % 2][:, 0:512], wu_sb[j % 2][:, dc * 128:(dc + 1) * 128],
                                  xlt[:, dc * TLOC + tb * 512: dc * TLOC + (tb + 1) * 512],
                                  start=(dc == 0), stop=(dc == 15))
                p.inc(mm, "umm", 1)

    @blk.scalar
    def _(se):
        se = VD(se)
        for j in range(64):
            for tb in range(4):
                jj = j * 4 + tb
                se.wait_ge(p.sem("gmm"), b_gmm + jj + 1)
                if jj >= 2:
                    se.wait_ge(p.sem("bu"), b_bu + jj - 1)
                p.inc(se.activation(silu_sb[jj % 2][:], ps[tb % 2][:, 0:512], AF.Silu), "sl", 1)

    @blk.vector
    def _(v):
        v = VD(v)
        for j in range(64):
            ic, k = j // 8, j % 8
            hsrc = [hA, hB][(k - 1) % 2] if k >= 1 else None
            hdst = [hA, hB][k % 2]
            for tb in range(4):
                jj = j * 4 + tb
                tbr = slice(tb * 512, (tb + 1) * 512)
                v.wait_ge(p.sem("sl"), b_sl + jj + 1)
                v.wait_ge(p.sem("umm"), b_umm + jj + 1)
                p.inc(v.tensor_tensor(basis_sb[:], silu_sb[jj % 2][:], ps[2 + tb % 2][:, 0:512],
                                      ALU.mult), "bu", 1)
                v.tensor_tensor(bw_sb[:], basis_sb[:],
                                wkb[:, k * TLOC + tb * 512:k * TLOC + (tb + 1) * 512], ALU.mult)
                if k == 0:
                    v.tensor_copy(hdst[:, tbr], bw_sb[:])
                else:
                    v.tensor_tensor(hdst[:, tbr], hsrc[:, tbr], bw_sb[:], ALU.add)
                if k == 7:
                    p.inc(v.tensor_copy(hbf[:, ic * TLOC + tb * 512:ic * TLOC + (tb + 1) * 512],
                                        hdst[:, tbr]), "hcv", 1)

    if dbg is not None and "d_hbf" in dbg:
        p.new_block()
        blk = p.blk

        @blk.gpsimd
        def _(g_):
            p.inc(g_.dma_start(out=dbg["d_hbf"][:, :], in_=hbf[:, :]), "dbs", 16)
            p.wait(g_, "dbs")
    ar.free(*wg_sb, *wu_sb, *silu_sb, basis_sb, bw_sb, hA, hB, wkb)

    hs = [ar.alloc(f"{tag}_hs{i}", [128, 8 * TLOC], BF16) for i in range(2)]
    wgs_sb = [ar.alloc(f"{tag}_wgs{i}", [128, 2048], BF16) for i in range(2)]
    wus_sb = [ar.alloc(f"{tag}_wus{i}", [128, 2048], BF16) for i in range(2)]
    silu2 = [ar.alloc(f"{tag}_sl2{i}", [128, 512], BF16) for i in range(2)]
    p.new_block()
    blk = p.blk
    b_gmm, b_umm, b_sl, b_bu = p.base("gmm", "umm", "sl", "bu")
    WL = p.plan("wl", 2, 16, unit=32)
    wsg_ap = io["L0_ffn_shared_w_gate"]
    wsu_ap = io["L0_ffn_shared_w_up"]

    @blk.gpsimd
    def _(g_):
        for j in range(16):
            c0 = j * 128
            if j >= 2:
                g_.wait_ge(p.sem("bu"), b_bu + 4 * (j - 1))
            p.inc(g_.dma_start(out=wgs_sb[j % 2][:, :].rearrange("p (c f) -> p c f", c=16),
                               in_=wsg_ap[:, c0:c0 + 128].rearrange("(c p) f -> p c f", p=128)), WL[j][0], 16)
            tgt = p.inc(g_.dma_start(out=wus_sb[j % 2][:, :].rearrange("p (c f) -> p c f", c=16),
                               in_=wsu_ap[:, c0:c0 + 128].rearrange("(c p) f -> p c f", p=128)), WL[j][0], 16)
            assert tgt == WL[j][1]

    @blk.tensor
    def _(t):
        for j in range(16):
            t.wait_ge(p.sem(WL[j][0]), WL[j][1])
            for tb in range(4):
                jj = j * 4 + tb
                if jj >= 2:
                    t.wait_ge(p.sem("sl"), b_sl + jj - 1)
                    t.wait_ge(p.sem("bu"), b_bu + jj - 1)
                for dc in range(16):
                    mm = t.matmul(ps[tb % 2][:, 0:512], wgs_sb[j % 2][:, dc * 128:(dc + 1) * 128],
                                  xlt[:, dc * TLOC + tb * 512: dc * TLOC + (tb + 1) * 512],
                                  start=(dc == 0), stop=(dc == 15))
                p.inc(mm, "gmm", 1)
                for dc in range(16):
                    mm = t.matmul(ps[2 + tb % 2][:, 0:512], wus_sb[j % 2][:, dc * 128:(dc + 1) * 128],
                                  xlt[:, dc * TLOC + tb * 512: dc * TLOC + (tb + 1) * 512],
                                  start=(dc == 0), stop=(dc == 15))
                p.inc(mm, "umm", 1)

    @blk.scalar
    def _(se):
        se = VD(se)
        for j in range(16):
            for tb in range(4):
                jj = j * 4 + tb
                se.wait_ge(p.sem("gmm"), b_gmm + jj + 1)
                if jj >= 2:
                    se.wait_ge(p.sem("bu"), b_bu + jj - 1)
                p.inc(se.activation(silu2[jj % 2][:], ps[tb % 2][:, 0:512], AF.Silu), "sl", 1)

    @blk.vector
    def _(v):
        v = VD(v)
        for j in range(16):
            for tb in range(4):
                jj = j * 4 + tb
                v.wait_ge(p.sem("sl"), b_sl + jj + 1)
                v.wait_ge(p.sem("umm"), b_umm + jj + 1)
                p.inc(v.tensor_tensor(
                    hs[j // 8][:, (j % 8) * TLOC + tb * 512:(j % 8) * TLOC + (tb + 1) * 512],
                    silu2[jj % 2][:], ps[2 + tb % 2][:, 0:512], ALU.mult), "bu", 1)

    if dbg is not None and "d_hs" in dbg:
        p.new_block()
        blk = p.blk

        @blk.gpsimd
        def _(g_):
            p.inc(g_.dma_start(out=dbg["d_hs"][:, :], in_=hs[0][:, :]), "dbs", 16)
            p.inc(g_.dma_start(out=dbg["d_hs2"][:, :], in_=hs[1][:, :]), "dbs", 16)
            p.wait(g_, "dbs")
    ar.free(*wgs_sb, *wus_sb, *silu2, xlt)

    wdm = [ar.alloc(f"{tag}_wdm{i}", [128, 8 * 512], BF16) for i in range(2)]
    wsdm = [ar.alloc(f"{tag}_wsdm{i}", [128, 16 * 512], BF16) for i in range(2)]
    xo_sb = [ar.alloc(f"{tag}_xo{i}", [128, 512], F32) for i in range(3)]
    xn_sb = [ar.alloc(f"{tag}_xn{i}", [128, 512], F32) for i in range(2)]
    p.new_block()
    blk = p.blk
    b_emm, b_ef, b_xn = p.base("emm", "ef", "xn")
    WL = p.plan("wl", 2, 4, unit=32)
    EL = p.plan("el", 3, 64, q="s")
    ES = p.plan("es", 2, 64)
    wd_ap = io["L0_ffn_w_down"]
    wsd_ap = io["L0_ffn_shared_w_down"]

    def emit_wload(g_, m):
        if m >= 2:
            g_.wait_ge(p.sem("ef"), b_ef + (m - 1) * 16)
        p.inc(g_.dma_start(out=wdm[m % 2][:, :].rearrange("p (c f) -> p c f", c=8),
                           in_=wd_ap[:, m * 512:(m + 1) * 512].rearrange("(c p) f -> p c f", p=128)), WL[m][0], 16)
        tgt = p.inc(g_.dma_start(out=wsdm[m % 2][:, :].rearrange("p (c f) -> p c f", c=16),
                           in_=wsd_ap[:, m * 512:(m + 1) * 512].rearrange("(c p) f -> p c f", p=128)), WL[m][0], 16)
        assert tgt == WL[m][1]

    @blk.sync
    def _(s):
        for m in range(4):
            for t in range(16):
                e = m * 16 + t
                if e >= 3:
                    s.wait_ge(p.sem("xn"), b_xn + e - 2)
                p.inc(s.dma_start(out=xo_sb[e % 3][:],
                                  in_=x_ap[t * 128:(t + 1) * 128, m * 512:(m + 1) * 512]), EL[e][0], 16)

    @blk.tensor
    def _(t_):
        p.wait(t_, "hcv")
        for m in range(4):
            t_.wait_ge(p.sem(WL[m][0]), WL[m][1])
            for t in range(16):
                e = m * 16 + t
                if e >= 2:
                    t_.wait_ge(p.sem("ef"), b_ef + e - 1)
                bank = ps[4 + e % 2]
                for ic in range(8):
                    t_.matmul(bank[:, 0:512], hbf[:, ic * TLOC + t * 128:ic * TLOC + (t + 1) * 128],
                              wdm[m % 2][:, ic * 512:(ic + 1) * 512],
                              start=(ic == 0), stop=False)
                for i2 in range(16):
                    mm = t_.matmul(bank[:, 0:512],
                                   hs[i2 // 8][:, (i2 % 8) * TLOC + t * 128:(i2 % 8) * TLOC + (t + 1) * 128],
                                   wsdm[m % 2][:, i2 * 512:(i2 + 1) * 512],
                                   start=False, stop=(i2 == 15))
                p.inc(mm, "emm", 1)

    @blk.vector
    def _(v):
        v = VD(v)
        for m in range(4):
            for t in range(16):
                e = m * 16 + t
                v.wait_ge(p.sem("emm"), b_emm + e + 1)
                v.wait_ge(p.sem(EL[e][0]), EL[e][1])
                if e >= 2:
                    v.wait_ge(p.sem(ES[e - 2][0]), ES[e - 2][1])
                p.inc(v.scalar_tensor_tensor(xn_sb[e % 2][:], ps[4 + e % 2][:, 0:512], 0.5,
                                             xo_sb[e % 3][:], op0=ALU.mult, op1=ALU.add), "ef", 1)
                p.inc(v.bn_stats(statsE[:, t * 24 + m * 6:t * 24 + (m + 1) * 6],
                                 xn_sb[e % 2][:]), "xn", 1)

    @blk.gpsimd
    def _(g_):
        emit_wload(g_, 0)
        emit_wload(g_, 1)
        for m in range(4):
            for t in range(16):
                e = m * 16 + t
                g_.wait_ge(p.sem("xn"), b_xn + e + 1)
                p.inc(g_.dma_start(out=xo_ap[t * 128:(t + 1) * 128, m * 512:(m + 1) * 512],
                                   in_=xn_sb[e % 2][:]), ES[e][0], 16)
                if e in (14, 30):
                    emit_wload(g_, 2 + (e - 14) // 16)
        for i in range(2):
            g_.wait_ge(p.sem(ES[63 - i][0]), ES[63 - i][1])

    ar.free(*wdm, *wsdm, *xo_sb, *xn_sb, hbf, *hs)

    emit_lnpass(p, io, g, x_dst, xln_dst, statsE, tag)
    ar.free(statsE)


def emit_lnpass(p, io, g, x_src, xln_dst, statsE, tag, tps=24):
    nc = p.nc
    ar = p.ar
    x_ap, xln_ap = io[x_src], io[xln_dst]
    mv = ar.alloc(f"{tag}_mvF", [128, 32], F32)
    rstd = ar.alloc(f"{tag}_rsF", [128, 16], F32)
    rstd2 = ar.alloc(f"{tag}_rs2F", [128, 16], F32)
    xs = [ar.alloc(f"{tag}_xF{i}", [128, D], F32) for i in range(3)]
    xlb = [ar.alloc(f"{tag}_xlF{i}", [128, D], BF16) for i in range(2)]
    p.new_block()
    blk = p.blk
    b_vs, b_rs, b_nm = p.base("vs", "rs", "nm")
    LD = p.plan("Lld", 3, 16, q="s")
    ST = p.plan("Lst", 2, 16)

    @blk.vector
    def _(v):
        v = VD(v)
        v.drain()
        for t in range(16):
            ag = v.bn_aggr(mv[:, 2 * t:2 * t + 2], statsE[:, t * tps:(t + 1) * tps])
        p.inc(ag, "vs", 1)

    @blk.scalar
    def _(se):
        se = VD(se)
        p.wait(se, "vs")
        p.inc(se.activation(rstd[:], mv[:, 1:32:2], AF.Sqrt, bias=g["eps"][:, 0:1], scale=1.0), "rs", 1)

    @blk.vector
    def _(v):
        v = VD(v)
        p.wait(v, "rs")
        p.inc(v.reciprocal(rstd2[:], rstd[:]), "rs", 1)

    @blk.sync
    def _(s):
        for t in range(16):
            if t >= 3:
                s.wait_ge(p.sem("nm"), b_nm + t - 2)
            tgt = p.inc(s.dma_start(out=xs[t % 3][:], in_=x_ap[t * 128:(t + 1) * 128, :]), LD[t][0], 16)
            assert tgt == LD[t][1]

    @blk.vector
    def _(v):
        v = VD(v)
        v.wait_ge(p.sem("rs"), b_rs + 2)
        for t in range(16):
            v.wait_ge(p.sem(LD[t][0]), LD[t][1])
            if t >= 2:
                v.wait_ge(p.sem(ST[t - 2][0]), ST[t - 2][1])
            p.inc(v.tensor_scalar(xlb[t % 2][:], xs[t % 3][:], mv[:, 2 * t:2 * t + 1],
                                  rstd2[:, t:t + 1], ALU.subtract, ALU.mult), "nm", 1)

    @blk.gpsimd
    def _(g_):
        for t in range(16):
            g_.wait_ge(p.sem("nm"), b_nm + t + 1)
            p.inc(g_.dma_start(out=xln_ap[t * 128:(t + 1) * 128, :], in_=xlb[t % 2][:]), ST[t][0], 16)
        for i in range(2):
            g_.wait_ge(p.sem(ST[15 - i][0]), ST[15 - i][1])
    ar.free(mv, rstd, rstd2, *xs, *xlb)


def ph_castw(p, io, pairs):
    p.phase_boundary()
    p.new_block()
    blk = p.blk

    @blk.gpsimd
    def _(g_):
        for srcn, dstn in pairs:
            p.inc(g_.dma_start(out=io[dstn][:, :], in_=io[srcn][:, :]), "cw", 16)
        p.wait(g_, "cw")


def ph_l0qkv(p, io, g, xln_src, tag, do_ag=True):
    p.phase_boundary()
    nc = p.nc
    ar = p.ar
    xln_ap = io[xln_src]
    ps = g["ps"]
    xlt = ar.alloc(f"{tag}_xlt", [128, 16 * TLOC], BF16)
    wt = ar.alloc(f"{tag}_wt", [128, 16 * TLOC], BF16)
    qsb = ar.alloc(f"{tag}_qsb", [128, TLOC], F32)
    qsq = ar.alloc(f"{tag}_qsq", [128, TLOC], F32)
    scl = ar.alloc(f"{tag}_scl", [128, TLOC], F32)
    rowb = ar.alloc(f"{tag}_rowb", [1, 2048], F32)
    rowb2 = ar.alloc(f"{tag}_rowb2", [1, 2048], F32)
    grow = ar.alloc(f"{tag}_grow", [1, 256], F32)
    onb = [ar.alloc(f"{tag}_onb{i}", [128, TLOC], BF16) for i in range(2)]
    vsb = [ar.alloc(f"{tag}_vsb{i}", [128, 512], BF16) for i in range(2)]

    p.new_block()
    blk = p.blk
    b_qmm, b_qev, b_qsq, b_qss, b_qsr, b_qrc, b_qbc, b_qsc, b_qmu, b_qst = p.base(
        "qmm", "qev", "qsq", "qss", "qsr", "qrc", "qbc", "qsc", "qmu", "qst")
    HS = 128 ** -0.5

    @blk.sync
    def _(s):
        p.inc(s.dma_start(out=grow[0:1, 0:128], in_=io["L0_attn_gq"][None, :]), "gld", 16)
        p.inc(s.dma_start(out=grow[0:1, 128:256], in_=io["L0_attn_gk"][None, :]), "gld", 16)

    @blk.vector
    def _(v):
        v = VD(v)
        p.wait(v, "gld")
        p.inc(v.tensor_scalar(grow[0:1, 0:128], grow[0:1, 0:128], HS, None, ALU.mult), "gsc", 1)

    @blk.scalar
    def _(se):
        se = VD(se)
        emit_xlnT_load(p, se, xln_ap, xlt, "xlt")

    def emit_wT_load(se, wname):
        se.wait_ge(p.sem("qmm"), p.cnt_of("qmm"))
        for dc in range(16):
            p.inc(se.dma_start(out=wt[:, dc * TLOC:(dc + 1) * TLOC],
                               in_=io[wname][:, dc * 128:(dc + 1) * 128],
                               transpose=True), "wtl", 16)

    def emit_qk(which, wname, dst_ap, grow_sl):
        b_wtl = p.cnt_of("wtl")

        @blk.scalar
        def _(se):
            se = VD(se)
            emit_wT_load(se, wname)

        bq = {n: p.cnt_of(n) for n in ["qmm", "qev", "qsq", "qss", "qsr", "qrc", "qbc", "qsc", "qmu"]}
        QN = p.plan("qnst", 2, 16, unit=16)

        @blk.tensor
        def _(t):
            p.wait(t, "xlt")
            t.wait_ge(p.sem("wtl"), b_wtl + 256)
            if which == 0:
                p.wait(t, "gsc")
            for m in range(16):
                for tb in range(4):
                    need = bq["qsc"] + (m - 1) * 4 + tb + 1
                    if need > 0:
                        t.wait_ge(p.sem("qsc"), need)
                    for dc in range(16):
                        mm = t.matmul(ps[tb][:, 0:512], wt[:, dc * TLOC + m * 128:dc * TLOC + (m + 1) * 128],
                                      xlt[:, dc * TLOC + tb * 512:dc * TLOC + (tb + 1) * 512],
                                      start=(dc == 0), stop=(dc == 15))
                    p.inc(mm, "qmm", 1)
                for tb in range(4):
                    t.wait_ge(p.sem("qsq"), bq["qsq"] + m + 1)
                    need = bq["qsr"] + (m - 1) * 4 + tb + 1
                    if need > 0:
                        t.wait_ge(p.sem("qsr"), need)
                    p.inc(t.matmul(ps[4 + tb][0:1, 0:512], g["onesf_c"][:, 0:1],
                                   qsq[:, tb * 512:(tb + 1) * 512], start=True, stop=True), "qss", 1)
                for tb in range(4):
                    t.wait_ge(p.sem("qrc"), bq["qrc"] + m * 4 + tb + 1)
                    t.wait_ge(p.sem("qev"), bq["qev"] + m * 4 + tb + 1)
                    p.inc(t.matmul(ps[tb][:, 0:512], grow_sl,
                                   rowb2[0:1, tb * 512:(tb + 1) * 512], start=True, stop=True), "qbc", 1)

        @blk.scalar
        def _(se):
            se = VD(se)
            for m in range(16):
                for tb in range(4):
                    se.wait_ge(p.sem("qmm"), bq["qmm"] + m * 4 + tb + 1)
                    need = bq["qmu"] + (m - 1) * 4 + tb + 1
                    if need > 0:
                        se.wait_ge(p.sem("qmu"), need)
                    p.inc(se.copy(qsb[:, tb * 512:(tb + 1) * 512], ps[tb][:, 0:512]), "qev", 1)
                if bq["qss"] + m * 4 > 0:
                    se.wait_ge(p.sem("qss"), bq["qss"] + m * 4)
                p.inc(se.activation(qsq[:], qsb[:], AF.Square), "qsq", 1)
                for tb in range(4):
                    se.wait_ge(p.sem("qss"), bq["qss"] + m * 4 + tb + 1)
                    p.inc(se.activation(rowb[0:1, tb * 512:(tb + 1) * 512], ps[4 + tb][0:1, 0:512],
                                        AF.Sqrt, bias=g["eps"][0:1, 0:1], scale=1.0 / 128.0), "qsr", 1)
                for tb in range(4):
                    se.wait_ge(p.sem("qbc"), bq["qbc"] + m * 4 + tb + 1)
                    p.inc(se.copy(scl[:, tb * 512:(tb + 1) * 512], ps[tb][:, 0:512]), "qsc", 1)

        @blk.vector
        def _(v):
            v = VD(v)
            for m in range(16):
                for tb in range(4):
                    v.wait_ge(p.sem("qsr"), bq["qsr"] + m * 4 + tb + 1)
                    need = bq["qbc"] + (m - 1) * 4 + tb + 1
                    if need > 0:
                        v.wait_ge(p.sem("qbc"), need)
                    p.inc(v.reciprocal(rowb2[0:1, tb * 512:(tb + 1) * 512],
                                       rowb[0:1, tb * 512:(tb + 1) * 512]), "qrc", 1)
                for tb in range(4):
                    v.wait_ge(p.sem("qsc"), bq["qsc"] + m * 4 + tb + 1)
                    if m >= 2:
                        v.wait_ge(p.sem(QN[m - 2][0]), QN[m - 2][1])
                    else:
                        v.wait_ge(p.sem(QN[m][0]), QN[m][1] - 16)
                    p.inc(v.tensor_tensor(onb[m % 2][:, tb * 512:(tb + 1) * 512],
                                          qsb[:, tb * 512:(tb + 1) * 512],
                                          scl[:, tb * 512:(tb + 1) * 512], ALU.mult), "qmu", 1)

        @blk.gpsimd
        def _(g_):
            for m in range(16):
                g_.wait_ge(p.sem("qmu"), bq["qmu"] + (m + 1) * 4)
                tgt = p.inc(g_.dma_start(out=dst_ap[m * 128:(m + 1) * 128, :], in_=onb[m % 2][:]),
                            QN[m][0], 16)
                assert tgt == QN[m][1]
            for i in range(2):
                g_.wait_ge(p.sem(QN[15 - i][0]), QN[15 - i][1])

    emit_qk(0, "wqt_bf", io["qn_d"].rearrange("h d t -> (h d) t"), grow[0:1, 0:128])
    emit_qk(1, "wkt_bf", io["kn_d"], grow[0:1, 128:256])

    b_wtl = p.cnt_of("wtl")
    bq = {n: p.cnt_of(n) for n in ["qmm", "qev"]}
    VST = p.plan("vst", 2, 64, unit=64)

    @blk.scalar
    def _(se):
        se = VD(se)
        emit_wT_load(se, "wvt_bf")

    @blk.tensor
    def _(t):
        t.wait_ge(p.sem("wtl"), b_wtl + 256)
        t.wait_ge(p.sem("qsc"), p.cnt_of("qsc"))
        for tt in range(16):
            for mb in range(4):
                e = tt * 4 + mb
                if e >= 2:
                    t.wait_ge(p.sem("qev"), bq["qev"] + e - 1)
                for dc in range(16):
                    mm = t.matmul(ps[e % 2][:, 0:512], xlt[:, dc * TLOC + tt * 128:dc * TLOC + (tt + 1) * 128],
                                  wt[:, dc * TLOC + mb * 512:dc * TLOC + (mb + 1) * 512],
                                  start=(dc == 0), stop=(dc == 15))
                p.inc(mm, "qmm", 1)

    @blk.vector
    def _(v):
        v = VD(v)
        for tt in range(16):
            for mb in range(4):
                e = tt * 4 + mb
                v.wait_ge(p.sem("qmm"), bq["qmm"] + e + 1)
                if e >= 2:
                    v.wait_ge(p.sem(VST[e - 2][0]), VST[e - 2][1])
                p.inc(v.tensor_copy(vsb[e % 2][:], ps[e % 2][:, 0:512]), "qev", 1)

    @blk.gpsimd
    def _(g_):
        vd_r = io["v_d"].rearrange("(h t) e -> h t e", t=TLOC)
        for tt in range(16):
            for mb in range(4):
                e = tt * 4 + mb
                g_.wait_ge(p.sem("qev"), bq["qev"] + e + 1)
                for hh in range(4):
                    h = mb * 4 + hh
                    tgt = p.inc(g_.dma_start(out=vd_r[h, tt * 128:(tt + 1) * 128, :],
                                             in_=vsb[e % 2][:, hh * 128:(hh + 1) * 128]), VST[e][0], 16)
                assert tgt == VST[e][1]
        for i in range(2):
            g_.wait_ge(p.sem(VST[63 - i][0]), VST[63 - i][1])

    ar.free(xlt, wt, qsb, qsq, scl, rowb, rowb2, grow, *onb, *vsb)

    if not do_ag:
        return
    p.new_block()
    blk = p.blk

    @blk.gpsimd
    def _(g_):
        pairs = [[0, 1], [2, 3], [4, 5], [6, 7]]
        p.inc(g_.collective_compute("AllGather", ALU.bypass, replica_groups=pairs,
                                    ins=[io["kn_d"]], outs=[io["kn_ag"]]), "cc", 1)
        p.inc(g_.collective_compute("AllGather", ALU.bypass, replica_groups=pairs,
                                    ins=[io["v_d"]], outs=[io["v_ag"]]), "cc", 1)
        p.wait(g_, "cc")


def ph_l0attn(p, io, g, tag):
    p.phase_boundary()
    nc = p.nc
    ar = p.ar
    ps = g["ps"]
    knh = [ar.alloc(f"{tag}_knh{i}", [128, 4096], BF16) for i in range(2)]
    vh = [ar.alloc(f"{tag}_vh{i}", [128, 4096], BF16) for i in range(2)]
    qh = [ar.alloc(f"{tag}_qh{i}", [128, TLOC], BF16) for i in range(2)]
    sco = [ar.alloc(f"{tag}_sco{i}", [128, 8192], F32) for i in range(2)]
    pT = [ar.alloc(f"{tag}_pT{i}", [128, 8192], BF16) for i in range(2)]
    mskb = ar.alloc(f"{tag}_mskb", [128, 32 * 256], BF16)
    bmall = [ar.alloc(f"{tag}_bm{i}", [128, 32], F32) for i in range(2)]
    mu1 = [ar.alloc(f"{tag}_mu1{i}", [128, 1], F32) for i in range(2)]
    negr = ar.alloc(f"{tag}_negr", [128, 32], F32)
    smal = ar.alloc(f"{tag}_smal", [1, 8], F32)
    rzrow = ar.alloc(f"{tag}_rzr", [1, 512], F32)
    rzb = [ar.alloc(f"{tag}_rzb{i}", [128, 256], F32) for i in range(2)]
    nmb = [ar.alloc(f"{tag}_nmb{i}", [128, 1], F32) for i in range(2)]
    aout = [ar.alloc(f"{tag}_ao{i}", [128, TLOC], BF16) for i in range(2)]
    kn_r = io["kn_ag"].rearrange("(x h d) t -> x h d t", h=NH, d=128)
    v_r = io["v_ag"].rearrange("(x h t) e -> x h t e", h=NH, t=TLOC)
    qn_r = io["qn_d"]
    ao_r = io["aout_d"].rearrange("(h d) t -> h d t", d=128)
    NB = sum(EXT)
    NSL = NH * NCHUNK
    exts = [EXT[s % 8] for s in range(NSL)]
    CB = [0]
    for e in exts:
        CB.append(CB[-1] + e)

    p.new_block()
    blk = p.blk
    SEMS = ["amm", "aev", "aex", "apv", "ax1", "ax2", "ax3", "ax4",
            "azm", "azr", "arm", "arx", "anm", "amk", "amkv", "amsk"]
    b = {n: p.cnt_of(n) for n in SEMS}
    AHL = p.plan("ahl", 2, NH, unit=49 * 16, q="s")
    AST = p.plan("ast", 2, NH, unit=16)

    @blk.sync
    def _(s):
        for h in range(NH):
            if h >= 2:
                s.wait_ge(p.sem("anm"), b["anm"] + (h - 1) * 8)
            for cabs in range(16):
                x, l = ABS2LOC[cabs]
                p.inc(s.dma_start(out=knh[h % 2][:, cabs * 256:(cabs + 1) * 256],
                                  in_=kn_r[x, h, :, l * 256:(l + 1) * 256]), AHL[h][0], 16)
                for a in range(2):
                    p.inc(s.dma_start(
                        out=vh[h % 2][:, (2 * cabs + a) * 128:(2 * cabs + a + 1) * 128],
                        in_=v_r[x, h, l * 256 + a * 128:l * 256 + (a + 1) * 128, :]),
                        AHL[h][0], 16)
            tgt = p.inc(s.dma_start(out=qh[h % 2][:], in_=qn_r[h]), AHL[h][0], 16)
            assert tgt == AHL[h][1]

    @blk.gpsimd
    def _(g_):
        p.inc(g_.dma_start(out=negr[:, :], in_=io["maskr"][:, :]), "amk", 16)

    @blk.vector
    def _(v):
        v = VD(v)
        v.wait_ge(p.sem("amk"), b["amk"] + 16)
        p.inc(v.tensor_scalar(negr[:], negr[:], -1.0, None, ALU.mult), "amkv", 1)
        for mi in range(32):
            mm = v.tensor_scalar(mskb[:, mi * 256:(mi + 1) * 256], g["Dt"][:],
                                 negr[:, mi:mi + 1], NEG, ALU.is_gt, ALU.mult)
        p.inc(mm, "amsk", 1)

    @blk.tensor
    def _(t):
        p.wait(t, "amkv")
        t.wait_ge(p.sem("amsk"), b["amsk"] + 1)

        def pe_early(j):
            pr = j % 2
            t.wait_ge(p.sem("ax1"), b["ax1"] + j + 1)
            if j >= 1:
                t.wait_ge(p.sem("ax3"), b["ax3"] + j)
            p.inc(t.transpose(ps[5][0:1, pr * 128:pr * 128 + 128], mu1[pr][:, 0:1],
                              g["ident"][:]), "ax2", 1)

        def pe_bcast(j):
            pr = j % 2
            t.wait_ge(p.sem("ax3"), b["ax3"] + j + 1)
            if j >= 1:
                t.wait_ge(p.sem("aex"), b["aex"] + CB[j])
            p.inc(t.matmul(ps[2][:, pr:pr + 1], g["onesf_r"][0:1, :],
                           smal[0:1, 2 + pr:3 + pr], start=True, stop=True), "ax4", 1)

        def pe_pvz_head(j):
            if j >= 2:
                t.wait_ge(p.sem("anm"), b["anm"] + j - 1)
            if j >= 1:
                t.wait_ge(p.sem("azr"), b["azr"] + j)

        def pe_pvz_blk(j, kb):
            pr = j % 2
            h_ = j // 8
            ext_ = exts[j]
            t.wait_ge(p.sem("aex"), b["aex"] + CB[j] + (kb // 2 + 1) * 2)
            p.inc(t.matmul(ps[3 + pr][:, 0:256], vh[h_ % 2][:, kb * 128:(kb + 1) * 128],
                           pT[pr][:, kb * 256:(kb + 1) * 256],
                           start=(kb == 0), stop=(kb == ext_ - 1)), "apv", 1)
            p.inc(t.matmul(ps[6][0:1, pr * 256:(pr + 1) * 256], g["onesb_c"][:, 0:1],
                           pT[pr][:, kb * 256:(kb + 1) * 256],
                           start=(kb == 0), stop=(kb == ext_ - 1)), "azm", 1)

        def pe_rzb(j):
            pr = j % 2
            t.wait_ge(p.sem("azr"), b["azr"] + j + 1)
            if j >= 2:
                t.wait_ge(p.sem("arx"), b["arx"] + j)
            p.inc(t.matmul(ps[7][:, pr * 256:(pr + 1) * 256], g["onesf_r"][0:1, :],
                           rzrow[0:1, pr * 256:(pr + 1) * 256], start=True, stop=True), "arm", 1)

        def pe_scores(sl, k0, k1):
            h, l = sl // 8, sl % 8
            ext_ = exts[sl]
            for kb in range(k0, k1):
                gb = CB[sl] + kb
                if gb >= 2:
                    t.wait_ge(p.sem("aev"), b["aev"] + gb - 1)
                masked = kb >= ext_ - 4
                mm = t.matmul(ps[gb % 2][:, 0:256], knh[h % 2][:, kb * 128:(kb + 1) * 128],
                              qh[h % 2][:, l * 256:(l + 1) * 256],
                              start=True, stop=not masked)
                if masked:
                    mi = l * 4 + (kb - (ext_ - 4))
                    mm = t.matmul(ps[gb % 2][:, 0:256], g["identb"][:],
                                  mskb[:, mi * 256:(mi + 1) * 256], start=False, stop=True)
                p.inc(mm, "amm", 1)

        LAG = 4
        for sl in range(NSL):
            h, l = sl // 8, sl % 8
            ext = exts[sl]
            epv = exts[sl - 1] if sl >= 1 else 0
            if sl >= 1:
                pe_early(sl - 1)
            if l == 0:
                t.wait_ge(p.sem(AHL[h][0]), AHL[h][1])
            pre = min(4, ext)
            pe_scores(sl, 0, pre)
            if sl >= 1:
                pe_bcast(sl - 1)
                pe_pvz_head(sl - 1)
            for k in range(pre, max(ext, epv + LAG)):
                if k < ext:
                    pe_scores(sl, k, k + 1)
                if 0 <= k - LAG < epv:
                    pe_pvz_blk(sl - 1, k - LAG)
            if sl >= 2:
                pe_rzb(sl - 2)
        pe_early(NSL - 1)
        pe_bcast(NSL - 1)
        pe_pvz_head(NSL - 1)
        for kb in range(exts[NSL - 1]):
            pe_pvz_blk(NSL - 1, kb)
        pe_rzb(NSL - 2)
        pe_rzb(NSL - 1)

    @blk.vector
    def _(v):
        v = VD(v)
        p.wait(v, "amkv")

        def dve_negmu(j):
            pr = j % 2
            v.wait_ge(p.sem("ax2"), b["ax2"] + j + 1)
            if j >= 2:
                v.wait_ge(p.sem("ax4"), b["ax4"] + j - 1)
            p.inc(v.tensor_reduce(smal[0:1, 2 + pr:3 + pr], ps[5][0:1, pr * 128:pr * 128 + 128],
                                  axis=AX.X, op=ALU.max, negate=True), "ax3", 1)

        def dve_rz(j):
            pr = j % 2
            v.wait_ge(p.sem("azm"), b["azm"] + CB[j + 1])
            if j >= 2:
                v.wait_ge(p.sem("arm"), b["arm"] + j - 1)
            p.inc(v.reciprocal(rzrow[0:1, pr * 256:(pr + 1) * 256],
                               ps[6][0:1, pr * 256:(pr + 1) * 256]), "azr", 1)

        def dve_aout(j):
            pr = j % 2
            h_, l_ = j // 8, j % 8
            v.wait_ge(p.sem("apv"), b["apv"] + CB[j + 1])
            v.wait_ge(p.sem("arx"), b["arx"] + j + 1)
            if l_ == 0 and h_ >= 2:
                v.wait_ge(p.sem(AST[h_ - 2][0]), AST[h_ - 2][1])
            p.inc(v.tensor_tensor(aout[h_ % 2][:, l_ * 256:(l_ + 1) * 256],
                                  ps[3 + pr][:, 0:256], rzb[pr][:], ALU.mult), "anm", 1)

        for sl in range(NSL):
            pr = sl % 2
            ext = exts[sl]
            if sl >= 1:
                dve_negmu(sl - 1)
            if sl >= 2:
                dve_rz(sl - 2)
            if sl >= 2:
                v.wait_ge(p.sem("aex"), b["aex"] + CB[sl - 1])
            for kb in range(ext):
                gb = CB[sl] + kb
                v.wait_ge(p.sem("amm"), b["amm"] + gb + 1)
                p.inc(v.tensor_scalar(sco[pr][:, kb * 256:(kb + 1) * 256],
                                      ps[gb % 2][:, 0:256], 1.0, NEG,
                                      ALU.mult, ALU.max,
                                      accum_out=bmall[pr][:, kb:kb + 1]), "aev", 1)
            if sl >= 2:
                v.wait_ge(p.sem("ax2"), b["ax2"] + sl - 1)
            p.inc(v.tensor_reduce(mu1[pr][:, 0:1], bmall[pr][:, 0:ext], axis=AX.X,
                                  op=ALU.max), "ax1", 1)
            if sl >= 2:
                dve_aout(sl - 2)
        dve_negmu(NSL - 1)
        dve_rz(NSL - 2)
        dve_rz(NSL - 1)
        dve_aout(NSL - 2)
        dve_aout(NSL - 1)

    @blk.scalar
    def _(se):
        se = VD(se)

        def act_exps(j):
            pr = j % 2
            ext_ = exts[j]
            se.wait_ge(p.sem("ax4"), b["ax4"] + j + 1)
            se.copy(nmb[pr][:], ps[2][:, pr:pr + 1])
            if j >= 2:
                se.wait_ge(p.sem("apv"), b["apv"] + CB[j - 1])
                se.wait_ge(p.sem("azm"), b["azm"] + CB[j - 1])
            for kb in range(0, ext_, 2):
                gb = CB[j] + kb
                se.wait_ge(p.sem("aev"), b["aev"] + gb + 2)
                p.inc(se.activation(pT[pr][:, kb * 256:(kb + 2) * 256],
                                    sco[pr][:, kb * 256:(kb + 2) * 256],
                                    AF.Exp, bias=nmb[pr][:, 0:1], scale=1.0), "aex", 2)

        def act_rzb(j):
            pr = j % 2
            se.wait_ge(p.sem("arm"), b["arm"] + j + 1)
            if j >= 2:
                se.wait_ge(p.sem("anm"), b["anm"] + j - 1)
            p.inc(se.copy(rzb[pr][:], ps[7][:, pr * 256:(pr + 1) * 256]), "arx", 1)

        for sl in range(NSL):
            if sl >= 1:
                act_exps(sl - 1)
            if sl >= 2:
                act_rzb(sl - 2)
        act_exps(NSL - 1)
        act_rzb(NSL - 2)
        act_rzb(NSL - 1)

    @blk.gpsimd
    def _(g_):
        for h in range(NH):
            g_.wait_ge(p.sem("anm"), b["anm"] + (h + 1) * 8)
            tgt = p.inc(g_.dma_start(out=ao_r[h], in_=aout[h % 2][:]), AST[h][0], 16)
            assert tgt == AST[h][1]
        for i in range(2):
            g_.wait_ge(p.sem(AST[NH - 1 - i][0]), AST[NH - 1 - i][1])

    ar.free(*knh, *vh, *qh, *sco, *pT, mskb, *bmall, *mu1, negr, smal, rzrow, *rzb, *nmb, *aout)


def ph_l0proj(p, io, g, x_src, x_dst, xln_dst, tag, scale=1.0, wname="wot_bf", bias=None):
    p.phase_boundary()
    nc = p.nc
    ar = p.ar
    ps = g["ps"]
    x_ap, xo_ap = io[x_src], io[x_dst]
    asb = ar.alloc(f"{tag}_asb", [128, 16 * TLOC], BF16)
    wot = ar.alloc(f"{tag}_wot", [128, 16 * TLOC], BF16)
    statsE = ar.alloc(f"{tag}_stE", [128, 16 * 24], F32)
    xo_sb = [ar.alloc(f"{tag}_xo{i}", [128, 512], F32) for i in range(3)]
    xn_sb = [ar.alloc(f"{tag}_xn{i}", [128, 512], F32) for i in range(2)]
    p.new_block()
    blk = p.blk
    b_emm, b_ef, b_xn = p.base("emm", "ef", "xn")
    EL = p.plan("el", 3, 64, q="s")
    ES = p.plan("es", 2, 64)
    b_aol = p.cnt_of("aol")

    @blk.sync
    def _(s):
        for cc in range(16):
            p.inc(s.dma_start(out=asb[:, cc * TLOC:(cc + 1) * TLOC],
                              in_=io["aout_d"][cc * 128:(cc + 1) * 128, :]), "aol", 16)
        for m in range(4):
            for t in range(16):
                e = m * 16 + t
                if e >= 3:
                    s.wait_ge(p.sem("xn"), b_xn + e - 2)
                p.inc(s.dma_start(out=xo_sb[e % 3][:],
                                  in_=x_ap[t * 128:(t + 1) * 128, m * 512:(m + 1) * 512]), EL[e][0], 16)

    brow = ar.alloc(f"{tag}_br", [1, D], F32) if bias else None
    if bias:
        @blk.gpsimd
        def _(g_):
            p.inc(g_.dma_start(out=brow[0:1, :], in_=io[bias][None, :]), "pbl", 16)

    @blk.scalar
    def _(se):
        se = VD(se)
        emit_xlnT_load(p, se, io[wname], wot, "wtl")

    @blk.tensor
    def _(t_):
        t_.wait_ge(p.sem("aol"), b_aol + 256)
        p.wait(t_, "wtl")
        if bias:
            p.wait(t_, "pbl")
        for m in range(4):
            for t in range(16):
                e = m * 16 + t
                if e >= 2:
                    t_.wait_ge(p.sem("ef"), b_ef + e - 1)
                bank = ps[4 + e % 2]
                for cc in range(16):
                    mm = t_.matmul(bank[:, 0:512], asb[:, cc * TLOC + t * 128:cc * TLOC + (t + 1) * 128],
                                   wot[:, cc * TLOC + m * 512:cc * TLOC + (m + 1) * 512],
                                   start=(cc == 0), stop=(cc == 15 and not bias))
                if bias:
                    mm = t_.matmul(bank[:, 0:512], g["onesf_r"][0:1, 0:128],
                                   brow[0:1, m * 512:(m + 1) * 512], start=False, stop=True)
                p.inc(mm, "emm", 1)

    @blk.vector
    def _(v):
        v = VD(v)
        for m in range(4):
            for t in range(16):
                e = m * 16 + t
                v.wait_ge(p.sem("emm"), b_emm + e + 1)
                v.wait_ge(p.sem(EL[e][0]), EL[e][1])
                if e >= 2:
                    v.wait_ge(p.sem(ES[e - 2][0]), ES[e - 2][1])
                p.inc(v.scalar_tensor_tensor(xn_sb[e % 2][:], ps[4 + e % 2][:, 0:512], scale,
                                             xo_sb[e % 3][:], op0=ALU.mult, op1=ALU.add), "ef", 1)
                p.inc(v.bn_stats(statsE[:, t * 24 + m * 6:t * 24 + (m + 1) * 6],
                                 xn_sb[e % 2][:]), "xn", 1)

    @blk.gpsimd
    def _(g_):
        for m in range(4):
            for t in range(16):
                e = m * 16 + t
                g_.wait_ge(p.sem("xn"), b_xn + e + 1)
                p.inc(g_.dma_start(out=xo_ap[t * 128:(t + 1) * 128, m * 512:(m + 1) * 512],
                                   in_=xn_sb[e % 2][:]), ES[e][0], 16)
        for i in range(2):
            g_.wait_ge(p.sem(ES[63 - i][0]), ES[63 - i][1])

    ar.free(asb, wot, *xo_sb, *xn_sb)
    if bias:
        ar.free(brow)
    emit_lnpass(p, io, g, x_dst, xln_dst, statsE, tag)
    ar.free(statsE)


def _chunk3(w):
    a = (w // 3) // 16 * 16
    b = a
    return [(0, a), (a, a + b), (a + b, w)]


def ph_ghost(p, io, g, tag="gh", do_ag=True, wait_cc=False):
    p.phase_boundary()
    nc = p.nc
    ar = p.ar
    ps = g["ps"]
    lutb = ar.alloc(f"{tag}_lutb", [128, 16], F32)
    lrow = ar.alloc(f"{tag}_lrow", [1, 16], F32)
    idxb = [ar.alloc(f"{tag}_idx{i}", [128, 5504], BF16) for i in range(2)]
    eqt = ar.alloc(f"{tag}_eq", [128, 5504], BF16)
    acc = ar.alloc(f"{tag}_acc", [128, 5504], BF16)
    wob = [ar.alloc(f"{tag}_wo{i}", [128, 5504], BF16) for i in range(2)]
    p.new_block()
    blk = p.blk
    b_gc, b_gw, b_gcd = p.base("ggc", "ggw", "gcd")
    tiles = []
    for srcn, onm in [("gidxb_d", "gwT_sl"), ("uidxb_d", "uwT_sl")]:
        for t in range(2):
            tiles.append((srcn, t, 5504, io[onm][t * 128:(t + 1) * 128, :], 128))
    for t in range(6):
        r = min(128, 688 - t * 128)
        tiles.append(("didxb_d", t, 2048, io["dwT_sl"][t * 128:t * 128 + r, :], r))
    NT = len(tiles)
    IL = p.plan("gil", 2, NT, unit=48, q="s")
    GS = p.plan("gst", 2, NT)
    grp = [list(range(8))]

    @blk.sync
    def _(s):
        p.inc(s.dma_start(out=lrow[0:1, :], in_=io["L1_ffn_lut"][None, :]), "ggl", 16)

    @blk.tensor
    def _(t):
        p.wait(t, "ggl")
        p.inc(t.matmul(ps[7][:, 0:16], g["onesf_r"][0:1, :], lrow[0:1, :],
                       start=True, stop=True), "glb", 1)

    @blk.scalar
    def _(se):
        se = VD(se)
        p.wait(se, "glb")
        p.inc(se.copy(lutb[:], ps[7][:, 0:16]), "glc", 1)
        se.wait_ge(p.sem("gcd"), b_gcd + 48)
        for i, (srcn, t, w, osl, r) in enumerate(tiles):
            if i >= 2:
                se.wait_ge(p.sem("ggc"), b_gc + i - 1)
            for c0, c1 in _chunk3(w):
                tgt = p.inc(se.dma_start(out=idxb[i % 2][:, c0:c1],
                                         in_=io[srcn][c0:c1, t * 128:(t + 1) * 128],
                                         transpose=True), IL[i][0], 16)
            assert tgt == IL[i][1]

    @blk.vector
    def _(v):
        v = VD(v)
        p.wait(v, "glc")
        for i, (srcn, t, w, osl, r) in enumerate(tiles):
            v.wait_ge(p.sem(IL[i][0]), IL[i][1])
            buf = idxb[i % 2]
            v.tensor_scalar(acc[:, 0:w], buf[:, 0:w], 0.0, lutb[:, 0:1],
                            ALU.is_equal, ALU.mult)
            for j in range(1, 15):
                v.tensor_scalar(eqt[:, 0:w], buf[:, 0:w], float(j), lutb[:, j:j + 1],
                                ALU.is_equal, ALU.mult)
                v.tensor_tensor(acc[:, 0:w], acc[:, 0:w], eqt[:, 0:w], ALU.add)
            p.inc(v.tensor_scalar(eqt[:, 0:w], buf[:, 0:w], 15.0, lutb[:, 15:16],
                                  ALU.is_equal, ALU.mult), "ggc", 1)
            if i >= 2:
                v.wait_ge(p.sem(GS[i - 2][0]), GS[i - 2][1])
            p.inc(v.tensor_tensor(wob[i % 2][0:r, 0:w], acc[0:r, 0:w], eqt[0:r, 0:w],
                                  ALU.add), "ggw", 1)

    @blk.gpsimd
    def _(g_):
        p.inc(g_.dma_start(out=io["gidxb_d"][:, :], in_=io["gate_idx_sl"][:, :]), "gcd", 16)
        p.inc(g_.dma_start(out=io["uidxb_d"][:, :], in_=io["up_idx_sl"][:, :]), "gcd", 16)
        p.inc(g_.dma_start(out=io["didxb_d"][:, :], in_=io["down_idx_sl"][:, :]), "gcd", 16)
        for i, (srcn, t, w, osl, r) in enumerate(tiles):
            g_.wait_ge(p.sem("ggw"), b_gw + i + 1)
            tgt = p.inc(g_.dma_start(out=osl, in_=wob[i % 2][0:r, 0:w]), GS[i][0], 16)
            assert tgt == GS[i][1]
            if do_ag and i in (1, 3, NT - 1):
                g_.wait_ge(p.sem(GS[i - 1][0]), GS[i - 1][1])
                g_.wait_ge(p.sem(GS[i][0]), GS[i][1])
                if i == 3:
                    g_.wait_ge(p.sem(GS[2][0]), GS[2][1])
                if i == NT - 1:
                    for ii in range(4, NT - 1):
                        g_.wait_ge(p.sem(GS[ii][0]), GS[ii][1])
                nm = {1: ("gwT_sl", "gwT_d"), 3: ("uwT_sl", "uwT_d"),
                      NT - 1: ("dwT_sl", "dwT_d")}[i]
                p.inc(g_.collective_compute("AllGather", ALU.bypass, replica_groups=grp,
                                            ins=[io[nm[0]]], outs=[io[nm[1]]]), "cc", 1)
        for i in range(2):
            g_.wait_ge(p.sem(GS[NT - 1 - i][0]), GS[NT - 1 - i][1])
        if wait_cc:
            p.wait(g_, "cc")

    ar.free(lutb, lrow, *idxb, eqt, acc, *wob)


def ph_l1ffn(p, io, g, xln_src, x_src, x_dst, xln_dst, tag):
    p.phase_boundary()
    nc = p.nc
    ar = p.ar
    ps = g["ps"]
    xln_ap, x_ap, xo_ap = io[xln_src], io[x_src], io[x_dst]
    OC = 43
    xltb = ar.alloc(f"{tag}_xlt", [128, 16 * 1024], BF16)
    hgh = ar.alloc(f"{tag}_h", [128, OC * 1024], BF16)
    gwc = [ar.alloc(f"{tag}_gw{i}", [128, 16 * 128], BF16) for i in range(2)]
    uwc = [ar.alloc(f"{tag}_uw{i}", [128, 16 * 128], BF16) for i in range(2)]
    dwm = [ar.alloc(f"{tag}_dw0", [128, OC * 256], BF16)]
    slb = [ar.alloc(f"{tag}_sl{i}", [128, 512], BF16) for i in range(2)]
    statsE = ar.alloc(f"{tag}_stE", [128, 16 * 48], F32)
    xo_sb = [ar.alloc(f"{tag}_xo{i}", [128, 256], F32) for i in range(3)]
    xn_sb = [ar.alloc(f"{tag}_xn{i}", [128, 256], F32) for i in range(2)]
    p.new_block()
    blk = p.blk
    b_gmm, b_umm, b_sl, b_bu, b_emm, b_ef, b_xn, b_xtl = p.base(
        "gmm", "umm", "sl", "bu", "emm", "ef", "xn", "xtl")
    WL = p.plan("wl", 2, 2 * OC, unit=32)
    DL = p.plan("dl", 1, 16)
    EL = p.plan("el", 3, 128, q="s")
    ES = p.plan("es", 2, 128)

    @blk.scalar
    def _(se):
        se = VD(se)
        for tb in range(2):
            if tb == 1:
                se.wait_ge(p.sem("gmm"), b_gmm + 2 * OC)
                se.wait_ge(p.sem("umm"), b_umm + 2 * OC)
            for hc in range(16):
                p.inc(se.dma_start(out=xltb[:, hc * 1024:(hc + 1) * 1024],
                                   in_=xln_ap[tb * 1024:(tb + 1) * 1024, hc * 128:(hc + 1) * 128],
                                   transpose=True), "xtl", 16)
            for oc in range(OC):
                i = tb * OC + oc
                for half in range(2):
                    jj = i * 2 + half
                    se.wait_ge(p.sem("gmm"), b_gmm + jj + 1)
                    if jj >= 2:
                        se.wait_ge(p.sem("bu"), b_bu + jj - 1)
                    p.inc(se.activation(slb[jj % 2][:], ps[half][:, 0:512], AF.Silu), "sl", 1)

    @blk.gpsimd
    def _(g_):
        g_.wait_ge(p.sem("cc"), p.cnt_of("cc"))
        for tb in range(2):
            for oc in range(OC):
                i = tb * OC + oc
                if i >= 2:
                    g_.wait_ge(p.sem("bu"), b_bu + 2 * (i - 1))
                p.inc(g_.dma_start(out=gwc[i % 2][:, :].rearrange("p (c f) -> p c f", c=16),
                                   in_=io["gwT_d"][:, oc * 128:(oc + 1) * 128].rearrange(
                                       "(c p) f -> p c f", p=128)), WL[i][0], 16)
                tgt = p.inc(g_.dma_start(out=uwc[i % 2][:, :].rearrange("p (c f) -> p c f", c=16),
                                         in_=io["uwT_d"][:, oc * 128:(oc + 1) * 128].rearrange(
                                             "(c p) f -> p c f", p=128)), WL[i][0], 16)
                assert tgt == WL[i][1]
            for m in range(8):
                dmi = tb * 8 + m
                if dmi >= 1:
                    g_.wait_ge(p.sem("emm"), b_emm + dmi * 8)
                tgt = p.inc(g_.dma_start(out=dwm[0][:, :].rearrange("p (c f) -> p c f", c=OC),
                                         in_=io["dwT_d"][:, m * 256:(m + 1) * 256].rearrange(
                                             "(c p) f -> p c f", p=128)), DL[dmi][0], 16)
                assert tgt == DL[dmi][1]
                for t2 in range(8):
                    e = tb * 64 + m * 8 + t2
                    g_.wait_ge(p.sem("xn"), b_xn + e + 1)
                    tgt = p.inc(g_.dma_start(
                        out=xo_ap[tb * 1024 + t2 * 128:tb * 1024 + (t2 + 1) * 128,
                                  m * 256:(m + 1) * 256],
                        in_=xn_sb[e % 2][:]), ES[e][0], 16)
                    assert tgt == ES[e][1]
        for i in range(2):
            g_.wait_ge(p.sem(ES[127 - i][0]), ES[127 - i][1])

    @blk.sync
    def _(s):
        for tb in range(2):
            for m in range(8):
                for t2 in range(8):
                    e = tb * 64 + m * 8 + t2
                    if e >= 3:
                        s.wait_ge(p.sem("xn"), b_xn + e - 2)
                    p.inc(s.dma_start(out=xo_sb[e % 3][:],
                                      in_=x_ap[tb * 1024 + t2 * 128:tb * 1024 + (t2 + 1) * 128,
                                               m * 256:(m + 1) * 256]), EL[e][0], 16)

    @blk.tensor
    def _(t):
        for tb in range(2):
            t.wait_ge(p.sem("xtl"), b_xtl + (tb + 1) * 256)
            for oc in range(OC):
                i = tb * OC + oc
                t.wait_ge(p.sem(WL[tb * OC + oc][0]), WL[tb * OC + oc][1])
                for half in range(2):
                    jj = i * 2 + half
                    if jj >= 2:
                        t.wait_ge(p.sem("sl"), b_sl + jj - 1)
                        t.wait_ge(p.sem("bu"), b_bu + jj - 1)
                    for hc in range(16):
                        mm = t.matmul(ps[half][:, 0:512], gwc[i % 2][:, hc * 128:(hc + 1) * 128],
                                      xltb[:, hc * 1024 + half * 512:hc * 1024 + (half + 1) * 512],
                                      start=(hc == 0), stop=(hc == 15))
                    p.inc(mm, "gmm", 1)
                    for hc in range(16):
                        mm = t.matmul(ps[2 + half][:, 0:512], uwc[i % 2][:, hc * 128:(hc + 1) * 128],
                                      xltb[:, hc * 1024 + half * 512:hc * 1024 + (half + 1) * 512],
                                      start=(hc == 0), stop=(hc == 15))
                    p.inc(mm, "umm", 1)
            t.wait_ge(p.sem("bu"), b_bu + (tb + 1) * 2 * OC)
            for m in range(8):
                dmi = tb * 8 + m
                t.wait_ge(p.sem(DL[dmi][0]), DL[dmi][1])
                for t2 in range(8):
                    e = tb * 64 + m * 8 + t2
                    if e >= 2:
                        t.wait_ge(p.sem("ef"), b_ef + e - 1)
                    bank = ps[4 + e % 2]
                    for oc in range(OC):
                        mm = t.matmul(bank[:, 0:256],
                                      hgh[:, oc * 1024 + t2 * 128:oc * 1024 + (t2 + 1) * 128],
                                      dwm[0][:, oc * 256:(oc + 1) * 256],
                                      start=(oc == 0), stop=(oc == OC - 1))
                    p.inc(mm, "emm", 1)

    @blk.vector
    def _(v):
        v = VD(v)
        for tb in range(2):
            for oc in range(OC):
                i = tb * OC + oc
                for half in range(2):
                    jj = i * 2 + half
                    v.wait_ge(p.sem("sl"), b_sl + jj + 1)
                    v.wait_ge(p.sem("umm"), b_umm + jj + 1)
                    if tb == 1:
                        v.wait_ge(p.sem("emm"), b_emm + 32)
                    p.inc(v.tensor_tensor(hgh[:, oc * 1024 + half * 512:oc * 1024 + (half + 1) * 512],
                                          slb[jj % 2][:], ps[2 + half][:, 0:512], ALU.mult), "bu", 1)
            for m in range(8):
                for t2 in range(8):
                    e = tb * 64 + m * 8 + t2
                    v.wait_ge(p.sem("emm"), b_emm + e + 1)
                    v.wait_ge(p.sem(EL[e][0]), EL[e][1])
                    if e >= 2:
                        v.wait_ge(p.sem(ES[e - 2][0]), ES[e - 2][1])
                    p.inc(v.scalar_tensor_tensor(xn_sb[e % 2][:], ps[4 + e % 2][:, 0:256], 0.5,
                                                 xo_sb[e % 3][:], op0=ALU.mult, op1=ALU.add), "ef", 1)
                    tg = tb * 8 + t2
                    p.inc(v.bn_stats(statsE[:, tg * 48 + m * 6:tg * 48 + (m + 1) * 6],
                                     xn_sb[e % 2][:]), "xn", 1)

    ar.free(xltb, hgh, *gwc, *uwc, *dwm, *slb, *xo_sb, *xn_sb)
    if xln_dst is not None:
        emit_lnpass(p, io, g, x_dst, xln_dst, statsE, tag, tps=48)
    ar.free(statsE)


def ph_l1qkv(p, io, g, xln_src, tag, do_ag=True):
    p.phase_boundary()
    nc = p.nc
    ar = p.ar
    ps = g["ps"]
    xln_ap = io[xln_src]
    xlt = ar.alloc(f"{tag}_xlt", [128, 16 * TLOC], BF16)
    wt = ar.alloc(f"{tag}_wt", [128, 16 * TLOC], BF16)
    qpc = [ar.alloc(f"{tag}_qpc{i}", [128, TLOC], BF16) for i in range(2)]
    brow = ar.alloc(f"{tag}_brow", [1, 3 * D], BF16)
    t1 = ar.alloc(f"{tag}_t1", [128, 512], F32)
    t2 = ar.alloc(f"{tag}_t2", [128, 512], F32)
    kpc = ar.alloc(f"{tag}_kpc", [128, TLOC], BF16)
    vc = ar.alloc(f"{tag}_vc", [128, TLOC], BF16)
    kvsb = ar.alloc(f"{tag}_kvsb", [128, 16 * 132], F32)
    ksr = ar.alloc(f"{tag}_ksr", [1, D], F32)
    g["l1_kvsb"] = kvsb

    p.new_block()
    blk = p.blk
    b = {n: p.cnt_of(n) for n in ["qmm", "qev", "qex", "qp2", "qpw", "bld", "wtl", "xlt"]}
    QS = p.plan("qps", 2, 16, unit=16)

    @blk.gpsimd
    def _(g_):
        p.inc(g_.dma_start(out=brow[0:1, 2 * D:3 * D], in_=io["L1_attn_q_b"][None, :]), "bld", 16)
        p.inc(g_.dma_start(out=brow[0:1, 0:D], in_=io["L1_attn_k_b"][None, :]), "bld", 16)
        p.inc(g_.dma_start(out=brow[0:1, D:2 * D], in_=io["L1_attn_v_b"][None, :]), "bld", 16)

    @blk.gpsimd
    def _(g_):
        g_.memset(kvsb[:], 0.0)
        g_.drain()
        p.inc(g_.nop(), "kvz", 1)

    @blk.scalar
    def _(se):
        se = VD(se)
        emit_xlnT_load(p, se, xln_ap, xlt, "xlt")
        for dc in range(16):
            p.inc(se.dma_start(out=wt[:, dc * TLOC:(dc + 1) * TLOC],
                               in_=io["qw1_bf"][:, dc * 128:(dc + 1) * 128],
                               transpose=True), "wtl", 16)

    @blk.tensor
    def _(t):
        t.wait_ge(p.sem("xlt"), b["xlt"] + 256)
        t.wait_ge(p.sem("wtl"), b["wtl"] + 256)
        p.wait(t, "bld")
        for m in range(16):
            for tb in range(4):
                e = m * 4 + tb
                if e >= 2:
                    t.wait_ge(p.sem("qp2"), b["qp2"] + e - 1)
                for dc in range(16):
                    t.matmul(ps[e % 2][:, 0:512], wt[:, dc * TLOC + m * 128:dc * TLOC + (m + 1) * 128],
                             xlt[:, dc * TLOC + tb * 512:dc * TLOC + (tb + 1) * 512],
                             start=(dc == 0), stop=False)
                mm = t.matmul(ps[e % 2][:, 0:512], brow[0:1, 2 * D + m * 128:2 * D + (m + 1) * 128],
                              g["onesb_512"][0:1, 0:512], start=False, stop=True)
                p.inc(mm, "qmm", 1)

    @blk.vector
    def _(v):
        v = VD(v)
        for m in range(16):
            for tb in range(4):
                e = m * 4 + tb
                v.wait_ge(p.sem("qmm"), b["qmm"] + e + 1)
                p.inc(v.tensor_scalar(t1[:], ps[e % 2][:, 0:512], 0.0, None, ALU.min), "qev", 1)
                v.wait_ge(p.sem("qex"), b["qex"] + e + 1)
                p.inc(v.tensor_scalar(t1[:], ps[e % 2][:, 0:512], 0.0, None, ALU.max), "qp2", 1)
                if tb == 0 and m >= 2:
                    v.wait_ge(p.sem(QS[m - 2][0]), QS[m - 2][1])
                qw = v.tensor_tensor(qpc[m % 2][:, tb * 512:(tb + 1) * 512],
                                     t1[:], t2[:], ALU.add)
            p.inc(qw, "qpw", 1)

    @blk.scalar
    def _(se):
        se = VD(se)
        for e in range(64):
            se.wait_ge(p.sem("qev"), b["qev"] + e + 1)
            p.inc(se.activation(t2[:], t1[:], AF.Exp), "qex", 1)

    @blk.gpsimd
    def _(g_):
        qp_r = io["qp_d"].rearrange("h d t -> (h d) t")
        for m in range(16):
            g_.wait_ge(p.sem("qpw"), b["qpw"] + m + 1)
            tgt = p.inc(g_.dma_start(out=qp_r[m * 128:(m + 1) * 128, :], in_=qpc[m % 2][:]), QS[m][0], 16)
            assert tgt == QS[m][1]
        for i in range(2):
            g_.wait_ge(p.sem(QS[15 - i][0]), QS[15 - i][1])

    p.new_block()
    blk = p.blk
    b = {n: p.cnt_of(n) for n in ["kmm", "kev", "kex", "kp2", "kpw", "kst", "wtl"]}
    KS = p.plan("kst2", 1, 16)

    @blk.scalar
    def _(se):
        se = VD(se)
        for dc in range(16):
            p.inc(se.dma_start(out=wt[:, dc * TLOC:(dc + 1) * TLOC],
                               in_=io["kw1_bf"][:, dc * 128:(dc + 1) * 128],
                               transpose=True), "wtl", 16)
        for e in range(64):
            se.wait_ge(p.sem("kev"), b["kev"] + e + 1)
            p.inc(se.activation(t2[:], t1[:], AF.Exp), "kex", 1)

    @blk.tensor
    def _(t):
        t.wait_ge(p.sem("wtl"), b["wtl"] + 256)
        for tt in range(16):
            for mb in range(4):
                e = tt * 4 + mb
                if e >= 2:
                    t.wait_ge(p.sem("kp2"), b["kp2"] + e - 1)
                for dc in range(16):
                    t.matmul(ps[e % 2][:, 0:512], xlt[:, dc * TLOC + tt * 128:dc * TLOC + (tt + 1) * 128],
                             wt[:, dc * TLOC + mb * 512:dc * TLOC + (mb + 1) * 512],
                             start=(dc == 0), stop=False)
                mm = t.matmul(ps[e % 2][:, 0:512], g["ones_r"][0:1, 0:128],
                              brow[0:1, mb * 512:(mb + 1) * 512], start=False, stop=True)
                p.inc(mm, "kmm", 1)

    @blk.vector
    def _(v):
        v = VD(v)
        for tt in range(16):
            if tt >= 1:
                v.wait_ge(p.sem(KS[tt - 1][0]), KS[tt - 1][1])
            for mb in range(4):
                e = tt * 4 + mb
                v.wait_ge(p.sem("kmm"), b["kmm"] + e + 1)
                p.inc(v.tensor_scalar(t1[:], ps[e % 2][:, 0:512], 0.0, None, ALU.min), "kev", 1)
                v.wait_ge(p.sem("kex"), b["kex"] + e + 1)
                p.inc(v.tensor_scalar(t1[:], ps[e % 2][:, 0:512], 0.0, None, ALU.max), "kp2", 1)
                kw = v.tensor_tensor(kpc[:, mb * 512:(mb + 1) * 512], t1[:], t2[:], ALU.add)
            p.inc(kw, "kpw", 1)

    @blk.gpsimd
    def _(g_):
        for tt in range(16):
            g_.wait_ge(p.sem("kpw"), b["kpw"] + tt + 1)
            tgt = p.inc(g_.dma_start(out=io["kp_d"][tt * 128:(tt + 1) * 128, :], in_=kpc[:]), KS[tt][0], 16)
            assert tgt == KS[tt][1]
        g_.wait_ge(p.sem(KS[15][0]), KS[15][1])

    p.new_block()
    blk = p.blk
    b = {n: p.cnt_of(n) for n in ["vmm", "vev", "kvm", "kva", "wtl"]}
    VL = p.plan("vld", 1, 16, q="s")

    @blk.scalar
    def _(se):
        se = VD(se)
        for dc in range(16):
            p.inc(se.dma_start(out=wt[:, dc * TLOC:(dc + 1) * TLOC],
                               in_=io["vw1_bf"][:, dc * 128:(dc + 1) * 128],
                               transpose=True), "wtl", 16)

    @blk.sync
    def _(s):
        for tt in range(16):
            if tt >= 1:
                s.wait_ge(p.sem("kvm"), b["kvm"] + tt * 16)
            tgt = p.inc(s.dma_start(out=kpc[:], in_=io["kp_d"][tt * 128:(tt + 1) * 128, :]), VL[tt][0], 16)
            assert tgt == VL[tt][1]

    @blk.tensor
    def _(t):
        t.wait_ge(p.sem("wtl"), b["wtl"] + 256)
        for tt in range(16):
            for mb in range(4):
                e = tt * 4 + mb
                if e >= 1:
                    t.wait_ge(p.sem("vev"), b["vev"] + e)
                for dc in range(16):
                    t.matmul(ps[e % 2][:, 0:512], xlt[:, dc * TLOC + tt * 128:dc * TLOC + (tt + 1) * 128],
                             wt[:, dc * TLOC + mb * 512:dc * TLOC + (mb + 1) * 512],
                             start=(dc == 0), stop=False)
                mm = t.matmul(ps[e % 2][:, 0:512], g["ones_r"][0:1, 0:128],
                              brow[0:1, D + mb * 512:D + (mb + 1) * 512], start=False, stop=True)
                p.inc(mm, "vmm", 1)
            t.wait_ge(p.sem(VL[tt][0]), VL[tt][1])
            t.wait_ge(p.sem("vev"), b["vev"] + (tt + 1) * 4)
            for h in range(NH):
                hh = tt * 16 + h
                if hh >= 2:
                    t.wait_ge(p.sem("kva"), b["kva"] + hh - 1)
                p.inc(t.matmul(ps[4 + h % 2][:, 0:128],
                               kpc[:, h * 128:(h + 1) * 128], vc[:, h * 128:(h + 1) * 128],
                               start=True, stop=True), "kvm", 1)

    @blk.vector
    def _(v):
        v = VD(v)
        p.wait(v, "kvz")
        for tt in range(16):
            for mb in range(4):
                e = tt * 4 + mb
                v.wait_ge(p.sem("vmm"), b["vmm"] + e + 1)
                if tt >= 1:
                    v.wait_ge(p.sem("kvm"), b["kvm"] + tt * 16)
                p.inc(v.tensor_copy(vc[:, mb * 512:(mb + 1) * 512], ps[e % 2][:, 0:512]), "vev", 1)
            for h in range(NH):
                hh = tt * 16 + h
                v.wait_ge(p.sem("kvm"), b["kvm"] + hh + 1)
                p.inc(v.tensor_tensor(kvsb[:, h * 132:h * 132 + 128],
                                      kvsb[:, h * 132:h * 132 + 128],
                                      ps[4 + h % 2][:, 0:128], ALU.add), "kva", 1)

    p.new_block()
    blk = p.blk
    b = {n: p.cnt_of(n) for n in ["ksm", "kse", "kts", "ktc", "kps", "kva", "cc"]}
    KL = p.plan("ksl", 2, 16, q="s")

    @blk.sync
    def _(s):
        for tt in range(16):
            if tt >= 2:
                s.wait_ge(p.sem("ksm"), b["ksm"] + (tt - 1) * 4)
            buf = kpc if tt % 2 == 0 else vc
            tgt = p.inc(s.dma_start(out=buf[:], in_=io["kp_d"][tt * 128:(tt + 1) * 128, :]), KL[tt][0], 16)
            assert tgt == KL[tt][1]

    @blk.tensor
    def _(t):
        for tt in range(16):
            t.wait_ge(p.sem(KL[tt][0]), KL[tt][1])
            buf = kpc if tt % 2 == 0 else vc
            for nb in range(4):
                p.inc(t.matmul(ps[nb][0:1, 0:512], g["onesb_c"][:, 0:1],
                               buf[:, nb * 512:(nb + 1) * 512],
                               start=(tt == 0), stop=(tt == 15)), "ksm", 1)
        t.wait_ge(p.sem("kse"), b["kse"] + 4)
        for h in range(NH):
            if h >= 2:
                t.wait_ge(p.sem("ktc"), b["ktc"] + h - 1)
            p.inc(t.transpose(ps[4 + h % 2][0:128, 0:1], ksr[0:1, h * 128:(h + 1) * 128],
                              g["ident"][0:1, 0:1]), "kts", 1)

    @blk.scalar
    def _(se):
        se = VD(se)
        se.wait_ge(p.sem("ksm"), b["ksm"] + 64)
        for nb in range(4):
            p.inc(se.copy(ksr[0:1, nb * 512:(nb + 1) * 512], ps[nb][0:1, 0:512]), "kse", 1)
        for h in range(NH):
            se.wait_ge(p.sem("kts"), b["kts"] + h + 1)
            p.inc(se.copy(kvsb[:, h * 132 + 128:h * 132 + 129], ps[4 + h % 2][0:128, 0:1]), "ktc", 1)

    @blk.gpsimd
    def _(g_):
        p.wait(g_, "ktc")
        p.wait(g_, "kva")
        p.inc(g_.dma_start(out=io["kvp_d"].rearrange("(h d) c -> d h c", d=128),
                           in_=kvsb[:, :].rearrange("p (h c) -> p h c", h=NH)), "kps", 16)
        g_.wait_ge(p.sem("kps"), b["kps"] + 16)
        if do_ag:
            pairs = [[0, 1], [2, 3], [4, 5], [6, 7]]
            p.inc(g_.collective_compute("AllGather", ALU.bypass, replica_groups=pairs,
                                        ins=[io["kvp_d"]], outs=[io["kvp_ag"]]), "cc", 1)
            p.wait(g_, "cc")

    ar.free(xlt, wt, *qpc, brow, t1, t2, kpc, vc, ksr)


def ph_l1attn(p, io, g, tag):
    p.phase_boundary()
    nc = p.nc
    ar = p.ar
    ps = g["ps"]
    kvsb = g["l1_kvsb"]
    qph = [ar.alloc(f"{tag}_qph{i}", [128, TLOC], BF16) for i in range(2)]
    kva = ar.alloc(f"{tag}_kva", [128, 16 * 132], F32)
    kvb = ar.alloc(f"{tag}_kvb", [128, 16 * 128], BF16)
    ksb = ar.alloc(f"{tag}_ksb", [128, 16], BF16)
    zrow = ar.alloc(f"{tag}_zr", [1, 2 * 512], F32)
    rzb = ar.alloc(f"{tag}_rzb", [128, 2 * 512], F32)
    aob = [ar.alloc(f"{tag}_ao{i}", [128, TLOC], BF16) for i in range(2)]
    p.new_block()
    blk = p.blk
    b = {n: p.cnt_of(n) for n in ["kal", "kaa", "omm", "zmm", "zrc", "zbm", "zbe", "oev"]}
    AS = p.plan("aos", 2, NH, unit=16)
    QL = p.plan("qpl", 2, NH, q="s")
    ao_r = io["aout_d"].rearrange("(h d) t -> h d t", d=128)
    qp_r = io["qp_d"].rearrange("h d t -> (h d) t")

    @blk.sync
    def _(s):
        for r in range(2):
            p.inc(s.dma_start(out=kva[:, :].rearrange("p (h c) -> p h c", h=NH) if r == 0 else
                              kvsb[:, :].rearrange("p (h c) -> p h c", h=NH),
                              in_=io["kvp_ag"][r].rearrange("(h d) c -> d h c", d=128)), "kal", 16)
        for h in range(NH):
            if h >= 2:
                s.wait_ge(p.sem("oev"), b["oev"] + (h - 1) * 4)
            tgt = p.inc(s.dma_start(out=qph[h % 2][:], in_=qp_r[h * 128:(h + 1) * 128, :]), QL[h][0], 16)
            assert tgt == QL[h][1]

    @blk.vector
    def _(v):
        v = VD(v)
        v.wait_ge(p.sem("kal"), b["kal"] + 32)
        p.inc(v.tensor_tensor(kva[:], kva[:], kvsb[:], ALU.add), "kaa", 1)
        for h in range(NH):
            v.tensor_copy(kvb[:, h * 128:(h + 1) * 128], kva[:, h * 132:h * 132 + 128])
            kk = v.tensor_copy(ksb[:, h:h + 1], kva[:, h * 132 + 128:h * 132 + 129])
        p.inc(kk, "kaa", 1)

    @blk.tensor
    def _(t):
        p.wait(t, "kaa")
        for h in range(NH):
            t.wait_ge(p.sem(QL[h][0]), QL[h][1])
            for tb in range(4):
                e = h * 4 + tb
                if e >= 2:
                    t.wait_ge(p.sem("oev"), b["oev"] + e - 1)
                p.inc(t.matmul(ps[e % 2][:, 0:512], kvb[:, h * 128:(h + 1) * 128],
                               qph[h % 2][:, tb * 512:(tb + 1) * 512],
                               start=True, stop=True), "omm", 1)
                if e >= 2:
                    t.wait_ge(p.sem("zrc"), b["zrc"] + e - 1)
                p.inc(t.matmul(ps[2 + e % 2][0:1, 0:512], ksb[:, h:h + 1],
                               qph[h % 2][:, tb * 512:(tb + 1) * 512],
                               start=True, stop=True), "zmm", 1)
                t.wait_ge(p.sem("zrc"), b["zrc"] + e + 1)
                if e >= 2:
                    t.wait_ge(p.sem("zbe"), b["zbe"] + e - 1)
                    t.wait_ge(p.sem("oev"), b["oev"] + e - 1)
                p.inc(t.matmul(ps[4 + e % 2][:, 0:512], g["onesf_r"][0:1, :],
                               zrow[0:1, 512:1024], start=True, stop=True), "zbm", 1)

    @blk.vector
    def _(v):
        v = VD(v)
        for h in range(NH):
            for tb in range(4):
                e = h * 4 + tb
                v.wait_ge(p.sem("zmm"), b["zmm"] + e + 1)
                v.tensor_scalar(zrow[0:1, 0:512], ps[2 + e % 2][0:1, 0:512], 1e-6, None, ALU.add)
                p.inc(v.reciprocal(zrow[0:1, 512:1024], zrow[0:1, 0:512]), "zrc", 1)
                v.wait_ge(p.sem("zbe"), b["zbe"] + e + 1)
                v.wait_ge(p.sem("omm"), b["omm"] + e + 1)
                if tb == 0 and h >= 2:
                    v.wait_ge(p.sem(AS[h - 2][0]), AS[h - 2][1])
                p.inc(v.tensor_tensor(aob[h % 2][:, tb * 512:(tb + 1) * 512], ps[e % 2][:, 0:512],
                                      rzb[:, (e % 2) * 512:(e % 2 + 1) * 512], ALU.mult), "oev", 1)

    @blk.scalar
    def _(se):
        se = VD(se)
        for h in range(NH):
            for tb in range(4):
                e = h * 4 + tb
                se.wait_ge(p.sem("zbm"), b["zbm"] + e + 1)
                p.inc(se.copy(rzb[:, (e % 2) * 512:(e % 2 + 1) * 512], ps[4 + e % 2][:, 0:512]), "zbe", 1)

    @blk.gpsimd
    def _(g_):
        for h in range(NH):
            g_.wait_ge(p.sem("oev"), b["oev"] + (h + 1) * 4)
            tgt = p.inc(g_.dma_start(out=ao_r[h], in_=aob[h % 2][:]), AS[h][0], 16)
            assert tgt == AS[h][1]
        for i in range(2):
            g_.wait_ge(p.sem(AS[NH - 1 - i][0]), AS[NH - 1 - i][1])

    ar.free(kva, kvb, ksb, zrow, rzb, *aob, *qph, kvsb)


MODE = "fused"

CAST_L0 = [("L0_attn_Wq", "wqt_bf"), ("L0_attn_Wk", "wkt_bf"),
           ("L0_attn_Wv", "wvt_bf"), ("L0_attn_Wo", "wot_bf")]
CAST_L1 = [("L1_attn_q_w", "qw1_bf"), ("L1_attn_k_w", "kw1_bf"),
           ("L1_attn_v_w", "vw1_bf"), ("L1_attn_o_w", "ow1_bf")]

L0FFN_W = ["L0_ffn_w_router", "L0_ffn_w_gate", "L0_ffn_w_up", "L0_ffn_w_down",
           "L0_ffn_amplitudes", "L0_ffn_expert_scale",
           "L0_ffn_shared_w_gate", "L0_ffn_shared_w_up", "L0_ffn_shared_w_down"]

def _launches(fused=False):
    if fused:
        l3 = ("L3", lambda p, io, g: (ph_ghost(p, io, g, "gh"),
                                      ph_l0ffn(p, io, g, "xln2", "x2", "x3", "xln3", "f2")),
              ["xln2", "x2"] + L0FFN_W, ["x3", "xln3"])
        l4 = ("L4", lambda p, io, g: (ph_l1ffn(p, io, g, "xln3", "x3", "x4", "xln4", "g1"),),
              ["xln3", "x3"], ["x4", "xln4"])
    else:
        l3 = ("L3", lambda p, io, g: (ph_l0ffn(p, io, g, "xln2", "x2", "x3", "xln3", "f2"),),
              ["xln2", "x2"] + L0FFN_W, ["x3", "xln3"])
        l4 = ("L4", lambda p, io, g: (ph_ghost(p, io, g, "gh", wait_cc=True),
                                      ph_l1ffn(p, io, g, "xln3", "x3", "x4", "xln4", "g1")),
              ["xln3", "x3", "gate_idx_sl", "up_idx_sl", "down_idx_sl", "L1_ffn_lut"],
              ["x4", "xln4"])
    return [
        ("L1", lambda p, io, g: (ph_ln0(p, io, g, "x0", "xln0"),
                                 ph_l0ffn(p, io, g, "xln0", "x0", "x1", "xln1", "f1")),
         ["x0"] + L0FFN_W, ["x1", "xln1"]),
        ("L2", lambda p, io, g: (ph_castw(p, io, CAST_L0),
                                 ph_l0qkv(p, io, g, "xln1", "qk"),
                                 ph_l0attn(p, io, g, "at"),
                                 ph_l0proj(p, io, g, "x1", "x2", "xln2", "pj", 1.0)),
         ["xln1", "x1", "maskr", "L0_attn_Wq", "L0_attn_Wk", "L0_attn_Wv", "L0_attn_Wo",
          "L0_attn_gq", "L0_attn_gk"], ["x2", "xln2"]),
        l3,
        l4,
        ("L5", lambda p, io, g: (ph_castw(p, io, CAST_L1),
                                 ph_l1qkv(p, io, g, "xln4", "lq"),
                                 ph_l1attn(p, io, g, "la"),
                                 ph_l0proj(p, io, g, "x4", "x5", "xln5", "lp", 1.0,
                                           wname="ow1_bf", bias="L1_attn_o_b")),
         ["xln4", "x4", "L1_attn_q_w", "L1_attn_q_b", "L1_attn_k_w", "L1_attn_k_b",
          "L1_attn_v_w", "L1_attn_v_b", "L1_attn_o_w", "L1_attn_o_b"], ["x5", "xln5"]),
        ("L6", lambda p, io, g: (ph_l1ffn(p, io, g, "xln5", "x5", "y_out", None, "g2"),),
         ["xln5", "x5", "gwT_d", "uwT_d", "dwT_d"], ["y_out"]),
    ]


def _all_dram_names():
    reg = dict(DRAM_REG)
    reg.update(WEIGHT_REG)
    return reg


def build_launch(phases_fn, ins, outs):
    nc = bass.Bass("TRN2", target_bir_lowering=False)
    reg = _all_dram_names()
    kinds = {}
    for n in ins:
        kinds[n] = "ExternalInput"
    for n in outs:
        kinds[n] = "ExternalOutput"
    io = declare_dram(nc, list(reg.keys()), kinds)
    p = Prog(nc)
    g = {}
    ph_init(p, g)
    phases_fn(p, io, g)
    p.end()
    return nc


def host_inputs_per_core(inputs):
    shards = shard_x(inputs["L0_attn_x"])
    mks = maskr_host()
    per_core = []
    for c in range(8):
        m = {"x0": shards[c], "maskr": mks[c],
             "gate_idx_sl": np.ascontiguousarray(inputs["L1_ffn_gate_idx"][:, c * 256:(c + 1) * 256]),
             "up_idx_sl": np.ascontiguousarray(inputs["L1_ffn_up_idx"][:, c * 256:(c + 1) * 256]),
             "down_idx_sl": np.pad(inputs["L1_ffn_down_idx"][:, c * 688:(c + 1) * 688],
                                   ((0, 0), (0, 80)))}
        for k, v in inputs.items():
            if k in WEIGHT_REG:
                m[k] = np.ascontiguousarray(v)
        per_core.append(m)
    return per_core


def kernel(**inputs):
    inputs = {k: np.asarray(v) for k, v in inputs.items()}
    per_core = host_inputs_per_core(inputs)
    if MODE == "multi":
        state = [dict(pc) for pc in per_core]
        for name, fn, ins, outs in _launches(fused=False):
            nc = build_launch(fn, ins, outs)
            in_maps = [{n: state[c][n] for n in ins} for c in range(8)]
            res = run_bass_kernel_spmd(nc, in_maps, core_ids=list(range(8)))
            for c in range(8):
                for n in outs:
                    state[c][n] = res.results[c][n]
        ys = [state[c]["y_out"] for c in range(8)]
        return unshard_y(ys)
    else:
        nc = build_fused()
        in_names = FUSED_INS
        in_maps = [{n: per_core[c][n] for n in in_names} for c in range(8)]
        res = run_bass_kernel_spmd(nc, in_maps, core_ids=list(range(8)))
        return unshard_y([res.results[c]["y_out"] for c in range(8)])


FUSED_INS = (["x0", "maskr", "gate_idx_sl", "up_idx_sl", "down_idx_sl"] +
             list(WEIGHT_REG.keys()))
FUSED_INS = [n for n in FUSED_INS if n not in ("gate_idx_sl", "up_idx_sl", "down_idx_sl")] + \
            ["gate_idx_sl", "up_idx_sl", "down_idx_sl"]
FUSED_INS = list(dict.fromkeys(FUSED_INS))


def build_fused():
    nc = bass.Bass("TRN2", target_bir_lowering=False)
    reg = _all_dram_names()
    kinds = {n: "ExternalInput" for n in FUSED_INS}
    kinds["y_out"] = "ExternalOutput"
    io = declare_dram(nc, list(reg.keys()), kinds)
    p = Prog(nc)
    g = {}
    ph_init(p, g)
    for name, fn, ins, outs in _launches(fused=True):
        fn(p, io, g)
    p.end()
    return nc


# version 16
# speedup vs baseline: 1.0649x; 1.0649x over previous
import contextlib
import os
import numpy as np
import ml_dtypes
import concourse.bass as bass
import concourse.mybir as mybir
from concourse.bass_utils import run_bass_kernel_spmd

F32 = mybir.dt.float32
BF16 = mybir.dt.bfloat16
I32 = mybir.dt.int32
AF = mybir.ActivationFunctionType
ALU = mybir.AluOpType
AX = mybir.AxisListType

D = 2048
TLOC = 2048
NCHUNK = 8
CH = 256
NH = 16
HD = 128
A_CHUNKS = [0, 3, 4, 7, 8, 11, 12, 15]
B_CHUNKS = [1, 2, 5, 6, 9, 10, 13, 14]
EXT = [4, 8, 12, 16, 20, 24, 28, 32]
NEG = -1e30

ABS2LOC = {}
for i, c in enumerate(A_CHUNKS):
    ABS2LOC[c] = (0, i)
for i, c in enumerate(B_CHUNKS):
    ABS2LOC[c] = (1, i)


def shard_x(x):
    shards = []
    for c in range(8):
        b = c // 2
        chunks = A_CHUNKS if c % 2 == 0 else B_CHUNKS
        xb = np.asarray(x[b]).reshape(16, CH, D)
        shards.append(np.ascontiguousarray(xb[chunks].reshape(TLOC, D)))
    return shards


def unshard_y(ys):
    out = np.empty((4, 4096, D), dtype=ys[0].dtype)
    for c in range(8):
        b = c // 2
        chunks = A_CHUNKS if c % 2 == 0 else B_CHUNKS
        out[b].reshape(16, CH, D)[chunks] = ys[c].reshape(NCHUNK, CH, D)
    return out


def maskr_host():
    out = []
    for c in range(8):
        chunks = A_CHUNKS if c % 2 == 0 else B_CHUNKS
        r = np.zeros((128, 32), dtype=np.float32)
        for s in range(NCHUNK):
            q0 = chunks[s] * CH
            for j in range(4):
                kb = EXT[s] - 4 + j
                r[:, s * 4 + j] = kb * 128 - q0
        out.append(r)
    return out


class Arena:
    BASE = 20 * 1024
    END = 212800

    def __init__(self, nc):
        self.nc = nc
        self.free_list = [(self.BASE, self.END)]
        self.live = {}
        self.n = 0

    def alloc(self, name, shape, dtype):
        per = int(np.prod(shape[1:])) * mybir.dt.size(dtype)
        per = (per + 63) & ~63
        for i, (a, b) in enumerate(self.free_list):
            if b - a >= per:
                self.free_list[i] = (a + per, b)
                if self.free_list[i][0] == self.free_list[i][1]:
                    self.free_list.pop(i)
                self.n += 1
                h = self.nc.alloc_sbuf_tensor_at(f"{name}_{self.n}", shape, dtype, offset=a)
                self.live[id(h)] = (a, per)
                return h
        raise MemoryError(f"SBUF arena full for {name} ({per}B): {self.free_list}")

    def free(self, *handles):
        for h in handles:
            a, per = self.live.pop(id(h))
            self.free_list.append((a, a + per))
        self.free_list.sort()
        out = []
        for a, b in self.free_list:
            if out and out[-1][1] == a:
                out[-1] = (out[-1][0], b)
            else:
                out.append((a, b))
        self.free_list = [tuple(x) for x in out]


class Prog:

    def __init__(self, nc):
        self.nc = nc
        self.cnt = {}
        self.hdl = {}
        self.pool = {}
        self.nsem = 0
        self.ar = Arena(nc)
        self.blk_ctx = None
        self.blk = None

    DMA_G = {"amk", "bld", "cw", "dbs", "gd", "gcd", "kps", "pbl"}
    DMA_S = {"aol", "ggl", "gld", "kal", "wtl", "xlt", "xtl"}

    def kind_of(self, name):
        if name == "cc":
            return "cc"
        if name in self.DMA_G or name.startswith("Dg_"):
            return "dmag"
        if name in self.DMA_S or name.startswith("Ds_"):
            return "dmas"
        return "eng"

    def sem(self, name):
        if name not in self.hdl:
            kind = self.kind_of(name)
            pool = self.pool.setdefault(kind, [])
            if pool:
                h, c = pool.pop()
                self.hdl[name] = h
                self.cnt[name] = c
            else:
                self.hdl[name] = self.nc.alloc_semaphore(f"s{self.nsem}")
                self.nsem += 1
                self.cnt[name] = 0
        return self.hdl[name]

    def phase_boundary(self, keep=()):
        for name in list(self.hdl.keys()):
            if name in keep:
                continue
            self.pool.setdefault(self.kind_of(name), []).append(
                (self.hdl.pop(name), self.cnt.pop(name)))

    def inc(self, instr, name, n=1):
        instr.then_inc(self.sem(name), n)
        self.cnt[name] += n
        return self.cnt[name]

    def cnt_of(self, name):
        self.sem(name)
        return self.cnt[name]

    def base(self, *names):
        return [self.cnt_of(n) for n in names]

    def plan(self, stream, nslots, n, unit=16, q="g"):
        stream = f"D{q}_{stream}"
        for i in range(nslots):
            self.sem(f"{stream}{i}")
        cnt = {i: self.cnt[f"{stream}{i}"] for i in range(nslots)}
        out = []
        for i in range(n):
            sl = i % nslots
            cnt[sl] += unit
            out.append((f"{stream}{sl}", cnt[sl]))
        return out

    def wait(self, eng, name, val=None):
        eng.wait_ge(self.sem(name), self.cnt[name] if val is None else val)

    def new_block(self):
        if self.blk_ctx is not None:
            self.blk_ctx.__exit__(None, None, None)
        self.blk_ctx = self.nc.Block()
        self.blk = self.blk_ctx.__enter__()

    def end(self):
        if self.blk_ctx is not None:
            self.blk_ctx.__exit__(None, None, None)
            self.blk_ctx = None


class VD:
    OPS = {"tensor_scalar", "tensor_tensor", "tensor_copy", "tensor_reduce",
           "bn_stats", "bn_aggr", "reciprocal", "scalar_tensor_tensor",
           "select", "memset", "activation", "copy"}
    ENABLED = bool(os.environ.get("BASS_SIM_FP64") or os.environ.get("KSIM_DRAINS"))

    def __init__(self, v):
        object.__setattr__(self, "_v", v)

    def __getattr__(self, n):
        f = getattr(self._v, n)
        if n in VD.OPS:
            def wrap(*a, **k):
                r = f(*a, **k)
                self._v.drain()
                return r
            return wrap
        return f


DRAM_REG = {
    "d_wkb": ([128, 8 * TLOC], BF16),
    "d_ctT": ([128, TLOC], F32),
    "d_wT": ([8, TLOC], F32),
    "d_hbf": ([128, 8 * TLOC], BF16),
    "d_hs": ([128, 8 * TLOC], BF16),
    "d_hs2": ([128, 8 * TLOC], BF16),
    "x_in": ([TLOC, D], F32),
    "y_out": ([TLOC, D], F32),
    "maskr": ([128, 32], F32),
    "qn_d": ([NH, 128, TLOC], BF16),
    "kn_d": ([NH * 128, TLOC], BF16),
    "v_d": ([NH * TLOC, 128], BF16),
    "kn_ag": ([2 * NH * 128, TLOC], BF16),
    "aout_d": ([NH * 128, TLOC], BF16),
    "v_ag": ([2 * NH * TLOC, 128], BF16),
    "kp_d": ([TLOC, D], BF16),
    "qp_d": ([NH, 128, TLOC], BF16),
    "kvw_d": ([TLOC, 2 * D], BF16),
    "kvp_d": ([D, 132], F32),
    "kvp_ag": ([2, D, 132], F32),
    "gwT_d": ([D, 5504], BF16),
    "uwT_d": ([D, 5504], BF16),
    "dwT_d": ([5504, D], BF16),
    "gwT_sl": ([256, 5504], BF16),
    "uwT_sl": ([256, 5504], BF16),
    "dwT_sl": ([688, D], BF16),
    "gidxb_d": ([5504, 256], BF16),
    "uidxb_d": ([5504, 256], BF16),
    "didxb_d": ([2048, 768], BF16),
    "wqt_bf": ([D, D], BF16), "wkt_bf": ([D, D], BF16),
    "wvt_bf": ([D, D], BF16), "wot_bf": ([D, D], BF16),
    "qw1_bf": ([D, D], BF16), "kw1_bf": ([D, D], BF16),
    "vw1_bf": ([D, D], BF16), "ow1_bf": ([D, D], BF16),
}
for i in range(0, 7):
    DRAM_REG[f"x{i}"] = ([TLOC, D], F32)
    DRAM_REG[f"xln{i}"] = ([TLOC, D], BF16)

WEIGHT_REG = {
    "L0_attn_Wq": ([D, D], F32), "L0_attn_Wk": ([D, D], F32),
    "L0_attn_Wv": ([D, D], F32), "L0_attn_Wo": ([D, D], F32),
    "L0_attn_gq": ([HD], F32), "L0_attn_gk": ([HD], F32),
    "L0_ffn_w_router": ([D, 128], F32),
    "L0_ffn_w_gate": ([D, 8192], F32), "L0_ffn_w_up": ([D, 8192], F32),
    "L0_ffn_w_down": ([1024, D], F32),
    "L0_ffn_amplitudes": ([128, 8, 2], F32), "L0_ffn_expert_scale": ([128], F32),
    "L0_ffn_shared_w_gate": ([D, D], F32), "L0_ffn_shared_w_up": ([D, D], F32),
    "L0_ffn_shared_w_down": ([D, D], F32),
    "L1_attn_q_w": ([D, D], F32), "L1_attn_q_b": ([D], F32),
    "L1_attn_k_w": ([D, D], F32), "L1_attn_k_b": ([D], F32),
    "L1_attn_v_w": ([D, D], F32), "L1_attn_v_b": ([D], F32),
    "L1_attn_o_w": ([D, D], F32), "L1_attn_o_b": ([D], F32),
    "L1_ffn_lut": ([16], F32),
    "gate_idx_sl": ([5504, 256], I32), "up_idx_sl": ([5504, 256], I32),
    "down_idx_sl": ([2048, 768], I32),
}


def declare_dram(nc, names, kind_map):
    aps = {}
    reg = dict(DRAM_REG)
    reg.update(WEIGHT_REG)
    for n in names:
        shape, dt = reg[n]
        aps[n] = nc.dram_tensor(n, shape, dt, kind=kind_map.get(n, "Internal")).ap()
    return aps


def emit_xlnT_load(p, eng, src_ap, xlt, sem, ncols=16, col0=0):
    R = src_ap.shape[0]
    for dc in range(ncols):
        c = col0 + dc
        p.inc(eng.dma_start(
            out=xlt[:, dc * R:(dc + 1) * R],
            in_=src_ap[:, c * 128:(c + 1) * 128],
            transpose=True), sem, 16)
    return p.cnt[sem]


def ph_init(p, g):
    nc = p.nc
    g["ident"] = p.ar.alloc("g_ident", [128, 128], F32)
    g["identb"] = p.ar.alloc("g_identb", [128, 128], BF16)
    g["ones_r"] = p.ar.alloc("g_ones", [1, 128], BF16)
    g["onesf_r"] = p.ar.alloc("g_onesf", [1, 128], F32)
    g["onesf_c"] = p.ar.alloc("g_onesfc", [128, 1], F32)
    g["onesb_512"] = p.ar.alloc("g_onesb512", [1, 512], BF16)
    g["onesb_c"] = p.ar.alloc("g_onesbc", [128, 1], BF16)
    g["sel"] = p.ar.alloc("g_sel", [8, 8 * 128], F32)
    g["eps"] = p.ar.alloc("g_eps", [128, 1], F32)
    g["Dt"] = p.ar.alloc("g_Dt", [128, 256], F32)
    g["ps"] = [nc.alloc_psum_tensor(f"g_ps{i}", [128, 512], F32) for i in range(8)]
    p.new_block()
    blk = p.blk

    @blk.gpsimd
    def _(g_):
        g_.memset(g["ident"][:], 0.0)
        g_.memset(g["identb"][:], 0.0)
        g_.memset(g["sel"][:], 0.0)
        g_.memset(g["ones_r"][:], 1.0)
        g_.memset(g["onesf_r"][:], 1.0)
        g_.memset(g["onesf_c"][:], 1.0)
        g_.memset(g["onesb_512"][:], 1.0)
        g_.memset(g["onesb_c"][:], 1.0)
        g_.memset(g["eps"][:], 1e-5)
        g_.drain()
        g_.affine_select(out=g["ident"][:], in_=g["ident"][:], compare_op=ALU.not_equal,
                         fill=1.0, base=0, pattern=[[-1, 128]], channel_multiplier=1)
        g_.affine_select(out=g["identb"][:], in_=g["identb"][:], compare_op=ALU.not_equal,
                         fill=1.0, base=0, pattern=[[-1, 128]], channel_multiplier=1)
        g_.iota(g["Dt"][:], pattern=[[-1, 256]], base=0, channel_multiplier=1,
                allow_small_or_imprecise_dtypes=True)
        for k in range(8):
            g_.affine_select(out=g["sel"][:, k * 128:(k + 1) * 128],
                             in_=g["sel"][:, k * 128:(k + 1) * 128],
                             compare_op=ALU.not_equal, fill=1.0, base=-k,
                             pattern=[[0, 128]], channel_multiplier=1)
        g_.drain()
        p.inc(g_.nop(), "init", 1)


def ph_ln0(p, io, g, xsrc="x0", xdst="xln0"):
    p.phase_boundary()
    nc = p.nc
    ar = p.ar
    x_ap, xln_ap = io[xsrc], io[xdst]
    xs = [ar.alloc(f"ln0_x{i}", [128, D], F32) for i in range(3)]
    stats = ar.alloc("ln0_stats", [128, 16 * 24], F32)
    mv = ar.alloc("ln0_mv", [128, 32], F32)
    rstd = ar.alloc("ln0_rstd", [128, 16], F32)
    rstd2 = ar.alloc("ln0_rstd2", [128, 16], F32)
    xlb = [ar.alloc(f"ln0_xlb{i}", [128, D], BF16) for i in range(2)]
    p.new_block()
    blk = p.blk
    b_vs, b_rs, b_nm = p.base("vs", "rs", "nm")
    LD = p.plan("Lld", 3, 32, q="s")
    ST = p.plan("Lst", 2, 16)

    @blk.sync
    def _(s):
        for t in range(16):
            if t >= 3:
                s.wait_ge(p.sem("vs"), b_vs + t - 2)
            tgt = p.inc(s.dma_start(out=xs[t % 3][:], in_=x_ap[t * 128:(t + 1) * 128, :]), LD[t][0], 16)
            assert tgt == LD[t][1]

    @blk.vector
    def _(v):
        v = VD(v)
        for t in range(16):
            v.wait_ge(p.sem(LD[t][0]), LD[t][1])
            for q in range(4):
                v.bn_stats(stats[:, t * 24 + q * 6:t * 24 + (q + 1) * 6],
                           xs[t % 3][:, q * 512:(q + 1) * 512])
            v.drain()
            p.inc(v.bn_aggr(mv[:, 2 * t:2 * t + 2], stats[:, t * 24:(t + 1) * 24]), "vs", 1)

    @blk.scalar
    def _(se):
        se = VD(se)
        p.wait(se, "vs")
        p.inc(se.activation(rstd[:], mv[:, 1:32:2], AF.Sqrt, bias=g["eps"][:, 0:1], scale=1.0), "rs", 1)

    @blk.vector
    def _(v):
        v = VD(v)
        p.wait(v, "rs")
        p.inc(v.reciprocal(rstd2[:], rstd[:]), "rs", 1)

    @blk.sync
    def _(s):
        s.wait_ge(p.sem("rs"), b_rs + 2)
        for t in range(16):
            if t >= 3:
                s.wait_ge(p.sem("nm"), b_nm + t - 2)
            p.inc(s.dma_start(out=xs[t % 3][:], in_=x_ap[t * 128:(t + 1) * 128, :]), LD[16 + t][0], 16)

    @blk.vector
    def _(v):
        v = VD(v)
        for t in range(16):
            v.wait_ge(p.sem(LD[16 + t][0]), LD[16 + t][1])
            if t >= 2:
                v.wait_ge(p.sem(ST[t - 2][0]), ST[t - 2][1])
            p.inc(v.tensor_scalar(xlb[t % 2][:], xs[t % 3][:], mv[:, 2 * t:2 * t + 1],
                                  rstd2[:, t:t + 1], ALU.subtract, ALU.mult), "nm", 1)

    @blk.gpsimd
    def _(g_):
        for t in range(16):
            g_.wait_ge(p.sem("nm"), b_nm + t + 1)
            p.inc(g_.dma_start(out=xln_ap[t * 128:(t + 1) * 128, :], in_=xlb[t % 2][:]), ST[t][0], 16)
        for i in range(2):
            g_.wait_ge(p.sem(ST[15 - i][0]), ST[15 - i][1])
    ar.free(*xs, stats, mv, rstd, rstd2, *xlb)


def ph_l0ffn(p, io, g, xln_src, x_src, x_dst, xln_dst, tag, dbg=None, cast_pairs=None):
    p.phase_boundary()
    nc = p.nc
    ar = p.ar
    xln_ap, x_ap, xo_ap, xlo_ap = io[xln_src], io[x_src], io[x_dst], io[xln_dst]
    ps = g["ps"]

    xlt = ar.alloc(f"{tag}_xlt", [128, 16 * TLOC], BF16)
    hbf = ar.alloc(f"{tag}_hbf", [128, 8 * TLOC], BF16)
    wkb = ar.alloc(f"{tag}_wkb", [128, 8 * TLOC], BF16)
    statsE = ar.alloc(f"{tag}_stE", [128, 16 * 24], F32)
    wrb = ar.alloc(f"{tag}_wrb", [128, 16 * 128], BF16)
    Af = ar.alloc(f"{tag}_Af", [128, 8], F32)

    amp2 = ar.alloc(f"{tag}_amp", [128, 16], F32)
    escb = ar.alloc(f"{tag}_esc", [128, 1], F32)
    ppb = ar.alloc(f"{tag}_pp", [128, 11], F32)
    p.new_block()
    blk = p.blk

    @blk.scalar
    def _(se):
        se = VD(se)
        emit_xlnT_load(p, se, xln_ap, xlt, "xlt")

    @blk.gpsimd
    def _(g_):
        p.inc(g_.dma_start(out=wrb[:, :].rearrange("p (c f) -> p c f", c=16),
                           in_=io["L0_ffn_w_router"].rearrange("(c p) f -> p c f", p=128)), "gd", 16)
        p.inc(g_.dma_start(out=amp2[:, :], in_=io["L0_ffn_amplitudes"].rearrange("e k c -> e (k c)")), "gd", 16)
        p.inc(g_.dma_start(out=escb[:, :], in_=io["L0_ffn_expert_scale"][:, None]), "gd", 16)

    @blk.scalar
    def _(se):
        se = VD(se)
        p.wait(se, "gd")
        p.inc(se.activation(amp2[:], amp2[:], AF.Square), "ae", 1)

    @blk.vector
    def _(v):
        v = VD(v)
        p.wait(v, "ae")
        v.tensor_tensor(ppb[:, 0:8], amp2[:, 0:16:2], amp2[:, 1:16:2], ALU.add)
        v.tensor_reduce(ppb[:, 8:9], ppb[:, 0:8], axis=AX.X, op=ALU.add)
        v.tensor_scalar(ppb[:, 9:10], ppb[:, 8:9], 1e-8, None, ALU.add)
        v.reciprocal(ppb[:, 10:11], ppb[:, 9:10])
        p.inc(v.tensor_scalar(Af[:], ppb[:, 0:8], ppb[:, 10:11], escb[:, 0:1],
                              ALU.mult, ALU.mult), "ve", 1)

    lg = [ar.alloc(f"{tag}_lg{i}", [128, 128], F32) for i in range(2)]
    cur = ar.alloc(f"{tag}_cur", [128, 128], F32)
    eqn = ar.alloc(f"{tag}_eqn", [128, 128], F32)
    esb = [ar.alloc(f"{tag}_esb{i}", [128, 128], F32) for i in range(2)]
    sml = [ar.alloc(f"{tag}_sml{i}", [128, 12], F32) for i in range(2)]
    coeff = [ar.alloc(f"{tag}_cf{i}", [128, 128], F32) for i in range(2)]
    ctT = ar.alloc(f"{tag}_ctT", [128, TLOC], F32)
    wT = ar.alloc(f"{tag}_wT", [8, TLOC], F32)
    p.new_block()
    blk = p.blk
    b_rlg, b_rlx, b_rng, b_rex, b_rcf, b_rtp, b_rtx, b_rwkx = p.base(
        "rlg", "rlx", "rng", "rex", "rcf", "rtp", "rtx", "rwkx")

    @blk.tensor
    def _(t):
        p.wait(t, "xlt")
        p.wait(t, "gd")
        def emit_tr(s):
            t.wait_ge(p.sem("rcf"), b_rcf + s + 1)
            if s >= 2:
                t.wait_ge(p.sem("rtx"), b_rtx + s - 1)
            p.inc(t.transpose(ps[2 + s % 2][:, 0:128], coeff[s % 2][:], g["ident"][:]), "rtp", 1)
        for s in range(16):
            if s >= 2:
                t.wait_ge(p.sem("rlx"), b_rlx + s - 1)
            for dc in range(16):
                mm = t.matmul(ps[s % 2][:, 0:128],
                              xlt[:, dc * TLOC + s * 128: dc * TLOC + (s + 1) * 128],
                              wrb[:, dc * 128:(dc + 1) * 128],
                              start=(dc == 0), stop=(dc == 15))
            p.inc(mm, "rlg", 1)
            if s >= 2:
                emit_tr(s - 2)
        emit_tr(14)
        emit_tr(15)

    @blk.vector
    def _(v):
        v = VD(v)
        for s in range(16):
            m = sml[s % 2]
            v.wait_ge(p.sem("rlg"), b_rlg + s + 1)
            p.inc(v.tensor_scalar(lg[s % 2][:], ps[s % 2][:, 0:128], 1.0, NEG,
                                  ALU.mult, ALU.max, accum_out=m[:, 0:1]), "rlx", 1)
            p.inc(v.tensor_scalar(m[:, 1:2], m[:, 0:1], -1.0, None, ALU.mult), "rng", 1)
            v.tensor_copy(cur[:], lg[s % 2][:])
            mprev = m[:, 0:1]
            for it in range(3):
                v.tensor_scalar(eqn[:], cur[:], mprev, NEG, ALU.is_equal, ALU.mult)
                v.tensor_tensor(cur[:], cur[:], eqn[:], ALU.add)
                v.tensor_reduce(m[:, 2 + it:3 + it], cur[:], axis=AX.X, op=ALU.max)
                mprev = m[:, 2 + it:3 + it]
            v.wait_ge(p.sem("rex"), b_rex + s + 1)
            v.tensor_scalar(eqn[:], lg[s % 2][:], m[:, 4:5], None, ALU.is_ge)
            v.tensor_tensor(esb[s % 2][:], esb[s % 2][:], eqn[:], ALU.mult)
            v.tensor_reduce(m[:, 6:7], esb[s % 2][:], axis=AX.X, op=ALU.add)
            v.scalar_tensor_tensor(m[:, 7:8], m[:, 5:6], 1e-8, m[:, 6:7],
                                   op0=ALU.mult, op1=ALU.add)
            v.reciprocal(m[:, 8:9], m[:, 7:8])
            if s >= 2:
                v.wait_ge(p.sem("rtp"), b_rtp + s - 1)
            p.inc(v.tensor_scalar(coeff[s % 2][:], esb[s % 2][:], m[:, 8:9], None,
                                  ALU.mult), "rcf", 1)

    @blk.scalar
    def _(se):
        se = VD(se)
        def emit_cp(s):
            se.wait_ge(p.sem("rtp"), b_rtp + s + 1)
            p.inc(se.copy(ctT[:, s * 128:(s + 1) * 128], ps[2 + s % 2][:, 0:128]), "rtx", 1)
        for s in range(16):
            se.wait_ge(p.sem("rng"), b_rng + s + 1)
            if s >= 2:
                se.wait_ge(p.sem("rcf"), b_rcf + s - 1)
            p.inc(se.activation(esb[s % 2][:], lg[s % 2][:], AF.Exp, bias=sml[s % 2][:, 1:2],
                                scale=1.0, accum_out=sml[s % 2][:, 5:6]), "rex", 1)
            if s >= 2:
                emit_cp(s - 2)
        emit_cp(14)
        emit_cp(15)

    @blk.tensor
    def _(t):
        p.wait(t, "rtx")
        p.wait(t, "ve")
        for tb in range(4):
            mm = t.matmul(ps[4 + tb][0:8, 0:512], Af[:], ctT[:, tb * 512:(tb + 1) * 512],
                          start=True, stop=True)
        p.inc(mm, "rwt", 1)

    @blk.scalar
    def _(se):
        se = VD(se)
        p.wait(se, "rwt")
        for tb in range(4):
            cp = se.copy(wT[:, tb * 512:(tb + 1) * 512], ps[4 + tb][0:8, 0:512])
        p.inc(cp, "rwte", 1)

    @blk.tensor
    def _(t):
        p.wait(t, "rwte")
        for k in range(8):
            if k >= 1:
                t.wait_ge(p.sem("rwkx"), b_rwkx + k)
            for tb in range(4):
                mm = t.matmul(ps[4 + tb][:, 0:512], g["sel"][:, k * 128:(k + 1) * 128],
                              wT[0:8, tb * 512:(tb + 1) * 512], start=True, stop=True)
            p.inc(mm, "rwk", 1)

    @blk.vector
    def _(v):
        v = VD(v)
        b_rwk = p.cnt_of("rwk") - 8
        for k in range(8):
            v.wait_ge(p.sem("rwk"), b_rwk + k + 1)
            for tb in range(4):
                cp = v.tensor_copy(wkb[:, k * TLOC + tb * 512: k * TLOC + (tb + 1) * 512],
                                   ps[4 + tb][:, 0:512])
            p.inc(cp, "rwkx", 1)

    if dbg is not None and "d_wkb" in dbg:
        p.new_block()
        blk = p.blk

        @blk.gpsimd
        def _(g_):
            p.inc(g_.dma_start(out=dbg["d_wkb"][:, :], in_=wkb[:, :]), "dbs", 16)
            p.inc(g_.dma_start(out=dbg["d_ctT"][:, :], in_=ctT[:, :]), "dbs", 16)
            p.inc(g_.dma_start(out=dbg["d_wT"][:, :], in_=wT[:, :]), "dbs", 16)
            p.wait(g_, "dbs")
    ar.free(*lg, cur, eqn, *esb, *sml, *coeff, ctT, wT, amp2, escb, ppb, wrb, Af)

    wg_sb = [ar.alloc(f"{tag}_wg{i}", [128, 2048], BF16) for i in range(2)]
    wu_sb = [ar.alloc(f"{tag}_wu{i}", [128, 2048], BF16) for i in range(2)]
    silu_sb = [ar.alloc(f"{tag}_sl{i}", [128, 512], BF16) for i in range(2)]
    basis_sb = ar.alloc(f"{tag}_bs", [128, 512], BF16)
    bw_sb = ar.alloc(f"{tag}_bw", [128, 512], BF16)
    hA = ar.alloc(f"{tag}_hA", [128, 2048], F32)
    hB = ar.alloc(f"{tag}_hB", [128, 2048], F32)
    p.new_block()
    blk = p.blk
    b_gmm, b_umm, b_sl, b_bu = p.base("gmm", "umm", "sl", "bu")
    WL = p.plan("wl", 2, 64, unit=32)
    wg_ap = io["L0_ffn_w_gate"]
    wu_ap = io["L0_ffn_w_up"]

    @blk.gpsimd
    def _(g_):
        for j in range(64):
            ic, k = j // 8, j % 8
            c0 = k * 1024 + ic * 128
            if j >= 2:
                g_.wait_ge(p.sem("bu"), b_bu + 4 * (j - 1))
            p.inc(g_.dma_start(out=wg_sb[j % 2][:, :].rearrange("p (c f) -> p c f", c=16),
                               in_=wg_ap[:, c0:c0 + 128].rearrange("(c p) f -> p c f", p=128)), WL[j][0], 16)
            tgt = p.inc(g_.dma_start(out=wu_sb[j % 2][:, :].rearrange("p (c f) -> p c f", c=16),
                               in_=wu_ap[:, c0:c0 + 128].rearrange("(c p) f -> p c f", p=128)), WL[j][0], 16)
            assert tgt == WL[j][1]

    @blk.tensor
    def _(t):
        for j in range(64):
            ic, k = j // 8, j % 8
            t.wait_ge(p.sem(WL[j][0]), WL[j][1])
            for tb in range(4):
                jj = j * 4 + tb
                if jj >= 2:
                    t.wait_ge(p.sem("sl"), b_sl + jj - 1)
                    t.wait_ge(p.sem("bu"), b_bu + jj - 1)
                for hf in range(2):
                    c0 = tb * 512 + hf * 256
                    for dc in range(16):
                        mm = t.matmul(ps[tb % 2][:, hf * 256:hf * 256 + 256],
                                      wg_sb[j % 2][:, dc * 128:(dc + 1) * 128],
                                      xlt[:, dc * TLOC + c0: dc * TLOC + c0 + 256],
                                      start=(dc == 0), stop=(dc == 15))
                p.inc(mm, "gmm", 1)
                for hf in range(2):
                    c0 = tb * 512 + hf * 256
                    for dc in range(16):
                        mm = t.matmul(ps[2 + tb % 2][:, hf * 256:hf * 256 + 256],
                                      wu_sb[j % 2][:, dc * 128:(dc + 1) * 128],
                                      xlt[:, dc * TLOC + c0: dc * TLOC + c0 + 256],
                                      start=(dc == 0), stop=(dc == 15))
                p.inc(mm, "umm", 1)

    @blk.scalar
    def _(se):
        se = VD(se)
        for j in range(64):
            for tb in range(4):
                jj = j * 4 + tb
                se.wait_ge(p.sem("gmm"), b_gmm + jj + 1)
                if jj >= 2:
                    se.wait_ge(p.sem("bu"), b_bu + jj - 1)
                p.inc(se.activation(silu_sb[jj % 2][:], ps[tb % 2][:, 0:512], AF.Silu), "sl", 1)

    @blk.vector
    def _(v):
        v = VD(v)
        for j in range(64):
            ic, k = j // 8, j % 8
            hsrc = [hA, hB][(k - 1) % 2] if k >= 1 else None
            hdst = [hA, hB][k % 2]
            for tb in range(4):
                jj = j * 4 + tb
                tbr = slice(tb * 512, (tb + 1) * 512)
                v.wait_ge(p.sem("sl"), b_sl + jj + 1)
                v.wait_ge(p.sem("umm"), b_umm + jj + 1)
                p.inc(v.tensor_tensor(basis_sb[:], silu_sb[jj % 2][:], ps[2 + tb % 2][:, 0:512],
                                      ALU.mult), "bu", 1)
                v.tensor_tensor(bw_sb[:], basis_sb[:],
                                wkb[:, k * TLOC + tb * 512:k * TLOC + (tb + 1) * 512], ALU.mult)
                if k == 0:
                    v.tensor_copy(hdst[:, tbr], bw_sb[:])
                else:
                    v.tensor_tensor(hdst[:, tbr], hsrc[:, tbr], bw_sb[:], ALU.add)
                if k == 7:
                    p.inc(v.tensor_copy(hbf[:, ic * TLOC + tb * 512:ic * TLOC + (tb + 1) * 512],
                                        hdst[:, tbr]), "hcv", 1)

    if dbg is not None and "d_hbf" in dbg:
        p.new_block()
        blk = p.blk

        @blk.gpsimd
        def _(g_):
            p.inc(g_.dma_start(out=dbg["d_hbf"][:, :], in_=hbf[:, :]), "dbs", 16)
            p.wait(g_, "dbs")
    ar.free(*wg_sb, *wu_sb, *silu_sb, basis_sb, bw_sb, hA, hB, wkb)

    hs = [ar.alloc(f"{tag}_hs{i}", [128, 8 * TLOC], BF16) for i in range(2)]
    wgs_sb = [ar.alloc(f"{tag}_wgs{i}", [128, 2048], BF16) for i in range(2)]
    wus_sb = [ar.alloc(f"{tag}_wus{i}", [128, 2048], BF16) for i in range(2)]
    silu2 = [ar.alloc(f"{tag}_sl2{i}", [128, 512], BF16) for i in range(2)]
    p.new_block()
    blk = p.blk
    b_gmm, b_umm, b_sl, b_bu = p.base("gmm", "umm", "sl", "bu")
    WL = p.plan("wl", 2, 16, unit=32)
    wsg_ap = io["L0_ffn_shared_w_gate"]
    wsu_ap = io["L0_ffn_shared_w_up"]

    @blk.gpsimd
    def _(g_):
        for j in range(16):
            c0 = j * 128
            if j >= 2:
                g_.wait_ge(p.sem("bu"), b_bu + 4 * (j - 1))
            p.inc(g_.dma_start(out=wgs_sb[j % 2][:, :].rearrange("p (c f) -> p c f", c=16),
                               in_=wsg_ap[:, c0:c0 + 128].rearrange("(c p) f -> p c f", p=128)), WL[j][0], 16)
            tgt = p.inc(g_.dma_start(out=wus_sb[j % 2][:, :].rearrange("p (c f) -> p c f", c=16),
                               in_=wsu_ap[:, c0:c0 + 128].rearrange("(c p) f -> p c f", p=128)), WL[j][0], 16)
            assert tgt == WL[j][1]

    @blk.tensor
    def _(t):
        for j in range(16):
            t.wait_ge(p.sem(WL[j][0]), WL[j][1])
            for tb in range(4):
                jj = j * 4 + tb
                if jj >= 2:
                    t.wait_ge(p.sem("sl"), b_sl + jj - 1)
                    t.wait_ge(p.sem("bu"), b_bu + jj - 1)
                for hf in range(2):
                    c0 = tb * 512 + hf * 256
                    for dc in range(16):
                        mm = t.matmul(ps[tb % 2][:, hf * 256:hf * 256 + 256],
                                      wgs_sb[j % 2][:, dc * 128:(dc + 1) * 128],
                                      xlt[:, dc * TLOC + c0: dc * TLOC + c0 + 256],
                                      start=(dc == 0), stop=(dc == 15))
                p.inc(mm, "gmm", 1)
                for hf in range(2):
                    c0 = tb * 512 + hf * 256
                    for dc in range(16):
                        mm = t.matmul(ps[2 + tb % 2][:, hf * 256:hf * 256 + 256],
                                      wus_sb[j % 2][:, dc * 128:(dc + 1) * 128],
                                      xlt[:, dc * TLOC + c0: dc * TLOC + c0 + 256],
                                      start=(dc == 0), stop=(dc == 15))
                p.inc(mm, "umm", 1)

    @blk.scalar
    def _(se):
        se = VD(se)
        for j in range(16):
            for tb in range(4):
                jj = j * 4 + tb
                se.wait_ge(p.sem("gmm"), b_gmm + jj + 1)
                if jj >= 2:
                    se.wait_ge(p.sem("bu"), b_bu + jj - 1)
                p.inc(se.activation(silu2[jj % 2][:], ps[tb % 2][:, 0:512], AF.Silu), "sl", 1)

    @blk.vector
    def _(v):
        v = VD(v)
        for j in range(16):
            for tb in range(4):
                jj = j * 4 + tb
                v.wait_ge(p.sem("sl"), b_sl + jj + 1)
                v.wait_ge(p.sem("umm"), b_umm + jj + 1)
                p.inc(v.tensor_tensor(
                    hs[j // 8][:, (j % 8) * TLOC + tb * 512:(j % 8) * TLOC + (tb + 1) * 512],
                    silu2[jj % 2][:], ps[2 + tb % 2][:, 0:512], ALU.mult), "bu", 1)

    if dbg is not None and "d_hs" in dbg:
        p.new_block()
        blk = p.blk

        @blk.gpsimd
        def _(g_):
            p.inc(g_.dma_start(out=dbg["d_hs"][:, :], in_=hs[0][:, :]), "dbs", 16)
            p.inc(g_.dma_start(out=dbg["d_hs2"][:, :], in_=hs[1][:, :]), "dbs", 16)
            p.wait(g_, "dbs")
    ar.free(*wgs_sb, *wus_sb, *silu2, xlt)

    wdm = [ar.alloc(f"{tag}_wdm{i}", [128, 8 * 512], BF16) for i in range(2)]
    wsdm = [ar.alloc(f"{tag}_wsdm{i}", [128, 16 * 512], BF16) for i in range(2)]
    xo_sb = [ar.alloc(f"{tag}_xo{i}", [128, 512], F32) for i in range(3)]
    xn_sb = [ar.alloc(f"{tag}_xn{i}", [128, 512], F32) for i in range(2)]
    p.new_block()
    blk = p.blk
    b_emm, b_ef, b_xn = p.base("emm", "ef", "xn")
    WL = p.plan("wl", 2, 4, unit=32)
    EL = p.plan("el", 3, 64, q="s")
    ES = p.plan("es", 2, 64)
    wd_ap = io["L0_ffn_w_down"]
    wsd_ap = io["L0_ffn_shared_w_down"]

    def emit_wload(g_, m):
        if m >= 2:
            g_.wait_ge(p.sem("ef"), b_ef + (m - 1) * 16)
        p.inc(g_.dma_start(out=wdm[m % 2][:, :].rearrange("p (c f) -> p c f", c=8),
                           in_=wd_ap[:, m * 512:(m + 1) * 512].rearrange("(c p) f -> p c f", p=128)), WL[m][0], 16)
        tgt = p.inc(g_.dma_start(out=wsdm[m % 2][:, :].rearrange("p (c f) -> p c f", c=16),
                           in_=wsd_ap[:, m * 512:(m + 1) * 512].rearrange("(c p) f -> p c f", p=128)), WL[m][0], 16)
        assert tgt == WL[m][1]

    @blk.sync
    def _(s):
        for m in range(4):
            for t in range(16):
                e = m * 16 + t
                if e >= 3:
                    s.wait_ge(p.sem("xn"), b_xn + e - 2)
                p.inc(s.dma_start(out=xo_sb[e % 3][:],
                                  in_=x_ap[t * 128:(t + 1) * 128, m * 512:(m + 1) * 512]), EL[e][0], 16)

    @blk.tensor
    def _(t_):
        p.wait(t_, "hcv")
        for m in range(4):
            t_.wait_ge(p.sem(WL[m][0]), WL[m][1])
            for t in range(16):
                e = m * 16 + t
                if e >= 2:
                    t_.wait_ge(p.sem("ef"), b_ef + e - 1)
                bank = ps[4 + e % 2]
                for hf in range(2):
                    for ic in range(8):
                        t_.matmul(bank[:, hf * 256:hf * 256 + 256],
                                  hbf[:, ic * TLOC + t * 128:ic * TLOC + (t + 1) * 128],
                                  wdm[m % 2][:, ic * 512 + hf * 256:ic * 512 + hf * 256 + 256],
                                  start=(ic == 0), stop=False)
                    for i2 in range(16):
                        mm = t_.matmul(bank[:, hf * 256:hf * 256 + 256],
                                       hs[i2 // 8][:, (i2 % 8) * TLOC + t * 128:(i2 % 8) * TLOC + (t + 1) * 128],
                                       wsdm[m % 2][:, i2 * 512 + hf * 256:i2 * 512 + hf * 256 + 256],
                                       start=False, stop=(i2 == 15))
                p.inc(mm, "emm", 1)

    @blk.vector
    def _(v):
        v = VD(v)
        for m in range(4):
            for t in range(16):
                e = m * 16 + t
                v.wait_ge(p.sem("emm"), b_emm + e + 1)
                v.wait_ge(p.sem(EL[e][0]), EL[e][1])
                if e >= 2:
                    v.wait_ge(p.sem(ES[e - 2][0]), ES[e - 2][1])
                p.inc(v.scalar_tensor_tensor(xn_sb[e % 2][:], ps[4 + e % 2][:, 0:512], 0.5,
                                             xo_sb[e % 3][:], op0=ALU.mult, op1=ALU.add), "ef", 1)
                p.inc(v.bn_stats(statsE[:, t * 24 + m * 6:t * 24 + (m + 1) * 6],
                                 xn_sb[e % 2][:]), "xn", 1)

    @blk.gpsimd
    def _(g_):
        if cast_pairs:
            for srcn, dstn in cast_pairs:
                p.inc(g_.dma_start(out=io[dstn][:, :], in_=io[srcn][:, :]), "cw", 16)
        emit_wload(g_, 0)
        emit_wload(g_, 1)
        for m in range(4):
            for t in range(16):
                e = m * 16 + t
                g_.wait_ge(p.sem("xn"), b_xn + e + 1)
                p.inc(g_.dma_start(out=xo_ap[t * 128:(t + 1) * 128, m * 512:(m + 1) * 512],
                                   in_=xn_sb[e % 2][:]), ES[e][0], 16)
                if e in (14, 30):
                    emit_wload(g_, 2 + (e - 14) // 16)
        for i in range(2):
            g_.wait_ge(p.sem(ES[63 - i][0]), ES[63 - i][1])

    ar.free(*wdm, *wsdm, *xo_sb, *xn_sb, hbf, *hs)

    emit_lnpass(p, io, g, x_dst, xln_dst, statsE, tag)
    ar.free(statsE)


def emit_lnpass(p, io, g, x_src, xln_dst, statsE, tag, tps=24):
    nc = p.nc
    ar = p.ar
    x_ap, xln_ap = io[x_src], io[xln_dst]
    mv = ar.alloc(f"{tag}_mvF", [128, 32], F32)
    rstd = ar.alloc(f"{tag}_rsF", [128, 16], F32)
    rstd2 = ar.alloc(f"{tag}_rs2F", [128, 16], F32)
    xs = [ar.alloc(f"{tag}_xF{i}", [128, D], F32) for i in range(3)]
    xlb = [ar.alloc(f"{tag}_xlF{i}", [128, D], BF16) for i in range(2)]
    p.new_block()
    blk = p.blk
    b_vs, b_rs, b_nm = p.base("vs", "rs", "nm")
    LD = p.plan("Lld", 3, 16, q="s")
    ST = p.plan("Lst", 2, 16)

    @blk.vector
    def _(v):
        v = VD(v)
        v.drain()
        for t in range(16):
            ag = v.bn_aggr(mv[:, 2 * t:2 * t + 2], statsE[:, t * tps:(t + 1) * tps])
        p.inc(ag, "vs", 1)

    @blk.scalar
    def _(se):
        se = VD(se)
        p.wait(se, "vs")
        p.inc(se.activation(rstd[:], mv[:, 1:32:2], AF.Sqrt, bias=g["eps"][:, 0:1], scale=1.0), "rs", 1)

    @blk.vector
    def _(v):
        v = VD(v)
        p.wait(v, "rs")
        p.inc(v.reciprocal(rstd2[:], rstd[:]), "rs", 1)

    @blk.sync
    def _(s):
        for t in range(16):
            if t >= 3:
                s.wait_ge(p.sem("nm"), b_nm + t - 2)
            tgt = p.inc(s.dma_start(out=xs[t % 3][:], in_=x_ap[t * 128:(t + 1) * 128, :]), LD[t][0], 16)
            assert tgt == LD[t][1]

    @blk.vector
    def _(v):
        v = VD(v)
        v.wait_ge(p.sem("rs"), b_rs + 2)
        for t in range(16):
            v.wait_ge(p.sem(LD[t][0]), LD[t][1])
            if t >= 2:
                v.wait_ge(p.sem(ST[t - 2][0]), ST[t - 2][1])
            p.inc(v.tensor_scalar(xlb[t % 2][:], xs[t % 3][:], mv[:, 2 * t:2 * t + 1],
                                  rstd2[:, t:t + 1], ALU.subtract, ALU.mult), "nm", 1)

    @blk.gpsimd
    def _(g_):
        for t in range(16):
            g_.wait_ge(p.sem("nm"), b_nm + t + 1)
            p.inc(g_.dma_start(out=xln_ap[t * 128:(t + 1) * 128, :], in_=xlb[t % 2][:]), ST[t][0], 16)
        for i in range(2):
            g_.wait_ge(p.sem(ST[15 - i][0]), ST[15 - i][1])
    ar.free(mv, rstd, rstd2, *xs, *xlb)


def ph_castw(p, io, pairs):
    p.phase_boundary()
    p.new_block()
    blk = p.blk

    @blk.gpsimd
    def _(g_):
        for srcn, dstn in pairs:
            p.inc(g_.dma_start(out=io[dstn][:, :], in_=io[srcn][:, :]), "cw", 16)
        p.wait(g_, "cw")


def ph_l0qkv(p, io, g, xln_src, tag, do_ag=True):
    p.phase_boundary()
    nc = p.nc
    ar = p.ar
    xln_ap = io[xln_src]
    ps = g["ps"]
    xlt = ar.alloc(f"{tag}_xlt", [128, 16 * TLOC], BF16)
    wt = ar.alloc(f"{tag}_wt", [128, 16 * TLOC], BF16)
    qsb = ar.alloc(f"{tag}_qsb", [128, TLOC], F32)
    qsq = ar.alloc(f"{tag}_qsq", [128, TLOC], F32)
    scl = ar.alloc(f"{tag}_scl", [128, TLOC], F32)
    rowb = ar.alloc(f"{tag}_rowb", [1, 2048], F32)
    rowb2 = ar.alloc(f"{tag}_rowb2", [1, 2048], F32)
    grow = ar.alloc(f"{tag}_grow", [1, 256], F32)
    onb = [ar.alloc(f"{tag}_onb{i}", [128, TLOC], BF16) for i in range(2)]
    vsb = [ar.alloc(f"{tag}_vsb{i}", [128, 512], BF16) for i in range(2)]

    p.new_block()
    blk = p.blk
    b_qmm, b_qev, b_qsq, b_qss, b_qsr, b_qrc, b_qbc, b_qsc, b_qmu, b_qst = p.base(
        "qmm", "qev", "qsq", "qss", "qsr", "qrc", "qbc", "qsc", "qmu", "qst")
    HS = 128 ** -0.5

    @blk.sync
    def _(s):
        p.inc(s.dma_start(out=grow[0:1, 0:128], in_=io["L0_attn_gq"][None, :]), "gld", 16)
        p.inc(s.dma_start(out=grow[0:1, 128:256], in_=io["L0_attn_gk"][None, :]), "gld", 16)

    @blk.vector
    def _(v):
        v = VD(v)
        p.wait(v, "gld")
        p.inc(v.tensor_scalar(grow[0:1, 0:128], grow[0:1, 0:128], HS, None, ALU.mult), "gsc", 1)

    @blk.scalar
    def _(se):
        se = VD(se)
        emit_xlnT_load(p, se, xln_ap, xlt, "xlt")

    def emit_wT_load(se, wname):
        p.wait(se, "cw")
        se.wait_ge(p.sem("qmm"), p.cnt_of("qmm"))
        for dc in range(16):
            p.inc(se.dma_start(out=wt[:, dc * TLOC:(dc + 1) * TLOC],
                               in_=io[wname][:, dc * 128:(dc + 1) * 128],
                               transpose=True), "wtl", 16)

    def emit_qk(which, wname, dst_ap, grow_sl):
        b_wtl = p.cnt_of("wtl")

        @blk.scalar
        def _(se):
            se = VD(se)
            emit_wT_load(se, wname)

        bq = {n: p.cnt_of(n) for n in ["qmm", "qev", "qsq", "qss", "qsr", "qrc", "qbc", "qsc", "qmu"]}
        QN = p.plan("qnst", 2, 16, unit=16)

        @blk.tensor
        def _(t):
            p.wait(t, "xlt")
            t.wait_ge(p.sem("wtl"), b_wtl + 256)
            if which == 0:
                p.wait(t, "gsc")
            for m in range(16):
                for tb in range(4):
                    need = bq["qsc"] + (m - 1) * 4 + tb + 1
                    if need > 0:
                        t.wait_ge(p.sem("qsc"), need)
                    for hf in range(2):
                        c0 = tb * 512 + hf * 256
                        for dc in range(16):
                            mm = t.matmul(ps[tb][:, hf * 256:hf * 256 + 256],
                                          wt[:, dc * TLOC + m * 128:dc * TLOC + (m + 1) * 128],
                                          xlt[:, dc * TLOC + c0:dc * TLOC + c0 + 256],
                                          start=(dc == 0), stop=(dc == 15))
                    p.inc(mm, "qmm", 1)
                for tb in range(4):
                    t.wait_ge(p.sem("qsq"), bq["qsq"] + m + 1)
                    need = bq["qsr"] + (m - 1) * 4 + tb + 1
                    if need > 0:
                        t.wait_ge(p.sem("qsr"), need)
                    p.inc(t.matmul(ps[4 + tb][0:1, 0:512], g["onesf_c"][:, 0:1],
                                   qsq[:, tb * 512:(tb + 1) * 512], start=True, stop=True), "qss", 1)
                for tb in range(4):
                    t.wait_ge(p.sem("qrc"), bq["qrc"] + m * 4 + tb + 1)
                    t.wait_ge(p.sem("qev"), bq["qev"] + m * 4 + tb + 1)
                    p.inc(t.matmul(ps[tb][:, 0:512], grow_sl,
                                   rowb2[0:1, tb * 512:(tb + 1) * 512], start=True, stop=True), "qbc", 1)

        @blk.scalar
        def _(se):
            se = VD(se)
            for m in range(16):
                for tb in range(4):
                    se.wait_ge(p.sem("qmm"), bq["qmm"] + m * 4 + tb + 1)
                    need = bq["qmu"] + (m - 1) * 4 + tb + 1
                    if need > 0:
                        se.wait_ge(p.sem("qmu"), need)
                    p.inc(se.copy(qsb[:, tb * 512:(tb + 1) * 512], ps[tb][:, 0:512]), "qev", 1)
                if bq["qss"] + m * 4 > 0:
                    se.wait_ge(p.sem("qss"), bq["qss"] + m * 4)
                p.inc(se.activation(qsq[:], qsb[:], AF.Square), "qsq", 1)
                for tb in range(4):
                    se.wait_ge(p.sem("qss"), bq["qss"] + m * 4 + tb + 1)
                    p.inc(se.activation(rowb[0:1, tb * 512:(tb + 1) * 512], ps[4 + tb][0:1, 0:512],
                                        AF.Sqrt, bias=g["eps"][0:1, 0:1], scale=1.0 / 128.0), "qsr", 1)
                for tb in range(4):
                    se.wait_ge(p.sem("qbc"), bq["qbc"] + m * 4 + tb + 1)
                    p.inc(se.copy(scl[:, tb * 512:(tb + 1) * 512], ps[tb][:, 0:512]), "qsc", 1)

        @blk.vector
        def _(v):
            v = VD(v)
            for m in range(16):
                for tb in range(4):
                    v.wait_ge(p.sem("qsr"), bq["qsr"] + m * 4 + tb + 1)
                    need = bq["qbc"] + (m - 1) * 4 + tb + 1
                    if need > 0:
                        v.wait_ge(p.sem("qbc"), need)
                    p.inc(v.reciprocal(rowb2[0:1, tb * 512:(tb + 1) * 512],
                                       rowb[0:1, tb * 512:(tb + 1) * 512]), "qrc", 1)
                for tb in range(4):
                    v.wait_ge(p.sem("qsc"), bq["qsc"] + m * 4 + tb + 1)
                    if m >= 2:
                        v.wait_ge(p.sem(QN[m - 2][0]), QN[m - 2][1])
                    else:
                        v.wait_ge(p.sem(QN[m][0]), QN[m][1] - 16)
                    p.inc(v.tensor_tensor(onb[m % 2][:, tb * 512:(tb + 1) * 512],
                                          qsb[:, tb * 512:(tb + 1) * 512],
                                          scl[:, tb * 512:(tb + 1) * 512], ALU.mult), "qmu", 1)

        @blk.gpsimd
        def _(g_):
            for m in range(16):
                g_.wait_ge(p.sem("qmu"), bq["qmu"] + (m + 1) * 4)
                tgt = p.inc(g_.dma_start(out=dst_ap[m * 128:(m + 1) * 128, :], in_=onb[m % 2][:]),
                            QN[m][0], 16)
                assert tgt == QN[m][1]
            for i in range(2):
                g_.wait_ge(p.sem(QN[15 - i][0]), QN[15 - i][1])

    emit_qk(0, "wqt_bf", io["qn_d"].rearrange("h d t -> (h d) t"), grow[0:1, 0:128])
    emit_qk(1, "wkt_bf", io["kn_d"], grow[0:1, 128:256])

    b_wtl = p.cnt_of("wtl")
    bq = {n: p.cnt_of(n) for n in ["qmm", "qev"]}
    VST = p.plan("vst", 2, 64, unit=64)

    @blk.scalar
    def _(se):
        se = VD(se)
        emit_wT_load(se, "wvt_bf")

    @blk.tensor
    def _(t):
        t.wait_ge(p.sem("wtl"), b_wtl + 256)
        t.wait_ge(p.sem("qsc"), p.cnt_of("qsc"))
        for tt in range(16):
            for mb in range(4):
                e = tt * 4 + mb
                if e >= 2:
                    t.wait_ge(p.sem("qev"), bq["qev"] + e - 1)
                for hf in range(2):
                    c0 = mb * 512 + hf * 256
                    for dc in range(16):
                        mm = t.matmul(ps[e % 2][:, hf * 256:hf * 256 + 256],
                                      xlt[:, dc * TLOC + tt * 128:dc * TLOC + (tt + 1) * 128],
                                      wt[:, dc * TLOC + c0:dc * TLOC + c0 + 256],
                                      start=(dc == 0), stop=(dc == 15))
                p.inc(mm, "qmm", 1)

    @blk.vector
    def _(v):
        v = VD(v)
        for tt in range(16):
            for mb in range(4):
                e = tt * 4 + mb
                v.wait_ge(p.sem("qmm"), bq["qmm"] + e + 1)
                if e >= 2:
                    v.wait_ge(p.sem(VST[e - 2][0]), VST[e - 2][1])
                p.inc(v.tensor_copy(vsb[e % 2][:], ps[e % 2][:, 0:512]), "qev", 1)

    @blk.gpsimd
    def _(g_):
        vd_r = io["v_d"].rearrange("(h t) e -> h t e", t=TLOC)
        for tt in range(16):
            for mb in range(4):
                e = tt * 4 + mb
                g_.wait_ge(p.sem("qev"), bq["qev"] + e + 1)
                for hh in range(4):
                    h = mb * 4 + hh
                    tgt = p.inc(g_.dma_start(out=vd_r[h, tt * 128:(tt + 1) * 128, :],
                                             in_=vsb[e % 2][:, hh * 128:(hh + 1) * 128]), VST[e][0], 16)
                assert tgt == VST[e][1]
        for i in range(2):
            g_.wait_ge(p.sem(VST[63 - i][0]), VST[63 - i][1])

    ar.free(xlt, wt, qsb, qsq, scl, rowb, rowb2, grow, *onb, *vsb)

    if not do_ag:
        return
    p.new_block()
    blk = p.blk

    @blk.gpsimd
    def _(g_):
        pairs = [[0, 1], [2, 3], [4, 5], [6, 7]]
        p.inc(g_.collective_compute("AllGather", ALU.bypass, replica_groups=pairs,
                                    ins=[io["kn_d"]], outs=[io["kn_ag"]]), "cc", 1)
        p.inc(g_.collective_compute("AllGather", ALU.bypass, replica_groups=pairs,
                                    ins=[io["v_d"]], outs=[io["v_ag"]]), "cc", 1)
        p.wait(g_, "cc")


def ph_l0attn(p, io, g, tag):
    p.phase_boundary()
    nc = p.nc
    ar = p.ar
    ps = g["ps"]
    knh = [ar.alloc(f"{tag}_knh{i}", [128, 4096], BF16) for i in range(2)]
    vh = [ar.alloc(f"{tag}_vh{i}", [128, 4096], BF16) for i in range(2)]
    qh = [ar.alloc(f"{tag}_qh{i}", [128, TLOC], BF16) for i in range(2)]
    sco = [ar.alloc(f"{tag}_sco{i}", [128, 8192], F32) for i in range(2)]
    pT = [ar.alloc(f"{tag}_pT{i}", [128, 8192], BF16) for i in range(2)]
    mskb = ar.alloc(f"{tag}_mskb", [128, 32 * 256], BF16)
    bmall = [ar.alloc(f"{tag}_bm{i}", [128, 32], F32) for i in range(2)]
    mu1 = [ar.alloc(f"{tag}_mu1{i}", [128, 1], F32) for i in range(2)]
    negr = ar.alloc(f"{tag}_negr", [128, 32], F32)
    smal = ar.alloc(f"{tag}_smal", [1, 8], F32)
    rzrow = ar.alloc(f"{tag}_rzr", [1, 512], F32)
    rzb = [ar.alloc(f"{tag}_rzb{i}", [128, 256], F32) for i in range(2)]
    nmb = [ar.alloc(f"{tag}_nmb{i}", [128, 1], F32) for i in range(2)]
    aout = [ar.alloc(f"{tag}_ao{i}", [128, TLOC], BF16) for i in range(2)]
    kn_r = io["kn_ag"].rearrange("(x h d) t -> x h d t", h=NH, d=128)
    v_r = io["v_ag"].rearrange("(x h t) e -> x h t e", h=NH, t=TLOC)
    qn_r = io["qn_d"]
    ao_r = io["aout_d"].rearrange("(h d) t -> h d t", d=128)
    NB = sum(EXT)
    NSL = NH * NCHUNK
    exts = [EXT[s % 8] for s in range(NSL)]
    CB = [0]
    for e in exts:
        CB.append(CB[-1] + e)

    p.new_block()
    blk = p.blk
    SEMS = ["amm", "aev", "aex", "apv", "ax1", "ax2", "ax3", "ax4",
            "azm", "azr", "arm", "arx", "anm", "amk", "amkv", "amsk"]
    b = {n: p.cnt_of(n) for n in SEMS}
    AHL = p.plan("ahl", 2, NH, unit=49 * 16, q="s")
    AST = p.plan("ast", 2, NH, unit=16)

    @blk.sync
    def _(s):
        for h in range(NH):
            if h >= 2:
                s.wait_ge(p.sem("anm"), b["anm"] + (h - 1) * 8)
            for cabs in range(16):
                x, l = ABS2LOC[cabs]
                p.inc(s.dma_start(out=knh[h % 2][:, cabs * 256:(cabs + 1) * 256],
                                  in_=kn_r[x, h, :, l * 256:(l + 1) * 256]), AHL[h][0], 16)
                for a in range(2):
                    p.inc(s.dma_start(
                        out=vh[h % 2][:, (2 * cabs + a) * 128:(2 * cabs + a + 1) * 128],
                        in_=v_r[x, h, l * 256 + a * 128:l * 256 + (a + 1) * 128, :]),
                        AHL[h][0], 16)
            tgt = p.inc(s.dma_start(out=qh[h % 2][:], in_=qn_r[h]), AHL[h][0], 16)
            assert tgt == AHL[h][1]

    @blk.gpsimd
    def _(g_):
        p.inc(g_.dma_start(out=negr[:, :], in_=io["maskr"][:, :]), "amk", 16)

    @blk.vector
    def _(v):
        v = VD(v)
        v.wait_ge(p.sem("amk"), b["amk"] + 16)
        p.inc(v.tensor_scalar(negr[:], negr[:], -1.0, None, ALU.mult), "amkv", 1)
        for mi in range(32):
            mm = v.tensor_scalar(mskb[:, mi * 256:(mi + 1) * 256], g["Dt"][:],
                                 negr[:, mi:mi + 1], NEG, ALU.is_gt, ALU.mult)
        p.inc(mm, "amsk", 1)

    @blk.tensor
    def _(t):
        p.wait(t, "amkv")
        t.wait_ge(p.sem("amsk"), b["amsk"] + 1)

        def pe_early(j):
            pr = j % 2
            t.wait_ge(p.sem("ax1"), b["ax1"] + j + 1)
            if j >= 1:
                t.wait_ge(p.sem("ax3"), b["ax3"] + j)
            p.inc(t.transpose(ps[5][0:1, pr * 128:pr * 128 + 128], mu1[pr][:, 0:1],
                              g["ident"][:]), "ax2", 1)

        def pe_bcast(j):
            pr = j % 2
            t.wait_ge(p.sem("ax3"), b["ax3"] + j + 1)
            if j >= 1:
                t.wait_ge(p.sem("aex"), b["aex"] + CB[j])
            p.inc(t.matmul(ps[2][:, pr:pr + 1], g["onesf_r"][0:1, :],
                           smal[0:1, 2 + pr:3 + pr], start=True, stop=True), "ax4", 1)

        def pe_pvz_head(j):
            if j >= 2:
                t.wait_ge(p.sem("anm"), b["anm"] + j - 1)
            if j >= 1:
                t.wait_ge(p.sem("azr"), b["azr"] + j)

        def pe_pvz_blk(j, kb):
            pr = j % 2
            h_ = j // 8
            ext_ = exts[j]
            t.wait_ge(p.sem("aex"), b["aex"] + CB[j] + (kb // 2 + 1) * 2)
            p.inc(t.matmul(ps[3 + pr][:, 0:256], vh[h_ % 2][:, kb * 128:(kb + 1) * 128],
                           pT[pr][:, kb * 256:(kb + 1) * 256],
                           start=(kb == 0), stop=(kb == ext_ - 1)), "apv", 1)
            p.inc(t.matmul(ps[6][0:1, pr * 256:(pr + 1) * 256], g["onesb_c"][:, 0:1],
                           pT[pr][:, kb * 256:(kb + 1) * 256],
                           start=(kb == 0), stop=(kb == ext_ - 1)), "azm", 1)

        def pe_rzb(j):
            pr = j % 2
            t.wait_ge(p.sem("azr"), b["azr"] + j + 1)
            if j >= 2:
                t.wait_ge(p.sem("arx"), b["arx"] + j)
            p.inc(t.matmul(ps[7][:, pr * 256:(pr + 1) * 256], g["onesf_r"][0:1, :],
                           rzrow[0:1, pr * 256:(pr + 1) * 256], start=True, stop=True), "arm", 1)

        def pe_scores(sl, k0, k1):
            h, l = sl // 8, sl % 8
            ext_ = exts[sl]
            for kb in range(k0, k1):
                gb = CB[sl] + kb
                if gb >= 2:
                    t.wait_ge(p.sem("aev"), b["aev"] + gb - 1)
                masked = kb >= ext_ - 4
                mm = t.matmul(ps[gb % 2][:, 0:256], knh[h % 2][:, kb * 128:(kb + 1) * 128],
                              qh[h % 2][:, l * 256:(l + 1) * 256],
                              start=True, stop=not masked)
                if masked:
                    mi = l * 4 + (kb - (ext_ - 4))
                    mm = t.matmul(ps[gb % 2][:, 0:256], g["identb"][:],
                                  mskb[:, mi * 256:(mi + 1) * 256], start=False, stop=True)
                p.inc(mm, "amm", 1)

        LAG = 4
        for sl in range(NSL):
            h, l = sl // 8, sl % 8
            ext = exts[sl]
            epv = exts[sl - 1] if sl >= 1 else 0
            if sl >= 1:
                pe_early(sl - 1)
            if l == 0:
                t.wait_ge(p.sem(AHL[h][0]), AHL[h][1])
            pre = min(4, ext)
            pe_scores(sl, 0, pre)
            if sl >= 1:
                pe_bcast(sl - 1)
                pe_pvz_head(sl - 1)
            for k in range(pre, max(ext, epv + LAG)):
                if k < ext:
                    pe_scores(sl, k, k + 1)
                if 0 <= k - LAG < epv:
                    pe_pvz_blk(sl - 1, k - LAG)
            if sl >= 2:
                pe_rzb(sl - 2)
        pe_early(NSL - 1)
        pe_bcast(NSL - 1)
        pe_pvz_head(NSL - 1)
        for kb in range(exts[NSL - 1]):
            pe_pvz_blk(NSL - 1, kb)
        pe_rzb(NSL - 2)
        pe_rzb(NSL - 1)

    @blk.vector
    def _(v):
        v = VD(v)
        p.wait(v, "amkv")

        def dve_negmu(j):
            pr = j % 2
            v.wait_ge(p.sem("ax2"), b["ax2"] + j + 1)
            if j >= 2:
                v.wait_ge(p.sem("ax4"), b["ax4"] + j - 1)
            p.inc(v.tensor_reduce(smal[0:1, 2 + pr:3 + pr], ps[5][0:1, pr * 128:pr * 128 + 128],
                                  axis=AX.X, op=ALU.max, negate=True), "ax3", 1)

        def dve_rz(j):
            pr = j % 2
            v.wait_ge(p.sem("azm"), b["azm"] + CB[j + 1])
            if j >= 2:
                v.wait_ge(p.sem("arm"), b["arm"] + j - 1)
            p.inc(v.reciprocal(rzrow[0:1, pr * 256:(pr + 1) * 256],
                               ps[6][0:1, pr * 256:(pr + 1) * 256]), "azr", 1)

        def dve_aout(j):
            pr = j % 2
            h_, l_ = j // 8, j % 8
            v.wait_ge(p.sem("apv"), b["apv"] + CB[j + 1])
            v.wait_ge(p.sem("arx"), b["arx"] + j + 1)
            if l_ == 0 and h_ >= 2:
                v.wait_ge(p.sem(AST[h_ - 2][0]), AST[h_ - 2][1])
            p.inc(v.tensor_tensor(aout[h_ % 2][:, l_ * 256:(l_ + 1) * 256],
                                  ps[3 + pr][:, 0:256], rzb[pr][:], ALU.mult), "anm", 1)

        for sl in range(NSL):
            pr = sl % 2
            ext = exts[sl]
            if sl >= 1:
                dve_negmu(sl - 1)
            if sl >= 2:
                v.wait_ge(p.sem("aex"), b["aex"] + CB[sl - 1])
            for kb in range(ext):
                gb = CB[sl] + kb
                v.wait_ge(p.sem("amm"), b["amm"] + gb + 1)
                p.inc(v.tensor_scalar(sco[pr][:, kb * 256:(kb + 1) * 256],
                                      ps[gb % 2][:, 0:256], 1.0, NEG,
                                      ALU.mult, ALU.max,
                                      accum_out=bmall[pr][:, kb:kb + 1]), "aev", 1)
                if kb == 3 and sl >= 2:
                    dve_rz(sl - 2)
            if sl >= 2:
                v.wait_ge(p.sem("ax2"), b["ax2"] + sl - 1)
            p.inc(v.tensor_reduce(mu1[pr][:, 0:1], bmall[pr][:, 0:ext], axis=AX.X,
                                  op=ALU.max), "ax1", 1)
            if sl >= 2:
                dve_aout(sl - 2)
        dve_negmu(NSL - 1)
        dve_rz(NSL - 2)
        dve_rz(NSL - 1)
        dve_aout(NSL - 2)
        dve_aout(NSL - 1)

    @blk.scalar
    def _(se):
        se = VD(se)

        def act_exps(j):
            pr = j % 2
            ext_ = exts[j]
            se.wait_ge(p.sem("ax4"), b["ax4"] + j + 1)
            se.copy(nmb[pr][:], ps[2][:, pr:pr + 1])
            if j >= 2:
                se.wait_ge(p.sem("apv"), b["apv"] + CB[j - 1])
                se.wait_ge(p.sem("azm"), b["azm"] + CB[j - 1])
            for kb in range(0, ext_, 2):
                gb = CB[j] + kb
                se.wait_ge(p.sem("aev"), b["aev"] + gb + 2)
                p.inc(se.activation(pT[pr][:, kb * 256:(kb + 2) * 256],
                                    sco[pr][:, kb * 256:(kb + 2) * 256],
                                    AF.Exp, bias=nmb[pr][:, 0:1], scale=1.0), "aex", 2)

        def act_rzb(j):
            pr = j % 2
            se.wait_ge(p.sem("arm"), b["arm"] + j + 1)
            if j >= 2:
                se.wait_ge(p.sem("anm"), b["anm"] + j - 1)
            p.inc(se.copy(rzb[pr][:], ps[7][:, pr * 256:(pr + 1) * 256]), "arx", 1)

        for sl in range(NSL):
            if sl >= 1:
                act_exps(sl - 1)
            if sl >= 2:
                act_rzb(sl - 2)
        act_exps(NSL - 1)
        act_rzb(NSL - 2)
        act_rzb(NSL - 1)

    @blk.gpsimd
    def _(g_):
        for h in range(NH):
            g_.wait_ge(p.sem("anm"), b["anm"] + (h + 1) * 8)
            tgt = p.inc(g_.dma_start(out=ao_r[h], in_=aout[h % 2][:]), AST[h][0], 16)
            assert tgt == AST[h][1]
        for i in range(2):
            g_.wait_ge(p.sem(AST[NH - 1 - i][0]), AST[NH - 1 - i][1])

    ar.free(*knh, *vh, *qh, *sco, *pT, mskb, *bmall, *mu1, negr, smal, rzrow, *rzb, *nmb, *aout)


def ph_l0proj(p, io, g, x_src, x_dst, xln_dst, tag, scale=1.0, wname="wot_bf", bias=None):
    p.phase_boundary()
    nc = p.nc
    ar = p.ar
    ps = g["ps"]
    x_ap, xo_ap = io[x_src], io[x_dst]
    asb = ar.alloc(f"{tag}_asb", [128, 16 * TLOC], BF16)
    wot = ar.alloc(f"{tag}_wot", [128, 16 * TLOC], BF16)
    statsE = ar.alloc(f"{tag}_stE", [128, 16 * 24], F32)
    xo_sb = [ar.alloc(f"{tag}_xo{i}", [128, 512], F32) for i in range(3)]
    xn_sb = [ar.alloc(f"{tag}_xn{i}", [128, 512], F32) for i in range(2)]
    p.new_block()
    blk = p.blk
    b_emm, b_ef, b_xn = p.base("emm", "ef", "xn")
    EL = p.plan("el", 3, 64, q="s")
    ES = p.plan("es", 2, 64)
    b_aol = p.cnt_of("aol")

    @blk.sync
    def _(s):
        for cc in range(16):
            p.inc(s.dma_start(out=asb[:, cc * TLOC:(cc + 1) * TLOC],
                              in_=io["aout_d"][cc * 128:(cc + 1) * 128, :]), "aol", 16)
        for m in range(4):
            for t in range(16):
                e = m * 16 + t
                if e >= 3:
                    s.wait_ge(p.sem("xn"), b_xn + e - 2)
                p.inc(s.dma_start(out=xo_sb[e % 3][:],
                                  in_=x_ap[t * 128:(t + 1) * 128, m * 512:(m + 1) * 512]), EL[e][0], 16)

    brow = ar.alloc(f"{tag}_br", [1, D], F32) if bias else None
    if bias:
        @blk.gpsimd
        def _(g_):
            p.inc(g_.dma_start(out=brow[0:1, :], in_=io[bias][None, :]), "pbl", 16)

    @blk.scalar
    def _(se):
        se = VD(se)
        emit_xlnT_load(p, se, io[wname], wot, "wtl")

    @blk.tensor
    def _(t_):
        t_.wait_ge(p.sem("aol"), b_aol + 256)
        p.wait(t_, "wtl")
        if bias:
            p.wait(t_, "pbl")
        for m in range(4):
            for t in range(16):
                e = m * 16 + t
                if e >= 2:
                    t_.wait_ge(p.sem("ef"), b_ef + e - 1)
                bank = ps[4 + e % 2]
                for hf in range(2):
                    c0 = m * 512 + hf * 256
                    for cc in range(16):
                        mm = t_.matmul(bank[:, hf * 256:hf * 256 + 256],
                                       asb[:, cc * TLOC + t * 128:cc * TLOC + (t + 1) * 128],
                                       wot[:, cc * TLOC + c0:cc * TLOC + c0 + 256],
                                       start=(cc == 0), stop=(cc == 15 and not bias))
                    if bias:
                        mm = t_.matmul(bank[:, hf * 256:hf * 256 + 256], g["onesf_r"][0:1, 0:128],
                                       brow[0:1, c0:c0 + 256], start=False, stop=True)
                p.inc(mm, "emm", 1)

    @blk.vector
    def _(v):
        v = VD(v)
        for m in range(4):
            for t in range(16):
                e = m * 16 + t
                v.wait_ge(p.sem("emm"), b_emm + e + 1)
                v.wait_ge(p.sem(EL[e][0]), EL[e][1])
                if e >= 2:
                    v.wait_ge(p.sem(ES[e - 2][0]), ES[e - 2][1])
                p.inc(v.scalar_tensor_tensor(xn_sb[e % 2][:], ps[4 + e % 2][:, 0:512], scale,
                                             xo_sb[e % 3][:], op0=ALU.mult, op1=ALU.add), "ef", 1)
                p.inc(v.bn_stats(statsE[:, t * 24 + m * 6:t * 24 + (m + 1) * 6],
                                 xn_sb[e % 2][:]), "xn", 1)

    @blk.gpsimd
    def _(g_):
        for m in range(4):
            for t in range(16):
                e = m * 16 + t
                g_.wait_ge(p.sem("xn"), b_xn + e + 1)
                p.inc(g_.dma_start(out=xo_ap[t * 128:(t + 1) * 128, m * 512:(m + 1) * 512],
                                   in_=xn_sb[e % 2][:]), ES[e][0], 16)
        for i in range(2):
            g_.wait_ge(p.sem(ES[63 - i][0]), ES[63 - i][1])

    ar.free(asb, wot, *xo_sb, *xn_sb)
    if bias:
        ar.free(brow)
    emit_lnpass(p, io, g, x_dst, xln_dst, statsE, tag)
    ar.free(statsE)


def _chunk3(w):
    a = (w // 3) // 16 * 16
    b = a
    return [(0, a), (a, a + b), (a + b, w)]


def ph_ghost(p, io, g, tag="gh", do_ag=True, wait_cc=False):
    p.phase_boundary()
    nc = p.nc
    ar = p.ar
    ps = g["ps"]
    lutb = ar.alloc(f"{tag}_lutb", [128, 16], F32)
    lrow = ar.alloc(f"{tag}_lrow", [1, 16], F32)
    idxb = [ar.alloc(f"{tag}_idx{i}", [128, 5504], BF16) for i in range(2)]
    eqt = ar.alloc(f"{tag}_eq", [128, 5504], BF16)
    acc = ar.alloc(f"{tag}_acc", [128, 5504], BF16)
    wob = [ar.alloc(f"{tag}_wo{i}", [128, 5504], BF16) for i in range(2)]
    p.new_block()
    blk = p.blk
    b_gc, b_gw, b_gcd = p.base("ggc", "ggw", "gcd")
    tiles = []
    for srcn, onm in [("gidxb_d", "gwT_sl"), ("uidxb_d", "uwT_sl")]:
        for t in range(2):
            tiles.append((srcn, t, 5504, io[onm][t * 128:(t + 1) * 128, :], 128))
    for t in range(6):
        r = min(128, 688 - t * 128)
        tiles.append(("didxb_d", t, 2048, io["dwT_sl"][t * 128:t * 128 + r, :], r))
    NT = len(tiles)
    IL = p.plan("gil", 2, NT, unit=48, q="s")
    GS = p.plan("gst", 2, NT)
    grp = [list(range(8))]

    @blk.sync
    def _(s):
        p.inc(s.dma_start(out=lrow[0:1, :], in_=io["L1_ffn_lut"][None, :]), "ggl", 16)

    @blk.tensor
    def _(t):
        p.wait(t, "ggl")
        p.inc(t.matmul(ps[7][:, 0:16], g["onesf_r"][0:1, :], lrow[0:1, :],
                       start=True, stop=True), "glb", 1)

    @blk.scalar
    def _(se):
        se = VD(se)
        p.wait(se, "glb")
        p.inc(se.copy(lutb[:], ps[7][:, 0:16]), "glc", 1)
        se.wait_ge(p.sem("gcd"), b_gcd + 48)
        for i, (srcn, t, w, osl, r) in enumerate(tiles):
            if i >= 2:
                se.wait_ge(p.sem("ggc"), b_gc + i - 1)
            for c0, c1 in _chunk3(w):
                tgt = p.inc(se.dma_start(out=idxb[i % 2][:, c0:c1],
                                         in_=io[srcn][c0:c1, t * 128:(t + 1) * 128],
                                         transpose=True), IL[i][0], 16)
            assert tgt == IL[i][1]

    @blk.vector
    def _(v):
        v = VD(v)
        p.wait(v, "glc")
        for i, (srcn, t, w, osl, r) in enumerate(tiles):
            v.wait_ge(p.sem(IL[i][0]), IL[i][1])
            buf = idxb[i % 2]
            v.tensor_scalar(acc[:, 0:w], buf[:, 0:w], 0.0, lutb[:, 0:1],
                            ALU.is_equal, ALU.mult)
            for j in range(1, 15):
                v.tensor_scalar(eqt[:, 0:w], buf[:, 0:w], float(j), lutb[:, j:j + 1],
                                ALU.is_equal, ALU.mult)
                v.tensor_tensor(acc[:, 0:w], acc[:, 0:w], eqt[:, 0:w], ALU.add)
            p.inc(v.tensor_scalar(eqt[:, 0:w], buf[:, 0:w], 15.0, lutb[:, 15:16],
                                  ALU.is_equal, ALU.mult), "ggc", 1)
            if i >= 2:
                v.wait_ge(p.sem(GS[i - 2][0]), GS[i - 2][1])
            p.inc(v.tensor_tensor(wob[i % 2][0:r, 0:w], acc[0:r, 0:w], eqt[0:r, 0:w],
                                  ALU.add), "ggw", 1)

    @blk.gpsimd
    def _(g_):
        p.inc(g_.dma_start(out=io["gidxb_d"][:, :], in_=io["gate_idx_sl"][:, :]), "gcd", 16)
        p.inc(g_.dma_start(out=io["uidxb_d"][:, :], in_=io["up_idx_sl"][:, :]), "gcd", 16)
        p.inc(g_.dma_start(out=io["didxb_d"][:, :], in_=io["down_idx_sl"][:, :]), "gcd", 16)
        for i, (srcn, t, w, osl, r) in enumerate(tiles):
            g_.wait_ge(p.sem("ggw"), b_gw + i + 1)
            tgt = p.inc(g_.dma_start(out=osl, in_=wob[i % 2][0:r, 0:w]), GS[i][0], 16)
            assert tgt == GS[i][1]
            if do_ag and i in (1, 3, NT - 1):
                g_.wait_ge(p.sem(GS[i - 1][0]), GS[i - 1][1])
                g_.wait_ge(p.sem(GS[i][0]), GS[i][1])
                if i == 3:
                    g_.wait_ge(p.sem(GS[2][0]), GS[2][1])
                if i == NT - 1:
                    for ii in range(4, NT - 1):
                        g_.wait_ge(p.sem(GS[ii][0]), GS[ii][1])
                nm = {1: ("gwT_sl", "gwT_d"), 3: ("uwT_sl", "uwT_d"),
                      NT - 1: ("dwT_sl", "dwT_d")}[i]
                p.inc(g_.collective_compute("AllGather", ALU.bypass, replica_groups=grp,
                                            ins=[io[nm[0]]], outs=[io[nm[1]]]), "cc", 1)
        for i in range(2):
            g_.wait_ge(p.sem(GS[NT - 1 - i][0]), GS[NT - 1 - i][1])
        if wait_cc:
            p.wait(g_, "cc")

    ar.free(lutb, lrow, *idxb, eqt, acc, *wob)


def ph_l1ffn(p, io, g, xln_src, x_src, x_dst, xln_dst, tag, cast_pairs=None):
    p.phase_boundary()
    nc = p.nc
    ar = p.ar
    ps = g["ps"]
    xln_ap, x_ap, xo_ap = io[xln_src], io[x_src], io[x_dst]
    OC = 43
    xltb = ar.alloc(f"{tag}_xlt", [128, 16 * 1024], BF16)
    hgh = ar.alloc(f"{tag}_h", [128, OC * 1024], BF16)
    gwc = [ar.alloc(f"{tag}_gw{i}", [128, 16 * 128], BF16) for i in range(2)]
    uwc = [ar.alloc(f"{tag}_uw{i}", [128, 16 * 128], BF16) for i in range(2)]
    dwm = [ar.alloc(f"{tag}_dw0", [128, OC * 256], BF16)]
    slb = [ar.alloc(f"{tag}_sl{i}", [128, 512], BF16) for i in range(2)]
    statsE = ar.alloc(f"{tag}_stE", [128, 16 * 48], F32)
    xo_sb = [ar.alloc(f"{tag}_xo{i}", [128, 256], F32) for i in range(3)]
    xn_sb = [ar.alloc(f"{tag}_xn{i}", [128, 256], F32) for i in range(2)]
    p.new_block()
    blk = p.blk
    b_gmm, b_umm, b_sl, b_bu, b_emm, b_ef, b_xn, b_xtl = p.base(
        "gmm", "umm", "sl", "bu", "emm", "ef", "xn", "xtl")
    WL = p.plan("wl", 2, 2 * OC, unit=32)
    DL = p.plan("dl", 1, 16)
    EL = p.plan("el", 3, 128, q="s")
    ES = p.plan("es", 2, 128)

    @blk.scalar
    def _(se):
        se = VD(se)
        for tb in range(2):
            if tb == 1:
                se.wait_ge(p.sem("gmm"), b_gmm + 2 * OC)
                se.wait_ge(p.sem("umm"), b_umm + 2 * OC)
            for hc in range(16):
                p.inc(se.dma_start(out=xltb[:, hc * 1024:(hc + 1) * 1024],
                                   in_=xln_ap[tb * 1024:(tb + 1) * 1024, hc * 128:(hc + 1) * 128],
                                   transpose=True), "xtl", 16)
            for oc in range(OC):
                i = tb * OC + oc
                for half in range(2):
                    jj = i * 2 + half
                    se.wait_ge(p.sem("gmm"), b_gmm + jj + 1)
                    if jj >= 2:
                        se.wait_ge(p.sem("bu"), b_bu + jj - 1)
                    p.inc(se.activation(slb[jj % 2][:], ps[half][:, 0:512], AF.Silu), "sl", 1)

    @blk.gpsimd
    def _(g_):
        if cast_pairs:
            for srcn, dstn in cast_pairs:
                p.inc(g_.dma_start(out=io[dstn][:, :], in_=io[srcn][:, :]), "cw", 16)
        g_.wait_ge(p.sem("cc"), p.cnt_of("cc"))
        for tb in range(2):
            for oc in range(OC):
                i = tb * OC + oc
                if i >= 2:
                    g_.wait_ge(p.sem("bu"), b_bu + 2 * (i - 1))
                p.inc(g_.dma_start(out=gwc[i % 2][:, :].rearrange("p (c f) -> p c f", c=16),
                                   in_=io["gwT_d"][:, oc * 128:(oc + 1) * 128].rearrange(
                                       "(c p) f -> p c f", p=128)), WL[i][0], 16)
                tgt = p.inc(g_.dma_start(out=uwc[i % 2][:, :].rearrange("p (c f) -> p c f", c=16),
                                         in_=io["uwT_d"][:, oc * 128:(oc + 1) * 128].rearrange(
                                             "(c p) f -> p c f", p=128)), WL[i][0], 16)
                assert tgt == WL[i][1]
            for m in range(8):
                dmi = tb * 8 + m
                if dmi >= 1:
                    g_.wait_ge(p.sem("emm"), b_emm + dmi * 8)
                tgt = p.inc(g_.dma_start(out=dwm[0][:, :].rearrange("p (c f) -> p c f", c=OC),
                                         in_=io["dwT_d"][:, m * 256:(m + 1) * 256].rearrange(
                                             "(c p) f -> p c f", p=128)), DL[dmi][0], 16)
                assert tgt == DL[dmi][1]
                for t2 in range(8):
                    e = tb * 64 + m * 8 + t2
                    g_.wait_ge(p.sem("xn"), b_xn + e + 1)
                    tgt = p.inc(g_.dma_start(
                        out=xo_ap[tb * 1024 + t2 * 128:tb * 1024 + (t2 + 1) * 128,
                                  m * 256:(m + 1) * 256],
                        in_=xn_sb[e % 2][:]), ES[e][0], 16)
                    assert tgt == ES[e][1]
        for i in range(2):
            g_.wait_ge(p.sem(ES[127 - i][0]), ES[127 - i][1])

    @blk.sync
    def _(s):
        for tb in range(2):
            for m in range(8):
                for t2 in range(8):
                    e = tb * 64 + m * 8 + t2
                    if e >= 3:
                        s.wait_ge(p.sem("xn"), b_xn + e - 2)
                    p.inc(s.dma_start(out=xo_sb[e % 3][:],
                                      in_=x_ap[tb * 1024 + t2 * 128:tb * 1024 + (t2 + 1) * 128,
                                               m * 256:(m + 1) * 256]), EL[e][0], 16)

    @blk.tensor
    def _(t):
        for tb in range(2):
            t.wait_ge(p.sem("xtl"), b_xtl + (tb + 1) * 256)
            for oc in range(OC):
                i = tb * OC + oc
                t.wait_ge(p.sem(WL[tb * OC + oc][0]), WL[tb * OC + oc][1])
                for half in range(2):
                    jj = i * 2 + half
                    if jj >= 2:
                        t.wait_ge(p.sem("sl"), b_sl + jj - 1)
                        t.wait_ge(p.sem("bu"), b_bu + jj - 1)
                    for hf in range(2):
                        c0 = half * 512 + hf * 256
                        for hc in range(16):
                            mm = t.matmul(ps[half][:, hf * 256:hf * 256 + 256],
                                          gwc[i % 2][:, hc * 128:(hc + 1) * 128],
                                          xltb[:, hc * 1024 + c0:hc * 1024 + c0 + 256],
                                          start=(hc == 0), stop=(hc == 15))
                    p.inc(mm, "gmm", 1)
                    for hf in range(2):
                        c0 = half * 512 + hf * 256
                        for hc in range(16):
                            mm = t.matmul(ps[2 + half][:, hf * 256:hf * 256 + 256],
                                          uwc[i % 2][:, hc * 128:(hc + 1) * 128],
                                          xltb[:, hc * 1024 + c0:hc * 1024 + c0 + 256],
                                          start=(hc == 0), stop=(hc == 15))
                    p.inc(mm, "umm", 1)
            t.wait_ge(p.sem("bu"), b_bu + (tb + 1) * 2 * OC)
            for m in range(8):
                dmi = tb * 8 + m
                t.wait_ge(p.sem(DL[dmi][0]), DL[dmi][1])
                for t2 in range(8):
                    e = tb * 64 + m * 8 + t2
                    if e >= 2:
                        t.wait_ge(p.sem("ef"), b_ef + e - 1)
                    bank = ps[4 + e % 2]
                    for oc in range(OC):
                        mm = t.matmul(bank[:, 0:256],
                                      hgh[:, oc * 1024 + t2 * 128:oc * 1024 + (t2 + 1) * 128],
                                      dwm[0][:, oc * 256:(oc + 1) * 256],
                                      start=(oc == 0), stop=(oc == OC - 1))
                    p.inc(mm, "emm", 1)

    @blk.vector
    def _(v):
        v = VD(v)
        for tb in range(2):
            for oc in range(OC):
                i = tb * OC + oc
                for half in range(2):
                    jj = i * 2 + half
                    v.wait_ge(p.sem("sl"), b_sl + jj + 1)
                    v.wait_ge(p.sem("umm"), b_umm + jj + 1)
                    if tb == 1:
                        v.wait_ge(p.sem("emm"), b_emm + 32)
                    p.inc(v.tensor_tensor(hgh[:, oc * 1024 + half * 512:oc * 1024 + (half + 1) * 512],
                                          slb[jj % 2][:], ps[2 + half][:, 0:512], ALU.mult), "bu", 1)
            for m in range(8):
                for t2 in range(8):
                    e = tb * 64 + m * 8 + t2
                    v.wait_ge(p.sem("emm"), b_emm + e + 1)
                    v.wait_ge(p.sem(EL[e][0]), EL[e][1])
                    if e >= 2:
                        v.wait_ge(p.sem(ES[e - 2][0]), ES[e - 2][1])
                    p.inc(v.scalar_tensor_tensor(xn_sb[e % 2][:], ps[4 + e % 2][:, 0:256], 0.5,
                                                 xo_sb[e % 3][:], op0=ALU.mult, op1=ALU.add), "ef", 1)
                    tg = tb * 8 + t2
                    p.inc(v.bn_stats(statsE[:, tg * 48 + m * 6:tg * 48 + (m + 1) * 6],
                                     xn_sb[e % 2][:]), "xn", 1)

    ar.free(xltb, hgh, *gwc, *uwc, *dwm, *slb, *xo_sb, *xn_sb)
    if xln_dst is not None:
        emit_lnpass(p, io, g, x_dst, xln_dst, statsE, tag, tps=48)
    ar.free(statsE)


def ph_l1qkv(p, io, g, xln_src, tag, do_ag=True):
    p.phase_boundary()
    nc = p.nc
    ar = p.ar
    ps = g["ps"]
    xln_ap = io[xln_src]
    xlt = ar.alloc(f"{tag}_xlt", [128, 16 * TLOC], BF16)
    wt = ar.alloc(f"{tag}_wt", [128, 16 * TLOC], BF16)
    qpc = [ar.alloc(f"{tag}_qpc{i}", [128, TLOC], BF16) for i in range(2)]
    brow = ar.alloc(f"{tag}_brow", [1, 3 * D], BF16)
    t1 = ar.alloc(f"{tag}_t1", [128, 512], F32)
    t2 = ar.alloc(f"{tag}_t2", [128, 512], F32)
    kpc = ar.alloc(f"{tag}_kpc", [128, TLOC], BF16)
    vc = ar.alloc(f"{tag}_vc", [128, TLOC], BF16)
    kvsb = ar.alloc(f"{tag}_kvsb", [128, 16 * 132], F32)
    ksr = ar.alloc(f"{tag}_ksr", [1, D], F32)
    g["l1_kvsb"] = kvsb

    p.new_block()
    blk = p.blk
    b = {n: p.cnt_of(n) for n in ["qmm", "qev", "qex", "qp2", "qpw", "bld", "wtl", "xlt"]}
    QS = p.plan("qps", 2, 16, unit=16)

    @blk.gpsimd
    def _(g_):
        p.inc(g_.dma_start(out=brow[0:1, 2 * D:3 * D], in_=io["L1_attn_q_b"][None, :]), "bld", 16)
        p.inc(g_.dma_start(out=brow[0:1, 0:D], in_=io["L1_attn_k_b"][None, :]), "bld", 16)
        p.inc(g_.dma_start(out=brow[0:1, D:2 * D], in_=io["L1_attn_v_b"][None, :]), "bld", 16)

    @blk.gpsimd
    def _(g_):
        g_.memset(kvsb[:], 0.0)
        g_.drain()
        p.inc(g_.nop(), "kvz", 1)

    @blk.scalar
    def _(se):
        se = VD(se)
        emit_xlnT_load(p, se, xln_ap, xlt, "xlt")
        for dc in range(16):
            if dc == 0:
                p.wait(se, "cw")
            p.inc(se.dma_start(out=wt[:, dc * TLOC:(dc + 1) * TLOC],
                               in_=io["qw1_bf"][:, dc * 128:(dc + 1) * 128],
                               transpose=True), "wtl", 16)

    @blk.tensor
    def _(t):
        t.wait_ge(p.sem("xlt"), b["xlt"] + 256)
        t.wait_ge(p.sem("wtl"), b["wtl"] + 256)
        p.wait(t, "bld")
        for m in range(16):
            for tb in range(4):
                e = m * 4 + tb
                if e >= 2:
                    t.wait_ge(p.sem("qp2"), b["qp2"] + e - 1)
                for hf in range(2):
                    c0 = tb * 512 + hf * 256
                    for dc in range(16):
                        t.matmul(ps[e % 2][:, hf * 256:hf * 256 + 256],
                                 wt[:, dc * TLOC + m * 128:dc * TLOC + (m + 1) * 128],
                                 xlt[:, dc * TLOC + c0:dc * TLOC + c0 + 256],
                                 start=(dc == 0), stop=False)
                    mm = t.matmul(ps[e % 2][:, hf * 256:hf * 256 + 256],
                                  brow[0:1, 2 * D + m * 128:2 * D + (m + 1) * 128],
                                  g["onesb_512"][0:1, 0:256], start=False, stop=True)
                p.inc(mm, "qmm", 1)

    @blk.vector
    def _(v):
        v = VD(v)
        for m in range(16):
            for tb in range(4):
                e = m * 4 + tb
                v.wait_ge(p.sem("qmm"), b["qmm"] + e + 1)
                p.inc(v.tensor_scalar(t1[:], ps[e % 2][:, 0:512], 0.0, None, ALU.min), "qev", 1)
                v.wait_ge(p.sem("qex"), b["qex"] + e + 1)
                p.inc(v.tensor_scalar(t1[:], ps[e % 2][:, 0:512], 0.0, None, ALU.max), "qp2", 1)
                if tb == 0 and m >= 2:
                    v.wait_ge(p.sem(QS[m - 2][0]), QS[m - 2][1])
                qw = v.tensor_tensor(qpc[m % 2][:, tb * 512:(tb + 1) * 512],
                                     t1[:], t2[:], ALU.add)
            p.inc(qw, "qpw", 1)

    @blk.scalar
    def _(se):
        se = VD(se)
        for e in range(64):
            se.wait_ge(p.sem("qev"), b["qev"] + e + 1)
            p.inc(se.activation(t2[:], t1[:], AF.Exp), "qex", 1)

    @blk.gpsimd
    def _(g_):
        qp_r = io["qp_d"].rearrange("h d t -> (h d) t")
        for m in range(16):
            g_.wait_ge(p.sem("qpw"), b["qpw"] + m + 1)
            tgt = p.inc(g_.dma_start(out=qp_r[m * 128:(m + 1) * 128, :], in_=qpc[m % 2][:]), QS[m][0], 16)
            assert tgt == QS[m][1]
        for i in range(2):
            g_.wait_ge(p.sem(QS[15 - i][0]), QS[15 - i][1])

    p.new_block()
    blk = p.blk
    b = {n: p.cnt_of(n) for n in ["kmm", "kev", "kex", "kp2", "kpw", "kst", "wtl"]}
    KS = p.plan("kst2", 1, 16)

    @blk.scalar
    def _(se):
        se = VD(se)
        for dc in range(16):
            p.inc(se.dma_start(out=wt[:, dc * TLOC:(dc + 1) * TLOC],
                               in_=io["kw1_bf"][:, dc * 128:(dc + 1) * 128],
                               transpose=True), "wtl", 16)
        for e in range(64):
            se.wait_ge(p.sem("kev"), b["kev"] + e + 1)
            p.inc(se.activation(t2[:], t1[:], AF.Exp), "kex", 1)

    @blk.tensor
    def _(t):
        t.wait_ge(p.sem("wtl"), b["wtl"] + 256)
        for tt in range(16):
            for mb in range(4):
                e = tt * 4 + mb
                if e >= 2:
                    t.wait_ge(p.sem("kp2"), b["kp2"] + e - 1)
                for hf in range(2):
                    c0 = mb * 512 + hf * 256
                    for dc in range(16):
                        t.matmul(ps[e % 2][:, hf * 256:hf * 256 + 256],
                                 xlt[:, dc * TLOC + tt * 128:dc * TLOC + (tt + 1) * 128],
                                 wt[:, dc * TLOC + c0:dc * TLOC + c0 + 256],
                                 start=(dc == 0), stop=False)
                    mm = t.matmul(ps[e % 2][:, hf * 256:hf * 256 + 256], g["ones_r"][0:1, 0:128],
                                  brow[0:1, c0:c0 + 256], start=False, stop=True)
                p.inc(mm, "kmm", 1)

    @blk.vector
    def _(v):
        v = VD(v)
        for tt in range(16):
            if tt >= 1:
                v.wait_ge(p.sem(KS[tt - 1][0]), KS[tt - 1][1])
            for mb in range(4):
                e = tt * 4 + mb
                v.wait_ge(p.sem("kmm"), b["kmm"] + e + 1)
                p.inc(v.tensor_scalar(t1[:], ps[e % 2][:, 0:512], 0.0, None, ALU.min), "kev", 1)
                v.wait_ge(p.sem("kex"), b["kex"] + e + 1)
                p.inc(v.tensor_scalar(t1[:], ps[e % 2][:, 0:512], 0.0, None, ALU.max), "kp2", 1)
                kw = v.tensor_tensor(kpc[:, mb * 512:(mb + 1) * 512], t1[:], t2[:], ALU.add)
            p.inc(kw, "kpw", 1)

    @blk.gpsimd
    def _(g_):
        for tt in range(16):
            g_.wait_ge(p.sem("kpw"), b["kpw"] + tt + 1)
            tgt = p.inc(g_.dma_start(out=io["kp_d"][tt * 128:(tt + 1) * 128, :], in_=kpc[:]), KS[tt][0], 16)
            assert tgt == KS[tt][1]
        g_.wait_ge(p.sem(KS[15][0]), KS[15][1])

    p.new_block()
    blk = p.blk
    b = {n: p.cnt_of(n) for n in ["vmm", "vev", "kvm", "kva", "wtl"]}
    VL = p.plan("vld", 1, 16, q="s")

    @blk.scalar
    def _(se):
        se = VD(se)
        for dc in range(16):
            p.inc(se.dma_start(out=wt[:, dc * TLOC:(dc + 1) * TLOC],
                               in_=io["vw1_bf"][:, dc * 128:(dc + 1) * 128],
                               transpose=True), "wtl", 16)

    @blk.sync
    def _(s):
        for tt in range(16):
            if tt >= 1:
                s.wait_ge(p.sem("kvm"), b["kvm"] + tt * 16)
            tgt = p.inc(s.dma_start(out=kpc[:], in_=io["kp_d"][tt * 128:(tt + 1) * 128, :]), VL[tt][0], 16)
            assert tgt == VL[tt][1]

    @blk.tensor
    def _(t):
        t.wait_ge(p.sem("wtl"), b["wtl"] + 256)
        for tt in range(16):
            for mb in range(4):
                e = tt * 4 + mb
                if e >= 1:
                    t.wait_ge(p.sem("vev"), b["vev"] + e)
                for hf in range(2):
                    c0 = mb * 512 + hf * 256
                    for dc in range(16):
                        t.matmul(ps[e % 2][:, hf * 256:hf * 256 + 256],
                                 xlt[:, dc * TLOC + tt * 128:dc * TLOC + (tt + 1) * 128],
                                 wt[:, dc * TLOC + c0:dc * TLOC + c0 + 256],
                                 start=(dc == 0), stop=False)
                    mm = t.matmul(ps[e % 2][:, hf * 256:hf * 256 + 256], g["ones_r"][0:1, 0:128],
                                  brow[0:1, D + c0:D + c0 + 256], start=False, stop=True)
                p.inc(mm, "vmm", 1)
            t.wait_ge(p.sem(VL[tt][0]), VL[tt][1])
            t.wait_ge(p.sem("vev"), b["vev"] + (tt + 1) * 4)
            for h in range(NH):
                hh = tt * 16 + h
                if hh >= 2:
                    t.wait_ge(p.sem("kva"), b["kva"] + hh - 1)
                p.inc(t.matmul(ps[4 + h % 2][:, 0:128],
                               kpc[:, h * 128:(h + 1) * 128], vc[:, h * 128:(h + 1) * 128],
                               start=True, stop=True), "kvm", 1)

    @blk.vector
    def _(v):
        v = VD(v)
        p.wait(v, "kvz")
        for tt in range(16):
            for mb in range(4):
                e = tt * 4 + mb
                v.wait_ge(p.sem("vmm"), b["vmm"] + e + 1)
                if tt >= 1:
                    v.wait_ge(p.sem("kvm"), b["kvm"] + tt * 16)
                p.inc(v.tensor_copy(vc[:, mb * 512:(mb + 1) * 512], ps[e % 2][:, 0:512]), "vev", 1)
            for h in range(NH):
                hh = tt * 16 + h
                v.wait_ge(p.sem("kvm"), b["kvm"] + hh + 1)
                p.inc(v.tensor_tensor(kvsb[:, h * 132:h * 132 + 128],
                                      kvsb[:, h * 132:h * 132 + 128],
                                      ps[4 + h % 2][:, 0:128], ALU.add), "kva", 1)

    p.new_block()
    blk = p.blk
    b = {n: p.cnt_of(n) for n in ["ksm", "kse", "kts", "ktc", "kps", "kva", "cc"]}
    KL = p.plan("ksl", 2, 16, q="s")

    @blk.sync
    def _(s):
        for tt in range(16):
            if tt >= 2:
                s.wait_ge(p.sem("ksm"), b["ksm"] + (tt - 1) * 4)
            buf = kpc if tt % 2 == 0 else vc
            tgt = p.inc(s.dma_start(out=buf[:], in_=io["kp_d"][tt * 128:(tt + 1) * 128, :]), KL[tt][0], 16)
            assert tgt == KL[tt][1]

    @blk.tensor
    def _(t):
        for tt in range(16):
            t.wait_ge(p.sem(KL[tt][0]), KL[tt][1])
            buf = kpc if tt % 2 == 0 else vc
            for nb in range(4):
                p.inc(t.matmul(ps[nb][0:1, 0:512], g["onesb_c"][:, 0:1],
                               buf[:, nb * 512:(nb + 1) * 512],
                               start=(tt == 0), stop=(tt == 15)), "ksm", 1)
        t.wait_ge(p.sem("kse"), b["kse"] + 4)
        for h in range(NH):
            if h >= 2:
                t.wait_ge(p.sem("ktc"), b["ktc"] + h - 1)
            p.inc(t.transpose(ps[4 + h % 2][0:128, 0:1], ksr[0:1, h * 128:(h + 1) * 128],
                              g["ident"][0:1, 0:1]), "kts", 1)

    @blk.scalar
    def _(se):
        se = VD(se)
        se.wait_ge(p.sem("ksm"), b["ksm"] + 64)
        for nb in range(4):
            p.inc(se.copy(ksr[0:1, nb * 512:(nb + 1) * 512], ps[nb][0:1, 0:512]), "kse", 1)
        for h in range(NH):
            se.wait_ge(p.sem("kts"), b["kts"] + h + 1)
            p.inc(se.copy(kvsb[:, h * 132 + 128:h * 132 + 129], ps[4 + h % 2][0:128, 0:1]), "ktc", 1)

    @blk.gpsimd
    def _(g_):
        p.wait(g_, "ktc")
        p.wait(g_, "kva")
        p.inc(g_.dma_start(out=io["kvp_d"].rearrange("(h d) c -> d h c", d=128),
                           in_=kvsb[:, :].rearrange("p (h c) -> p h c", h=NH)), "kps", 16)
        g_.wait_ge(p.sem("kps"), b["kps"] + 16)
        if do_ag:
            pairs = [[0, 1], [2, 3], [4, 5], [6, 7]]
            p.inc(g_.collective_compute("AllGather", ALU.bypass, replica_groups=pairs,
                                        ins=[io["kvp_d"]], outs=[io["kvp_ag"]]), "cc", 1)
            p.wait(g_, "cc")

    ar.free(xlt, wt, *qpc, brow, t1, t2, kpc, vc, ksr)


def ph_l1attn(p, io, g, tag):
    p.phase_boundary()
    nc = p.nc
    ar = p.ar
    ps = g["ps"]
    kvsb = g["l1_kvsb"]
    qph = [ar.alloc(f"{tag}_qph{i}", [128, TLOC], BF16) for i in range(2)]
    kva = ar.alloc(f"{tag}_kva", [128, 16 * 132], F32)
    kvb = ar.alloc(f"{tag}_kvb", [128, 16 * 128], BF16)
    ksb = ar.alloc(f"{tag}_ksb", [128, 16], BF16)
    zrow = ar.alloc(f"{tag}_zr", [1, 2 * 512], F32)
    rzb = ar.alloc(f"{tag}_rzb", [128, 2 * 512], F32)
    aob = [ar.alloc(f"{tag}_ao{i}", [128, TLOC], BF16) for i in range(2)]
    p.new_block()
    blk = p.blk
    b = {n: p.cnt_of(n) for n in ["kal", "kaa", "omm", "zmm", "zrc", "zbm", "zbe", "oev"]}
    AS = p.plan("aos", 2, NH, unit=16)
    QL = p.plan("qpl", 2, NH, q="s")
    ao_r = io["aout_d"].rearrange("(h d) t -> h d t", d=128)
    qp_r = io["qp_d"].rearrange("h d t -> (h d) t")

    @blk.sync
    def _(s):
        for r in range(2):
            p.inc(s.dma_start(out=kva[:, :].rearrange("p (h c) -> p h c", h=NH) if r == 0 else
                              kvsb[:, :].rearrange("p (h c) -> p h c", h=NH),
                              in_=io["kvp_ag"][r].rearrange("(h d) c -> d h c", d=128)), "kal", 16)
        for h in range(NH):
            if h >= 2:
                s.wait_ge(p.sem("oev"), b["oev"] + (h - 1) * 4)
            tgt = p.inc(s.dma_start(out=qph[h % 2][:], in_=qp_r[h * 128:(h + 1) * 128, :]), QL[h][0], 16)
            assert tgt == QL[h][1]

    @blk.vector
    def _(v):
        v = VD(v)
        v.wait_ge(p.sem("kal"), b["kal"] + 32)
        p.inc(v.tensor_tensor(kva[:], kva[:], kvsb[:], ALU.add), "kaa", 1)
        for h in range(NH):
            v.tensor_copy(kvb[:, h * 128:(h + 1) * 128], kva[:, h * 132:h * 132 + 128])
            kk = v.tensor_copy(ksb[:, h:h + 1], kva[:, h * 132 + 128:h * 132 + 129])
        p.inc(kk, "kaa", 1)

    @blk.tensor
    def _(t):
        p.wait(t, "kaa")
        for h in range(NH):
            t.wait_ge(p.sem(QL[h][0]), QL[h][1])
            for tb in range(4):
                e = h * 4 + tb
                if e >= 2:
                    t.wait_ge(p.sem("oev"), b["oev"] + e - 1)
                p.inc(t.matmul(ps[e % 2][:, 0:512], kvb[:, h * 128:(h + 1) * 128],
                               qph[h % 2][:, tb * 512:(tb + 1) * 512],
                               start=True, stop=True), "omm", 1)
                if e >= 2:
                    t.wait_ge(p.sem("zrc"), b["zrc"] + e - 1)
                p.inc(t.matmul(ps[2 + e % 2][0:1, 0:512], ksb[:, h:h + 1],
                               qph[h % 2][:, tb * 512:(tb + 1) * 512],
                               start=True, stop=True), "zmm", 1)
                t.wait_ge(p.sem("zrc"), b["zrc"] + e + 1)
                if e >= 2:
                    t.wait_ge(p.sem("zbe"), b["zbe"] + e - 1)
                    t.wait_ge(p.sem("oev"), b["oev"] + e - 1)
                p.inc(t.matmul(ps[4 + e % 2][:, 0:512], g["onesf_r"][0:1, :],
                               zrow[0:1, 512:1024], start=True, stop=True), "zbm", 1)

    @blk.vector
    def _(v):
        v = VD(v)
        for h in range(NH):
            for tb in range(4):
                e = h * 4 + tb
                v.wait_ge(p.sem("zmm"), b["zmm"] + e + 1)
                v.tensor_scalar(zrow[0:1, 0:512], ps[2 + e % 2][0:1, 0:512], 1e-6, None, ALU.add)
                p.inc(v.reciprocal(zrow[0:1, 512:1024], zrow[0:1, 0:512]), "zrc", 1)
                v.wait_ge(p.sem("zbe"), b["zbe"] + e + 1)
                v.wait_ge(p.sem("omm"), b["omm"] + e + 1)
                if tb == 0 and h >= 2:
                    v.wait_ge(p.sem(AS[h - 2][0]), AS[h - 2][1])
                p.inc(v.tensor_tensor(aob[h % 2][:, tb * 512:(tb + 1) * 512], ps[e % 2][:, 0:512],
                                      rzb[:, (e % 2) * 512:(e % 2 + 1) * 512], ALU.mult), "oev", 1)

    @blk.scalar
    def _(se):
        se = VD(se)
        for h in range(NH):
            for tb in range(4):
                e = h * 4 + tb
                se.wait_ge(p.sem("zbm"), b["zbm"] + e + 1)
                p.inc(se.copy(rzb[:, (e % 2) * 512:(e % 2 + 1) * 512], ps[4 + e % 2][:, 0:512]), "zbe", 1)

    @blk.gpsimd
    def _(g_):
        for h in range(NH):
            g_.wait_ge(p.sem("oev"), b["oev"] + (h + 1) * 4)
            tgt = p.inc(g_.dma_start(out=ao_r[h], in_=aob[h % 2][:]), AS[h][0], 16)
            assert tgt == AS[h][1]
        for i in range(2):
            g_.wait_ge(p.sem(AS[NH - 1 - i][0]), AS[NH - 1 - i][1])

    ar.free(kva, kvb, ksb, zrow, rzb, *aob, *qph, kvsb)


MODE = "fused"

CAST_L0 = [("L0_attn_Wq", "wqt_bf"), ("L0_attn_Wk", "wkt_bf"),
           ("L0_attn_Wv", "wvt_bf"), ("L0_attn_Wo", "wot_bf")]
CAST_L1 = [("L1_attn_q_w", "qw1_bf"), ("L1_attn_k_w", "kw1_bf"),
           ("L1_attn_v_w", "vw1_bf"), ("L1_attn_o_w", "ow1_bf")]

L0FFN_W = ["L0_ffn_w_router", "L0_ffn_w_gate", "L0_ffn_w_up", "L0_ffn_w_down",
           "L0_ffn_amplitudes", "L0_ffn_expert_scale",
           "L0_ffn_shared_w_gate", "L0_ffn_shared_w_up", "L0_ffn_shared_w_down"]

def _launches(fused=False):
    if fused:
        l3 = ("L3", lambda p, io, g: (ph_ghost(p, io, g, "gh"),
                                      ph_l0ffn(p, io, g, "xln2", "x2", "x3", "xln3", "f2")),
              ["xln2", "x2"] + L0FFN_W, ["x3", "xln3"])
        l4 = ("L4", lambda p, io, g: (ph_l1ffn(p, io, g, "xln3", "x3", "x4", "xln4", "g1",
                                               cast_pairs=CAST_L1),),
              ["xln3", "x3"], ["x4", "xln4"])
    else:
        l3 = ("L3", lambda p, io, g: (ph_l0ffn(p, io, g, "xln2", "x2", "x3", "xln3", "f2"),),
              ["xln2", "x2"] + L0FFN_W, ["x3", "xln3"])
        l4 = ("L4", lambda p, io, g: (ph_ghost(p, io, g, "gh", wait_cc=True),
                                      ph_l1ffn(p, io, g, "xln3", "x3", "x4", "xln4", "g1")),
              ["xln3", "x3", "gate_idx_sl", "up_idx_sl", "down_idx_sl", "L1_ffn_lut"],
              ["x4", "xln4"])
    l1 = ("L1", lambda p, io, g: (ph_ln0(p, io, g, "x0", "xln0"),
                                  ph_l0ffn(p, io, g, "xln0", "x0", "x1", "xln1", "f1",
                                           cast_pairs=(CAST_L0 if fused else None))),
          ["x0"] + L0FFN_W, ["x1", "xln1"])
    l2 = ("L2", lambda p, io, g: (() if fused else (ph_castw(p, io, CAST_L0),),
                                  ph_l0qkv(p, io, g, "xln1", "qk"),
                                  ph_l0attn(p, io, g, "at"),
                                  ph_l0proj(p, io, g, "x1", "x2", "xln2", "pj", 1.0)),
          ["xln1", "x1", "maskr", "L0_attn_Wq", "L0_attn_Wk", "L0_attn_Wv", "L0_attn_Wo",
           "L0_attn_gq", "L0_attn_gk"], ["x2", "xln2"])
    l5 = ("L5", lambda p, io, g: (() if fused else (ph_castw(p, io, CAST_L1),),
                                  ph_l1qkv(p, io, g, "xln4", "lq"),
                                  ph_l1attn(p, io, g, "la"),
                                  ph_l0proj(p, io, g, "x4", "x5", "xln5", "lp", 1.0,
                                            wname="ow1_bf", bias="L1_attn_o_b")),
          ["xln4", "x4", "L1_attn_q_w", "L1_attn_q_b", "L1_attn_k_w", "L1_attn_k_b",
           "L1_attn_v_w", "L1_attn_v_b", "L1_attn_o_w", "L1_attn_o_b"], ["x5", "xln5"])
    return [
        l1,
        l2,
        l3,
        l4,
        l5,
        ("L6", lambda p, io, g: (ph_l1ffn(p, io, g, "xln5", "x5", "y_out", None, "g2"),),
         ["xln5", "x5", "gwT_d", "uwT_d", "dwT_d"], ["y_out"]),
    ]


def _all_dram_names():
    reg = dict(DRAM_REG)
    reg.update(WEIGHT_REG)
    return reg


def build_launch(phases_fn, ins, outs):
    nc = bass.Bass("TRN2", target_bir_lowering=False)
    reg = _all_dram_names()
    kinds = {}
    for n in ins:
        kinds[n] = "ExternalInput"
    for n in outs:
        kinds[n] = "ExternalOutput"
    io = declare_dram(nc, list(reg.keys()), kinds)
    p = Prog(nc)
    g = {}
    ph_init(p, g)
    phases_fn(p, io, g)
    p.end()
    return nc


def host_inputs_per_core(inputs):
    shards = shard_x(inputs["L0_attn_x"])
    mks = maskr_host()
    per_core = []
    for c in range(8):
        m = {"x0": shards[c], "maskr": mks[c],
             "gate_idx_sl": np.ascontiguousarray(inputs["L1_ffn_gate_idx"][:, c * 256:(c + 1) * 256]),
             "up_idx_sl": np.ascontiguousarray(inputs["L1_ffn_up_idx"][:, c * 256:(c + 1) * 256]),
             "down_idx_sl": np.pad(inputs["L1_ffn_down_idx"][:, c * 688:(c + 1) * 688],
                                   ((0, 0), (0, 80)))}
        for k, v in inputs.items():
            if k in WEIGHT_REG:
                m[k] = np.ascontiguousarray(v)
        per_core.append(m)
    return per_core


def kernel(**inputs):
    inputs = {k: np.asarray(v) for k, v in inputs.items()}
    per_core = host_inputs_per_core(inputs)
    if MODE == "multi":
        state = [dict(pc) for pc in per_core]
        for name, fn, ins, outs in _launches(fused=False):
            nc = build_launch(fn, ins, outs)
            in_maps = [{n: state[c][n] for n in ins} for c in range(8)]
            res = run_bass_kernel_spmd(nc, in_maps, core_ids=list(range(8)))
            for c in range(8):
                for n in outs:
                    state[c][n] = res.results[c][n]
        ys = [state[c]["y_out"] for c in range(8)]
        return unshard_y(ys)
    else:
        nc = build_fused()
        in_names = FUSED_INS
        in_maps = [{n: per_core[c][n] for n in in_names} for c in range(8)]
        res = run_bass_kernel_spmd(nc, in_maps, core_ids=list(range(8)))
        return unshard_y([res.results[c]["y_out"] for c in range(8)])


FUSED_INS = (["x0", "maskr", "gate_idx_sl", "up_idx_sl", "down_idx_sl"] +
             list(WEIGHT_REG.keys()))
FUSED_INS = [n for n in FUSED_INS if n not in ("gate_idx_sl", "up_idx_sl", "down_idx_sl")] + \
            ["gate_idx_sl", "up_idx_sl", "down_idx_sl"]
FUSED_INS = list(dict.fromkeys(FUSED_INS))


def build_fused():
    nc = bass.Bass("TRN2", target_bir_lowering=False)
    reg = _all_dram_names()
    kinds = {n: "ExternalInput" for n in FUSED_INS}
    kinds["y_out"] = "ExternalOutput"
    io = declare_dram(nc, list(reg.keys()), kinds)
    p = Prog(nc)
    g = {}
    ph_init(p, g)
    for name, fn, ins, outs in _launches(fused=True):
        fn(p, io, g)
    p.end()
    return nc
